# version 17
# baseline (speedup 1.0000x reference)
import numpy as np
import concourse.bass as bass
import concourse.mybir as mybir
from concourse.ap import AP
from concourse.bass_utils import run_bass_kernel_spmd

F32 = mybir.dt.float32
BF16 = mybir.dt.bfloat16
AF = mybir.ActivationFunctionType
ALU = mybir.AluOpType

S = 4096
D = 1024
CH = 8
TB = 512
NTB = S // TB
H = 16
HP = 8
DH = 64
KC = S // 128
VTW = H * 65
SW = 2 * S
UW = 8064
SOUT = 2048
NQB = SOUT // TB
UOFF = 2048
UWB = 6018
EPS = 1e-5
SCALE = 0.125
import os as _os
NSUB = int(_os.environ.get("KNSUB", "4"))


class Bld:
    def __init__(self, nc):
        self.nc = nc
        self.sem = {}
        self.cnt = {}
        for name in ("DI", "DO", "DG", "SS", "PS", "AS", "VS", "GS",
                     "CA", "CV", "CG", "D0", "D1", "L0", "L1", "U0", "U1",
                     "V0", "V1"):
            self.sem[name] = nc.alloc_semaphore(name)
            self.cnt[name] = 0

    def inc(self, instr, s, n=1):
        self.cnt[s] += n
        instr.then_inc(self.sem[s], n)
        return self.cnt[s]

    def w(self, eng, s, val):
        if val > 0:
            eng.wait_ge(self.sem[s], val)

    ENTRY = ("SS", "PS", "AS", "VS", "DO", "DG", "D0", "D1", "DI",
             "L0", "L1", "U0", "U1", "V0", "V1")

    def entry(self, eng, ent):
        for s in self.ENTRY:
            self.w(eng, s, ent[s])

    def guard(self, eng, *sems):
        for s in sems:
            self.w(eng, s, self.cnt[s])

    def dma(self, eng, out, in_, s):
        i = eng.dma_start(out=out, in_=in_)
        return self.inc(i, s, 16)

    def chain(self, eng, instr, s):
        v = self.inc(instr, s, 1)
        eng.wait_ge(self.sem[s], v)

    _uid = 0

    def nm(self, base):
        Bld._uid += 1
        return f"{base}_{Bld._uid}" 


def xf_ap(xf):
    return xf.ap().rearrange("(c p) s -> p c s", p=128)


def build_program():
    nc = bass.Bass("TRN2", target_bir_lowering=False)
    B = Bld(nc)

    xin = nc.dram_tensor("xin", [S, D], F32, kind="ExternalInput")
    wq_d = nc.dram_tensor("wq", [D, D], F32, kind="ExternalInput")
    wk_d = nc.dram_tensor("wk", [D, D], F32, kind="ExternalInput")
    wv_d = nc.dram_tensor("wv", [D, D], F32, kind="ExternalInput")
    wo_d = nc.dram_tensor("wo", [D, D], F32, kind="ExternalInput")
    winT_d = nc.dram_tensor("winT", [D, 3 * D], F32, kind="ExternalInput")
    wrT_d = nc.dram_tensor("wrT", [D, D], F32, kind="ExternalInput")
    wk2T_d = nc.dram_tensor("wk2T", [D, D], F32, kind="ExternalInput")
    wv2T_d = nc.dram_tensor("wv2T", [D, D], F32, kind="ExternalInput")
    mur_d = nc.dram_tensor("mur", [128, CH], F32, kind="ExternalInput")
    muk_d = nc.dram_tensor("muk", [128, CH], F32, kind="ExternalInput")
    toff_d = nc.dram_tensor("toff", [128, 2], F32, kind="ExternalInput")
    tshp_d = nc.dram_tensor("tshp", [128, 2], F32, kind="ExternalInput")
    tamp_d = nc.dram_tensor("tampbd", [256, 16], F32, kind="ExternalInput")
    rolec_d = nc.dram_tensor("rolec", [1, 4], F32, kind="ExternalInput")
    ridx_d = nc.dram_tensor("ridx", [1, 4], mybir.dt.uint32,
                            kind="ExternalInput")
    y_d = nc.dram_tensor("y", [SOUT, D], F32, kind="ExternalOutput")

    xF = nc.dram_tensor("xF", [CH * 128, S], F32)
    xF2 = nc.dram_tensor("xF2", [CH * 128, SOUT], F32)
    xnD = nc.dram_tensor("xnD", [CH * 128, S], BF16)
    zD = nc.dram_tensor("zD", [1, 16], F32)
    oaD = nc.dram_tensor("oaD", [CH * 128, S], BF16)
    vtD = nc.dram_tensor("vtD", [KC * 128, VTW], BF16)
    sexpD = nc.dram_tensor("sexpD", [H, SW], F32)

    ones128 = nc.alloc_sbuf_tensor("ones128", [128, 1], F32)
    ones1 = nc.alloc_sbuf_tensor("ones1", [1, 128], F32)
    ident = nc.alloc_sbuf_tensor("ident", [128, 128], F32)
    masks = nc.alloc_sbuf_tensor("masks", [128, 4 * TB], BF16)
    mur_sb = nc.alloc_sbuf_tensor("mur_sb", [128, CH], F32)
    muk_sb = nc.alloc_sbuf_tensor("muk_sb", [128, CH], F32)
    mur1_sb = nc.alloc_sbuf_tensor("mur1_sb", [128, CH], F32)
    muk1_sb = nc.alloc_sbuf_tensor("muk1_sb", [128, CH], F32)
    deltaB = nc.alloc_sbuf_tensor("deltaB", [128, 1], F32)
    hmB = nc.alloc_sbuf_tensor("hmB", [128, 1], F32)
    ridx_sb = nc.alloc_sbuf_tensor("ridx_sb", [1, 4], mybir.dt.uint32)
    xnb = nc.alloc_sbuf_tensor("xnb", [128, CH * (S + 1)], BF16)

    cst = dict(ones128=ones128, ones1=ones1, ident=ident, masks=masks,
               mur=mur_sb, muk=muk_sb, mur1=mur1_sb, muk1=muk1_sb,
               xnb=xnb, deltaB=deltaB, hmB=hmB, ridx=ridx_sb)

    with nc.Block() as block:
        emit_setup(B, block, xin, mur_d, muk_d, toff_d, tshp_d, tamp_d,
                   rolec_d, ridx_d, xF, sexpD, cst)
        for sub in range(NSUB):
            xf_src = xF2 if sub == 3 else xF
            ntok = SOUT if sub == 3 else S
            emit_statsxn(B, block, xf_src, cst, ntok,
                         mirror=(xnD if sub == 2 else None))
            if sub % 2 == 0:
                causal = (sub == 0)
                emit_attn(B, block, xF, xF2, oaD, vtD, sexpD, cst,
                          wq_d, wk_d, wv_d, wo_d, winT_d, causal, zD, xnD)
            else:
                emit_ffn(B, block, xf_src, cst, wrT_d, wk2T_d, wv2T_d, ntok)
        emit_output(B, block, xF2 if NSUB >= 3 else xF, ident, y_d, cst,
                    dyn=(NSUB < 3))
    return nc


def emit_setup(B, block, xin, mur_d, muk_d, toff_d, tshp_d, tamp_d,
               rolec_d, ridx_d, xF, sexpD, cst):
    nc = B.nc
    with (
        nc.sbuf_tensor(B.nm("toff_sb"), [128, 2], F32) as toff_sb,
        nc.sbuf_tensor(B.nm("tshp_sb"), [128, 2], F32) as tshp_sb,
        nc.sbuf_tensor(B.nm("tnsh_sb"), [128, 2], F32) as tnsh_sb,
        nc.sbuf_tensor(B.nm("tamp_sb"), [128, 2 * 16], F32) as tamp_sb,
        nc.sbuf_tensor(B.nm("iotb"), [128, 2048], F32) as iotb,
        nc.sbuf_tensor(B.nm("tk0b"), [128, 2048], F32) as tk0b,
        nc.sbuf_tensor(B.nm("tk1b"), [128, 2048], F32) as tk1b,
        nc.sbuf_tensor(B.nm("sexpb"), [16, 2 * TB], F32) as sexpb,
        nc.psum_tensor(B.nm("ps_s"), [16, 2 * TB], F32) as ps_s,
    ):
        def g_body(g):
            B.dma(g, cst["mur"][:, :], mur_d[:, :], "DG")
            B.dma(g, cst["muk"][:, :], muk_d[:, :], "DG")
            B.dma(g, toff_sb[:, :], toff_d[:, :], "DG")
            B.dma(g, tshp_sb[:, :], tshp_d[:, :], "DG")
            B.dma(g, tamp_sb[:, :].rearrange("p (u k) -> p u k", k=16),
                  tamp_d.ap().rearrange("(u p) k -> p u k", p=128), "DG")
            B.dma(g, cst["deltaB"][:, :],
                  AP(rolec_d, 0, [[0, 128], [1, 1]]), "DG")
            B.dma(g, cst["hmB"][:, :],
                  AP(rolec_d, 1, [[0, 128], [1, 1]]), "DG")
            B.dma(g, cst["ridx"][:, :], ridx_d[:, :], "DG")
            g.memset(cst["ones128"][:, :], 1.0)
            g.memset(cst["ones1"][:, :], 1.0)
            B.chain(g, g.memset(cst["ident"][:, :], 0.0), "CG")
            g.affine_select(out=cst["ident"][:, :], in_=cst["ident"][:, :],
                            pattern=[[-1, 128]], compare_op=ALU.not_equal,
                            fill=1.0, base=0, channel_multiplier=1)
            B.chain(g, g.memset(cst["masks"][:, :], 1.0), "CG")
            for r in range(4):
                g.affine_select(out=cst["masks"][:, r * TB:(r + 1) * TB],
                                in_=cst["masks"][:, r * TB:(r + 1) * TB],
                                pattern=[[1, TB]], compare_op=ALU.is_ge,
                                fill=0.0, base=-128 * r,
                                channel_multiplier=-1)
            i = g.memset(cst["xnb"][:, 0:CH * (S + 1):(S + 1)], 0.0)
            B.inc(i, "GS")
        block.gpsimd(g_body)
        gs_ready = B.cnt["GS"]
        dg_ready = B.cnt["DG"]

        def v_mu1(v):
            B.w(v, "DG", dg_ready)
            i = v.tensor_scalar(cst["mur1"][:, :], cst["mur"][:, :],
                                -1.0, 1.0, ALU.mult, ALU.add)
            B.inc(i, "VS")
            i = v.tensor_scalar(cst["muk1"][:, :], cst["muk"][:, :],
                                -1.0, 1.0, ALU.mult, ALU.add)
            B.inc(i, "VS")
        block.vector(v_mu1)

        def s_abs(s):
            B.w(s, "DG", dg_ready)
            i = s.activation(tnsh_sb[:, :], tshp_sb[:, :], AF.Abs)
            B.inc(i, "AS")
        block.scalar(s_abs)
        as_abs = B.cnt["AS"]

        def v_neg(v):
            B.w(v, "AS", as_abs)
            i = v.tensor_scalar_mul(tnsh_sb[:, :], tnsh_sb[:, :], -1.0)
            B.inc(i, "VS")
        block.vector(v_neg)
        vs_neg = B.cnt["VS"]

        for blk in range(SW // 2048):
            i0 = blk * 2048

            def g_io(g, i0=i0):
                B.guard(g, "VS")
                i = g.iota(iotb[:, :], pattern=[[-1, 2048]],
                           base=S - 1 - i0, channel_multiplier=0,
                           allow_small_or_imprecise_dtypes=True)
                B.inc(i, "GS")
            block.gpsimd(g_io)
            gs_io = B.cnt["GS"]

            def v_tk(v, gs_io=gs_io):
                B.w(v, "GS", gs_io)
                B.w(v, "VS", vs_neg)
                B.guard(v, "AS")
                B.chain(v, v.tensor_scalar(iotb[:, :], iotb[:, :],
                                           cst["deltaB"][:, 0:1], None,
                                           ALU.add), "CV")
                for tk, half in ((tk0b, 0), (tk1b, 1)):
                    B.chain(v, v.tensor_scalar(tk[:, :], iotb[:, :],
                                               toff_sb[:, half:half + 1],
                                               None, ALU.subtract), "CV")
                    i = v.scalar_tensor_tensor(tk[:, :], tk[:, :],
                                               tnsh_sb[:, half:half + 1],
                                               tk[:, :], ALU.mult, ALU.mult)
                    B.inc(i, "VS")
            block.vector(v_tk)
            vs_tk = B.cnt["VS"]

            def s_ek(s, vs_tk=vs_tk):
                B.w(s, "VS", vs_tk)
                B.guard(s, "SS")
                s.activation(tk0b[:, :], tk0b[:, :], AF.Exp)
                i = s.activation(tk1b[:, :], tk1b[:, :], AF.Exp)
                B.inc(i, "AS")
            block.scalar(s_ek)
            as_ek = B.cnt["AS"]

            ss_mm0 = B.cnt["SS"]
            as_se0 = B.cnt["AS"]

            def t_amp(t, as_ek=as_ek, ss_mm0=ss_mm0, as_se0=as_se0):
                B.w(t, "AS", as_ek)
                for ch in range(4):
                    c0 = ch * TB
                    bank = ch % 2
                    B.w(t, "AS", as_se0 + max(0, ch - 1))
                    t.matmul(ps_s[0:16, bank * TB:(bank + 1) * TB],
                             tamp_sb[:, 0:16], tk0b[:, c0:c0 + TB],
                             start=True, stop=False)
                    i = t.matmul(ps_s[0:16, bank * TB:(bank + 1) * TB],
                                 tamp_sb[:, 16:32], tk1b[:, c0:c0 + TB],
                                 start=False, stop=True)
                    B.inc(i, "SS")
            block.tensor(t_amp)

            do_se0 = B.cnt["DO"]

            def s_se(s, ss_mm0=ss_mm0, do_se0=do_se0):
                for ch in range(4):
                    bank = ch % 2
                    B.w(s, "SS", ss_mm0 + ch + 1)
                    B.w(s, "DO", do_se0 + 16 * max(0, ch - 1))
                    i = s.activation(sexpb[0:16, bank * TB:(bank + 1) * TB],
                                     ps_s[0:16, bank * TB:(bank + 1) * TB],
                                     AF.Exp)
                    B.inc(i, "AS")
            block.scalar(s_se)
            as_se1 = B.cnt["AS"]

            do_se1 = B.cnt["DO"]

            def sy_se(sy, i0=i0, as_se0=as_se0, do_se1=do_se1):
                for ch in range(4):
                    bank = ch % 2
                    B.w(sy, "AS", as_se0 + ch + 1)
                    B.w(sy, "DO", do_se1 + 16 * ch)
                    B.dma(sy, sexpD.ap()[:, i0 + ch * TB:i0 + (ch + 1) * TB],
                          sexpb[0:16, bank * TB:(bank + 1) * TB], "DO")
            block.sync(sy_se)

    if True:
        do_sexp = B.cnt["DO"]
        with (
            nc.sbuf_tensor(B.nm("xt0"), [128, D], F32) as xt0,
            nc.sbuf_tensor(B.nm("xt1"), [128, D], F32) as xt1,
            nc.sbuf_tensor(B.nm("xt2"), [128, D], F32) as xt2,
            nc.psum_tensor(B.nm("ps_t"), [128, D], F32) as ps_t,
        ):
            xts = (xt0, xt1)
            ent_t = dict(B.cnt)
            dp = (B.cnt["D0"], B.cnt["D1"])
            ts0, as1, do0 = B.cnt["SS"], B.cnt["AS"], B.cnt["DO"]

            def syio(sy):
                B.entry(sy, ent_t)
                B.w(sy, "GS", ent_t["GS"])
                B.w(sy, "DO", do_sexp)
                for tt in range(KC):
                    B.w(sy, "SS", ts0 + max(0, tt - 1))
                    B.dma(sy, xts[tt % 2][:, :],
                          xin[tt * 128:(tt + 1) * 128, :], f"D{tt % 2}")
                    if tt >= 1:
                        B.w(sy, "AS", as1 + tt)
                        dst = xf_ap(xF)[:, :, (tt - 1) * 128:tt * 128]
                        B.dma(sy, dst,
                              xt2[:, :].rearrange("p (c j) -> p c j", j=128),
                              "DO")
                B.w(sy, "AS", as1 + KC)
                dst = xf_ap(xF)[:, :, (KC - 1) * 128:KC * 128]
                B.dma(sy, dst,
                      xt2[:, :].rearrange("p (c j) -> p c j", j=128), "DO")
            block.sync(syio)

            def tb_(t):
                B.entry(t, ent_t)
                B.w(t, "GS", gs_ready)
                for tt in range(KC):
                    B.w(t, f"D{tt % 2}", dp[tt % 2] + 16 * (tt // 2 + 1))
                    B.w(t, "AS", as1 + tt)
                    for c in range(CH):
                        i = t.transpose(ps_t[:, c * 128:(c + 1) * 128],
                                        xts[tt % 2][:, c * 128:(c + 1) * 128],
                                        cst["ident"][:, :])
                    B.inc(i, "SS")
            block.tensor(tb_)

            def sb_(s):
                B.entry(s, ent_t)
                for tt in range(KC):
                    B.w(s, "SS", ts0 + tt + 1)
                    B.w(s, "DO", do0 + 16 * tt)
                    i = s.copy(xt2[:, :], ps_t[:, :])
                    B.inc(i, "AS")
            block.scalar(sb_)




def emit_statsxn(B, block, xF, cst, ntok=S, mirror=None):
    NB = ntok // TB
    nc = B.nc
    with (
        nc.sbuf_tensor(B.nm("sx_x0"), [128, CH * TB], F32) as sx_x0,
        nc.sbuf_tensor(B.nm("sx_x1"), [128, CH * TB], F32) as sx_x1,
        nc.sbuf_tensor(B.nm("sx_q0"), [128, CH * TB], F32) as sx_q0,
        nc.sbuf_tensor(B.nm("sx_q1"), [128, CH * TB], F32) as sx_q1,
        nc.sbuf_tensor(B.nm("sx_m"), [1, 2 * TB], F32) as sx_m,
        nc.sbuf_tensor(B.nm("sx_s"), [1, 2 * TB], F32) as sx_s,
        nc.sbuf_tensor(B.nm("sx_t"), [1, 2 * TB], F32) as sx_t,
        nc.psum_tensor(B.nm("ps_m0"), [1, 2 * TB], F32) as ps_m0,
        nc.psum_tensor(B.nm("ps_m1"), [1, 2 * TB], F32) as ps_m1,
        nc.psum_tensor(B.nm("ps_b0"), [128, 2 * TB], F32) as ps_b0,
        nc.psum_tensor(B.nm("ps_b1"), [128, 2 * TB], F32) as ps_b1,
    ):
        xbs = (sx_x0, sx_x1)
        sqs = (sx_q0, sx_q1)
        pms = (ps_m0, ps_m1)
        pbs = (ps_b0, ps_b1)
        ent = dict(B.cnt)
        dp = (B.cnt["D0"], B.cnt["D1"])
        vs0 = B.cnt["VS"]
        ss0 = B.cnt["SS"]
        as0 = B.cnt["AS"]
        NVT = 2 + CH

        def vs_tb(tb):
            return vs0 + NVT * (tb + 1)

        def syi(sy):
            B.entry(sy, ent)
            for tb in range(NB):
                if tb >= 2:
                    B.w(sy, "VS", vs_tb(tb - 2))
                B.dma(sy, xbs[tb % 2][:, :].rearrange(
                    "p (c t) -> p c t", t=TB),
                    xf_ap(xF)[:, :, tb * TB:(tb + 1) * TB], f"D{tb % 2}")
        block.sync(syi)

        def vb(v):
            B.entry(v, ent)
            for tb in range(NB):
                B.w(v, f"D{tb % 2}", dp[tb % 2] + 16 * (tb // 2 + 1))
                if tb >= 2:
                    B.w(v, "SS", ss0 + 3 * (tb - 2) + 2)
                i = v.tensor_mul(sqs[tb % 2][:, :], xbs[tb % 2][:, :],
                                 xbs[tb % 2][:, :])
                B.inc(i, "VS")
                B.w(v, "SS", ss0 + 3 * tb + 2)
                pm = pms[tb % 2]
                B.chain(v, v.tensor_scalar_mul(sx_m[0:1, :], pm[0:1, :],
                                               1.0 / D), "CV")
                B.chain(v, v.tensor_mul(sx_t[0:1, 0:TB], sx_m[0:1, 0:TB],
                                        sx_m[0:1, 0:TB]), "CV")
                B.chain(v, v.tensor_sub(sx_m[0:1, TB:2 * TB],
                                        sx_m[0:1, TB:2 * TB],
                                        sx_t[0:1, 0:TB]), "CV")
                i = v.tensor_scalar_add(sx_m[0:1, TB:2 * TB],
                                        sx_m[0:1, TB:2 * TB], EPS)
                B.inc(i, "VS")
                B.w(v, "SS", ss0 + 3 * tb + 3)
                pb = pbs[tb % 2]
                tmp = sqs[tb % 2]
                for c in range(CH):
                    i = v.tensor_sub(tmp[:, c * TB:(c + 1) * TB],
                                     xbs[tb % 2][:, c * TB:(c + 1) * TB],
                                     pb[:, 0:TB])
                B.chain(v, i, "CV")
                for c in range(CH):
                    base = c * (S + 1) + 1 + tb * TB
                    i = v.tensor_mul(cst["xnb"][:, base:base + TB],
                                     tmp[:, c * TB:(c + 1) * TB],
                                     pb[:, TB:2 * TB])
                    B.inc(i, "VS")
        block.vector(vb)

        def tb_(t):
            B.entry(t, ent)
            for tb in range(NB):
                pm = pms[tb % 2]
                B.w(t, f"D{tb % 2}", dp[tb % 2] + 16 * (tb // 2 + 1))
                if tb >= 2:
                    B.w(t, "VS", vs0 + NVT * (tb - 2) + 2)
                for c in range(CH):
                    i = t.matmul(pm[0:1, 0:TB], cst["ones128"][:, :],
                                 xbs[tb % 2][:, c * TB:(c + 1) * TB],
                                 start=(c == 0), stop=(c == CH - 1))
                B.inc(i, "SS")
                B.w(t, "VS", vs0 + NVT * tb + 1)

                for c in range(CH):
                    i = t.matmul(pm[0:1, TB:2 * TB], cst["ones128"][:, :],
                                 sqs[tb % 2][:, c * TB:(c + 1) * TB],
                                 start=(c == 0), stop=(c == CH - 1))
                B.inc(i, "SS")
                B.w(t, "AS", as0 + tb + 1)
                if tb >= 2:
                    B.w(t, "VS", vs_tb(tb - 2))
                pb = pbs[tb % 2]
                t.matmul(pb[:, 0:TB], cst["ones1"][0:1, :],
                         sx_m[0:1, 0:TB], start=True, stop=True)
                i = t.matmul(pb[:, TB:2 * TB], cst["ones1"][0:1, :],
                             sx_s[0:1, 0:TB], start=True, stop=True)
                B.inc(i, "SS")
        block.tensor(tb_)

        def sb(s):
            B.entry(s, ent)
            for tb in range(NB):
                B.w(s, "VS", vs0 + NVT * tb + 2)
                B.chain(s, s.activation(sx_t[0:1, TB:2 * TB],
                                        sx_m[0:1, TB:2 * TB], AF.Ln), "CA")
                i = s.activation(sx_s[0:1, 0:TB], sx_t[0:1, TB:2 * TB],
                                 AF.Exp, scale=-0.5)
                B.inc(i, "AS")
        block.scalar(sb)

        if mirror is not None:
            def sym(sy):
                for tb in range(NB):
                    B.w(sy, "VS", vs_tb(tb))
                    src_ap = AP(cst["xnb"].tensor
                                if hasattr(cst["xnb"], "tensor")
                                else cst["xnb"], 1 + tb * TB,
                                [[CH * (S + 1), 128], [S + 1, CH], [1, TB]])
                    B.dma(sy, xf_ap(mirror)[:, :, tb * TB:(tb + 1) * TB],
                          src_ap, "DO")
            block.sync(sym)


def xn_sl(cst, c, t0, tl):
    return cst["xnb"][:, c * (S + 1) + 1 + t0: c * (S + 1) + 1 + t0 + tl]


def emit_attn(B, block, xF, xF2, oaD, vtD, sexpD, cst,
              wq_d, wk_d, wv_d, wo_d, winT_d, causal, zD, xnD):
    nc = B.nc
    from contextlib import ExitStack
    with ExitStack() as actx:
        if not causal:
            ae = actx.enter_context
            cst = dict(cst)
            cst["xn2own"] = ae(nc.sbuf_tensor(B.nm("xn2own"),
                                              [128, CH * SOUT], BF16))
            cst["xnqh"] = ae(nc.sbuf_tensor(B.nm("xnqh"), [128, CH * 128],
                                            BF16))
            cst["x2hp"] = ae(nc.sbuf_tensor(B.nm("x2hp"), [128, CH * 128],
                                            F32))
            cst["ohalo"] = ae(nc.sbuf_tensor(B.nm("ohalo"), [128, CH], F32))
            cst["zhst"] = ae(nc.sbuf_tensor(B.nm("zhst"), [65, 16], F32))
            cst["hstage"] = ae(nc.sbuf_tensor(B.nm("hstage"), [65, 1], F32))
            cst["zrow"] = ae(nc.sbuf_tensor(B.nm("zrow"), [1, 16], F32))
            cst["zcol"] = ae(nc.sbuf_tensor(B.nm("zcol"), [128, CH], F32))
        _emit_attn_inner(B, block, xF, xF2, oaD, vtD, sexpD, cst,
                         wq_d, wk_d, wv_d, wo_d, winT_d, causal, zD, xnD)


def _emit_attn_inner(B, block, xF, xF2, oaD, vtD, sexpD, cst,
                     wq_d, wk_d, wv_d, wo_d, winT_d, causal, zD, xnD):
    nc = B.nc
    ent_a = dict(B.cnt)
    with (
        nc.sbuf_tensor(B.nm("wv_sb"), [128, CH * D], BF16) as wv_sb,
        nc.sbuf_tensor(B.nm("vt_st"), [128, VTW], BF16) as vt_st,
        nc.psum_tensor(B.nm("ps_v0"), [128, D], F32) as ps_v0,
        nc.psum_tensor(B.nm("ps_v1"), [128, D], F32) as ps_v1,
    ):
        def gld(g):
            B.entry(g, ent_a)
            src = wv_d.ap() if causal else winT_d.ap()[:, D:2 * D]
            B.dma(g, wv_sb[:, :].rearrange("p (cb j) -> p cb j", j=D),
                  src.rearrange("(cb p) j -> p cb j", p=128), "DG")
            i = g.memset(vt_st[:, 64:VTW:65], 1.0)
            B.inc(i, "GS")
        block.gpsimd(gld)
        dg_w = B.cnt["DG"]
        gs_st = B.cnt["GS"]
        xn_vs = B.cnt["VS"]

        psv = (ps_v0, ps_v1)
        ts0, vs0, do0 = B.cnt["SS"], B.cnt["VS"], B.cnt["DO"]

        def tb_(t):
            B.entry(t, ent_a)
            B.w(t, "DG", dg_w)
            B.w(t, "VS", xn_vs)
            for tt in range(KC):
                ps = psv[tt % 2]
                B.w(t, "VS", vs0 + max(0, tt - 1))
                for c in range(CH):
                    t.matmul(ps[:, 0:TB],
                             xn_sl(cst, c, tt * 128, 128),
                             wv_sb[:, c * D:c * D + TB],
                             start=(c == 0), stop=(c == CH - 1))
                    i = t.matmul(ps[:, TB:D],
                                 xn_sl(cst, c, tt * 128, 128),
                                 wv_sb[:, c * D + TB:(c + 1) * D],
                                 start=(c == 0), stop=(c == CH - 1))
                B.inc(i, "SS")
        block.tensor(tb_)

        def vb(v):
            B.entry(v, ent_a)
            B.w(v, "GS", gs_st)
            for tt in range(KC):
                ps = psv[tt % 2]
                B.w(v, "SS", ts0 + tt + 1)
                B.w(v, "DO", do0 + 16 * tt)
                for half in range(2):
                    src = ps[:, half * TB:(half + 1) * TB].rearrange(
                        "p (h d) -> p h d", d=64)
                    dst = AP(vt_st, half * 8 * 65,
                             [[VTW, 128], [65, 8], [1, 64]])
                    i = v.tensor_copy(dst, src)
                B.inc(i, "VS")
        block.vector(vb)

        def syo(sy):
            B.entry(sy, ent_a)
            for tt in range(KC):
                B.w(sy, "VS", vs0 + tt + 1)
                dst = vtD.ap().rearrange("(tt p) w -> p tt w",
                                         p=128)[:, tt:tt + 1, :]
                B.dma(sy, dst,
                      vt_st[:, :].rearrange("p (o w) -> p o w", o=1), "DO")
        block.sync(syo)

    if not causal:
        ent_q = dict(B.cnt)

        def syq(sy):
            B.entry(sy, ent_q)
            sv = sy.value_load(cst["ridx"][0:1, 0:1])
            svh = sy.value_load(cst["ridx"][0:1, 1:2])
            B.dma(sy, cst["xn2own"][:, :].rearrange("p (c t) -> p c t",
                                                    t=SOUT),
                  AP(xnD, sv, [[S, 128], [128 * S, CH], [1, SOUT]]), "DI")
            B.dma(sy, cst["xnqh"][:, :].rearrange("p (c t) -> p c t", t=128),
                  AP(xnD, svh, [[S, 128], [128 * S, CH], [1, 128]]), "DI")
            B.dma(sy, cst["x2hp"][:, :].rearrange("p (c t) -> p c t", t=128),
                  AP(xF, svh, [[S, 128], [128 * S, CH], [1, 128]]), "DI")
        block.sync(syq)

    uw = UWB if not causal else 2
    from contextlib import ExitStack as _ES
    with _ES() as lctx:
        le = lctx.enter_context
        u0 = le(nc.sbuf_tensor(B.nm("ldu0"), [128, uw], BF16))
        u1 = le(nc.sbuf_tensor(B.nm("ldu1"), [128, uw], BF16))
        ldsets = []
        for _b in range(2):
            ldsets.append({
                "wA": le(nc.sbuf_tensor(B.nm("ldwA"), [128, CH * 128],
                                        BF16)),
                "wB": le(nc.sbuf_tensor(B.nm("ldwB"), [128, CH * 128],
                                        BF16)),
                "vt": le(nc.sbuf_tensor(B.nm("ldvt"), [128, KC * 130],
                                        BF16)),
                "u0": u0,
                "u1": u1,
            })
        lc = {}

        def emit_hp_loads(hpx, ent):
            ld = ldsets[hpx % 2]
            sname = f"L{hpx % 2}"

            def gl(g, hpx=hpx, ld=ld, ent=ent, sname=sname):
                B.entry(g, ent)
                if causal:
                    sq, sk = wq_d.ap(), wk_d.ap()
                else:
                    sk = winT_d.ap()[:, 0:D]
                    sq = winT_d.ap()[:, 2 * D:3 * D]
                for w_sb, srcw in ((ld["wA"], sq), (ld["wB"], sk)):
                    B.dma(g, w_sb[:, :].rearrange("p (cb j) -> p cb j",
                                                  j=128),
                          srcw[:, hpx * 128:(hpx + 1) * 128].rearrange(
                              "(cb p) j -> p cb j", p=128), sname)
            block.gpsimd(gl)

            vname = f"V{hpx % 2}"

            def syl(sy, hpx=hpx, ld=ld, ent=ent, vname=vname):
                B.entry(sy, ent)
                srcv = vtD.ap().rearrange(
                    "(tt p) w -> p tt w",
                    p=128)[:, :, hpx * 130:(hpx + 1) * 130]
                B.dma(sy, ld["vt"][:, :].rearrange("p (tt w) -> p tt w",
                                                   w=130), srcv, vname)
            block.sync(syl)
            lc[hpx] = (B.cnt[sname], B.cnt[vname])

        def emit_u_loads(hpx):
            if causal:
                return
            uname = f"U{hpx % 2}"
            vs_tgt = B.cnt["VS"]

            def gu(g, hpx=hpx, uname=uname, vs_tgt=vs_tgt):
                B.w(g, "VS", vs_tgt)
                B.dma(g, u0[:, 1:UWB],
                      AP(sexpD, (2 * hpx) * SW + UOFF,
                         [[1, 128], [1, UWB - 1]]), uname)
                B.dma(g, u1[:, 1:UWB],
                      AP(sexpD, (2 * hpx + 1) * SW + UOFF,
                         [[1, 128], [1, UWB - 1]]), uname)
            block.gpsimd(gu)

        emit_hp_loads(0, dict(B.cnt))
        emit_u_loads(0)
        qw = S if causal else SOUT + 128
        khb = [le(nc.sbuf_tensor(B.nm("khb"), [128, S], BF16))
               for _ in range(2)]
        qhb = [le(nc.sbuf_tensor(B.nm("qhb"), [128, qw], BF16))
               for _ in range(2)]

        with (
            nc.psum_tensor(B.nm("pp0"), [128, TB], F32) as pp0,
            nc.psum_tensor(B.nm("pp1"), [128, TB], F32) as pp1,
        ):
            ent_p0 = dict(B.cnt)
            jobs0 = _attn_jobs(cst, causal, ldsets[0]["wA"], ldsets[0]["wB"],
                               khb[0], qhb[0])
            pts0, pas0 = B.cnt["SS"], B.cnt["AS"]
            pps = (pp0, pp1)

            def tp0(t):
                B.entry(t, ent_p0)
                B.w(t, "L0", lc[0][0])
                for n, (w_sb, dst, rhs_fn, width) in enumerate(jobs0):
                    ps = pps[n % 2]
                    B.w(t, "AS", pas0 + max(0, n - 1))
                    for cc in range(CH):
                        i = t.matmul(ps[:, 0:width],
                                     w_sb[:, cc * 128:(cc + 1) * 128],
                                     rhs_fn(cc), start=(cc == 0),
                                     stop=(cc == CH - 1))
                    B.inc(i, "SS")
            block.tensor(tp0)

            def sp0_(s):
                B.entry(s, ent_p0)
                for n, (w_sb, dst, rhs_fn, width) in enumerate(jobs0):
                    B.w(s, "SS", pts0 + n + 1)
                    i = s.copy(dst, pps[n % 2][:, 0:width])
                    B.inc(i, "AS")
            block.scalar(sp0_)

        for hp in range(HP):
            _attn_head_pair(B, block, oaD, vtD, sexpD, cst, hp, causal,
                            wq_d, wk_d, winT_d, ldsets[hp % 2], lc,
                            emit_hp_loads, emit_u_loads, khb, qhb, ldsets)

    if causal:
        _attn_wo_residual(B, block, xF, oaD, wo_d)
    else:
        _halo_finalize(B, block, cst, zD, xF2)
        _attn_split_residual(B, block, xF, xF2, oaD, cst)


def _halo_finalize(B, block, cst, zD, xF2):
    nc = B.nc
    ent = dict(B.cnt)
    with (
        nc.sbuf_tensor(B.nm("hx3"), [128, CH], F32) as hx3,
        nc.sbuf_tensor(B.nm("hsq"), [128, CH], F32) as hsq,
        nc.sbuf_tensor(B.nm("hfold"), [128, CH], F32) as hfold,
        nc.sbuf_tensor(B.nm("hrow"), [1, CH], F32) as hrow,
        nc.sbuf_tensor(B.nm("hsc"), [1, 2], F32) as hsc,
        nc.psum_tensor(B.nm("ps_h"), [1, 2 * TB], F32) as ps_h,
        nc.psum_tensor(B.nm("ps_hb"), [128, 2], F32) as ps_hb,
    ):
        def sy1(sy):
            B.entry(sy, ent)
            B.dma(sy, cst["zrow"][0:1, :], cst["zhst"][64:65, :], "DI")
        block.sync(sy1)
        di_z = B.cnt["DI"]

        def s0(s):
            B.entry(s, ent)
            B.w(s, "DI", di_z)
            B.chain(s, s.activation(cst["zrow"][0:1, :], cst["zrow"][0:1, :],
                                    AF.Ln), "CA")
            i = s.activation(cst["zrow"][0:1, :], cst["zrow"][0:1, :],
                             AF.Exp, scale=-1.0)
            B.inc(i, "AS")
        block.scalar(s0)
        vs_rz = B.cnt["AS"]

        def sy2(sy):
            B.w(sy, "AS", vs_rz)
            B.dma(sy, zD[0:1, :], cst["zrow"][0:1, :], "DO")
            B.w(sy, "DO", B.cnt["DO"])
            B.dma(sy, cst["zcol"][0:64, :],
                  AP(zD, 0, [[0, 64], [1, CH]]), "DI")
            B.dma(sy, cst["zcol"][64:128, :],
                  AP(zD, CH, [[0, 64], [1, CH]]), "DI")
        block.sync(sy2)
        di_zc = B.cnt["DI"]

        def v2(v):
            B.w(v, "DI", di_zc)
            B.chain(v, v.tensor_mul(hx3[:, :], cst["ohalo"][:, :],
                                    cst["zcol"][:, :]), "CV")
            B.chain(v, v.tensor_add(
                hx3[:, :], hx3[:, :],
                AP(cst["x2hp"], 127, [[CH * 128, 128], [128, CH]])), "CV")
            B.chain(v, v.tensor_mul(hsq[:, :], hx3[:, :], hx3[:, :]), "CV")
            B.chain(v, v.tensor_add(hfold[:, 0:4], hx3[:, 0:4],
                                    hx3[:, 4:8]), "CV")
            B.chain(v, v.tensor_add(hfold[:, 0:2], hfold[:, 0:2],
                                    hfold[:, 2:4]), "CV")
            B.chain(v, v.tensor_add(hfold[:, 0:1], hfold[:, 0:1],
                                    hfold[:, 1:2]), "CV")
            B.chain(v, v.tensor_add(hfold[:, 4:8], hsq[:, 0:4],
                                    hsq[:, 4:8]), "CV")
            B.chain(v, v.tensor_add(hfold[:, 4:6], hfold[:, 4:6],
                                    hfold[:, 6:8]), "CV")
            i = v.tensor_add(hfold[:, 4:5], hfold[:, 4:5], hfold[:, 5:6])
            B.inc(i, "VS")
        block.vector(v2)
        vs_x3 = B.cnt["VS"]

        def t1(t):
            B.entry(t, ent)
            B.w(t, "VS", vs_x3)
            i = t.matmul(ps_h[0:1, 0:1], cst["ones128"][:, :],
                         hfold[:, 0:1], start=True, stop=True)
            B.inc(i, "SS")
            i = t.matmul(ps_h[0:1, 1:2], cst["ones128"][:, :],
                         hfold[:, 4:5], start=True, stop=True)
            B.inc(i, "SS")
        block.tensor(t1)
        ss_h = B.cnt["SS"]

        def v3(v):
            B.w(v, "SS", ss_h)
            B.chain(v, v.tensor_scalar_mul(hsc[0:1, 0:2], ps_h[0:1, 0:2],
                                           1.0 / D), "CV")
            B.chain(v, v.tensor_mul(hrow[0:1, 0:1], hsc[0:1, 0:1],
                                    hsc[0:1, 0:1]), "CV")
            B.chain(v, v.tensor_sub(hsc[0:1, 1:2], hsc[0:1, 1:2],
                                    hrow[0:1, 0:1]), "CV")
            i = v.tensor_scalar_add(hsc[0:1, 1:2], hsc[0:1, 1:2], EPS)
            B.inc(i, "VS")
        block.vector(v3)
        vs_st = B.cnt["VS"]

        def s1(s):
            B.entry(s, ent)
            B.w(s, "VS", vs_st)
            B.chain(s, s.activation(hrow[0:1, 1:2], hsc[0:1, 1:2], AF.Ln),
                    "CA")
            i = s.activation(hsc[0:1, 1:2], hrow[0:1, 1:2], AF.Exp,
                             scale=-0.5)
            B.inc(i, "AS")
        block.scalar(s1)
        as_r = B.cnt["AS"]

        def t2(t):
            B.w(t, "AS", as_r)
            i = t.matmul(ps_hb[:, 0:2], cst["ones1"][0:1, :], hsc[0:1, 0:2],
                         start=True, stop=True)
            B.inc(i, "SS")
        block.tensor(t2)
        ss_b = B.cnt["SS"]

        def v4(v):
            B.w(v, "SS", ss_b)
            B.chain(v, v.tensor_scalar(hx3[:, :], hx3[:, :],
                                       ps_hb[:, 0:1], None, ALU.subtract),
                    "CV")
            B.chain(v, v.tensor_scalar(hx3[:, :], hx3[:, :],
                                       ps_hb[:, 1:2], None, ALU.mult), "CV")
            B.chain(v, v.tensor_scalar(hx3[:, :], hx3[:, :],
                                       cst["hmB"][:, 0:1], None, ALU.mult),
                    "CV")
            i = v.tensor_copy(
                AP(cst["xnb"], 0, [[CH * (S + 1), 128], [S + 1, CH]]),
                hx3[:, :])
            B.inc(i, "VS")
        block.vector(v4)


def _attn_split_residual(B, block, xF, xF2, oaD, cst):
    nc = B.nc
    ent_p = dict(B.cnt)
    with (
        nc.sbuf_tensor(B.nm("pr_x0"), [128, CH * TB], F32) as pr_x0,
        nc.sbuf_tensor(B.nm("pr_x1"), [128, CH * TB], F32) as pr_x1,
        nc.sbuf_tensor(B.nm("pr_o0"), [128, CH * TB], BF16) as pr_o0,
        nc.sbuf_tensor(B.nm("pr_o1"), [128, CH * TB], BF16) as pr_o1,
    ):
        xs = (pr_x0, pr_x1)
        os_ = (pr_o0, pr_o1)
        dp = (B.cnt["D0"], B.cnt["D1"])
        vs0, do0 = B.cnt["VS"], B.cnt["DO"]

        def syio(sy):
            B.entry(sy, ent_p)
            sv = sy.value_load(cst["ridx"][0:1, 0:1])
            for tb in range(NQB):
                B.w(sy, "VS", vs0 + max(0, tb - 1))
                if tb >= 2:
                    B.w(sy, "DO", do0 + 16 * (tb - 1))
                B.dma(sy, os_[tb % 2][:, :].rearrange("p (c t) -> p c t",
                                                      t=TB),
                      oaD.ap().rearrange(
                          "(c p) s -> p c s",
                          p=128)[:, :, tb * TB:(tb + 1) * TB], f"D{tb % 2}")
                B.dma(sy, xs[tb % 2][:, :].rearrange("p (c t) -> p c t",
                                                     t=TB),
                      AP(xF, sv + tb * TB,
                         [[S, 128], [128 * S, CH], [1, TB]]), f"D{tb % 2}")
                if tb >= 1:
                    B.w(sy, "VS", vs0 + tb)
                    B.dma(sy, xf_ap(xF2)[:, :, (tb - 1) * TB:tb * TB],
                          xs[(tb - 1) % 2][:, :].rearrange(
                              "p (c t) -> p c t", t=TB), "DO")
            B.w(sy, "VS", vs0 + NQB)
            B.dma(sy, xf_ap(xF2)[:, :, (NQB - 1) * TB:NQB * TB],
                  xs[(NQB - 1) % 2][:, :].rearrange("p (c t) -> p c t",
                                                    t=TB), "DO")
        block.sync(syio)

        def vb(v):
            B.entry(v, ent_p)
            for tb in range(NQB):
                B.w(v, f"D{tb % 2}", dp[tb % 2] + 32 * (tb // 2 + 1))
                i = v.tensor_add(xs[tb % 2][:, :], xs[tb % 2][:, :],
                                 os_[tb % 2][:, :])
                B.inc(i, "VS")
        block.vector(vb)


def _attn_jobs(cst, causal, wA, wB, k_dst, q_dst):
    jobs = []
    for tb in range(NTB):
        jobs.append((wB, k_dst[:, tb * TB:(tb + 1) * TB],
                     (lambda tb: lambda c: xn_sl(cst, c, tb * TB,
                                                 TB))(tb), TB))
    if causal:
        for tb in range(NTB):
            jobs.append((wA, q_dst[:, tb * TB:(tb + 1) * TB],
                         (lambda tb: lambda c: xn_sl(cst, c, tb * TB,
                                                     TB))(tb), TB))
    else:
        xo = cst["xn2own"]
        for qb in range(NQB):
            jobs.append((wA, q_dst[:, qb * TB:(qb + 1) * TB],
                         (lambda qb: lambda c: xo[
                             :, c * SOUT + qb * TB:
                             c * SOUT + (qb + 1) * TB])(qb), TB))
        jobs.append((wA, q_dst[:, SOUT:SOUT + 1],
                     lambda c: cst["xnqh"][:, c * 128 + 127:
                                           c * 128 + 128], 1))
    return jobs


def _attn_head_pair(B, block, oaD, vtD, sexpD, cst, hp, causal,
                    wq_d, wk_d, winT_d, ld, lc, emit_hp_loads,
                    emit_u_loads, khb, qhb, ldsets):
    nc = B.nc
    ent_h = dict(B.cnt)
    if hp + 1 < HP:
        emit_hp_loads(hp + 1, ent_h)
    NQ = NTB if causal else NQB
    wA, wB, vt_hp = ld["wA"], ld["wB"], ld["vt"]
    u0, u1 = ld["u0"], ld["u1"]
    lsem, vsem = f"L{hp % 2}", f"V{hp % 2}"
    lc_hp, vc_hp = lc[hp]
    k_hp, q_hp = khb[hp % 2], qhb[hp % 2]
    if hp + 1 < HP:
        ld2 = ldsets[(hp + 1) % 2]
        jobs2 = _attn_jobs(cst, causal, ld2["wA"], ld2["wB"],
                           khb[(hp + 1) % 2], qhb[(hp + 1) % 2])
        l2sem, lc2 = f"L{(hp + 1) % 2}", None
    else:
        jobs2 = []
    from contextlib import ExitStack
    with ExitStack() as ctx:
        e = ctx.enter_context
        psb0 = e(nc.sbuf_tensor(B.nm(f"psb0_{hp}"), [128, 2 * TB], BF16))
        psb1 = e(nc.sbuf_tensor(B.nm(f"psb1_{hp}"), [128, 2 * TB], BF16))
        psb2 = e(nc.sbuf_tensor(B.nm(f"psb2_{hp}"), [128, 2 * TB], BF16))
        ob = [[e(nc.sbuf_tensor(B.nm(f"ob{hp}"), [64, TB], BF16))
               for _ in range(2)] for _ in range(2)]
        zrP = e(nc.sbuf_tensor(B.nm(f"zrP{hp}"), [65, 2 * TB], F32))
        rzP = e(nc.sbuf_tensor(B.nm(f"rzP{hp}"), [65, 2 * TB], F32))
        bcsP = e(nc.sbuf_tensor(B.nm(f"bcP{hp}"), [128, 2 * TB], F32))
        sp0 = e(nc.psum_tensor(B.nm(f"sp0_{hp}"), [128, 2 * TB], F32))
        sp1 = e(nc.psum_tensor(B.nm(f"sp1_{hp}"), [128, 2 * TB], F32))
        pv = [[e(nc.psum_tensor(B.nm(f"pv{hp}"), [65, TB], F32))
               for _ in range(2)] for _ in range(2)]
        rzD = nc.dram_tensor(B.nm(f"rzD{hp}"), [2, 2 * TB], F32)

        iters = []
        g_first = {}
        for qb in range(NQ):
            cmax = 4 * qb + 4 if causal else KC
            g_first[qb] = len(iters)
            for ci in range(cmax):
                if causal:
                    kind = "mask" if ci >= 4 * qb else "plain"
                else:
                    kind = "bias"
                iters.append((qb, ci, cmax, kind))
        nit = len(iters)
        units = []
        step = max(1, nit // len(jobs2)) if jobs2 else 0
        nins = 0
        for g in range(nit):
            units.append(("S", g))
            if jobs2 and (g + 1) % step == 0 and nins < len(jobs2):
                units.append(("P", nins))
                nins += 1
        while nins < len(jobs2):
            units.append(("P", nins))
            nins += 1
        s2k = {}
        for k, (tp_u, idx) in enumerate(units):
            if tp_u == "S":
                s2k[idx] = k

        c = dict(B.cnt)
        ss_u = [0] * len(units)
        as_e = [0] * nit
        vs_m = [None] * nit
        ps_p = [0] * nit
        vs_pe = [0] * len(jobs2)
        as_rz, vs_ob = {}, {}
        do_rz, di_bc, do_ob = {}, {}, {}

        def sim_evacA(qb):
            c["AS"] += 1
            as_rz[qb] = c["AS"]

        def sim_evacV(qb):
            c["VS"] += 2
            vs_ob[qb] = c["VS"]

        def sim_evacS(qb):
            c["DO"] += 16
            do_rz[qb] = c["DO"]
            c["DI"] += 16
            di_bc[qb] = c["DI"]
            c[f"D{qb % 2}"] += 32
            do_ob[qb] = c[f"D{qb % 2}"]

        for k, (tp_u, idx) in enumerate(units):
            if tp_u == "S":
                g = idx
                qb, ci, cmax, kind = iters[g]
                if ci == 0 and qb >= 1:
                    sim_evacA(qb - 1)
                c["SS"] += 1
                ss_u[k] = c["SS"]
                c["AS"] += 1
                as_e[g] = c["AS"]
                if ci == 4 and qb >= 1:
                    sim_evacV(qb - 1)
                if kind in ("mask", "bias"):
                    c["VS"] += 1
                    vs_m[g] = c["VS"]
                if g >= 2:
                    c["PS"] += 1
                    ps_p[g - 2] = c["PS"]
            else:
                c["SS"] += 1
                ss_u[k] = c["SS"]
                c["VS"] += 1
                vs_pe[idx] = c["VS"]
        for gg in (nit - 2, nit - 1):
            c["PS"] += 1
            ps_p[gg] = c["PS"]
        sim_evacA(NQ - 1)
        sim_evacV(NQ - 1)
        for qb in range(NQ):
            sim_evacS(qb)
        if not causal:
            c["SS"] += 1
            hs_ss = c["SS"]
            c["AS"] += 2
            a_he = c["AS"]
            c["VS"] += 1
            v_hm = c["VS"]
            c["PS"] += 1
            hs_ps = c["PS"]
            c["AS"] += 4
            a_hc = c["AS"]
            c["DI"] += 16
            d_hm = c["DI"]

        sps = (sp0, sp1)
        pbs = (psb0, psb1, psb2)
        as_prior = B.cnt["AS"]
        ps_prior = B.cnt["PS"]
        vs_prior = B.cnt["VS"]
        lc2v = lc[hp + 1][0] if jobs2 else 0

        def slot_free(t, k):
            if k < 2:
                return
            pt, pidx = units[k - 2]
            if pt == "S":
                B.w(t, "AS", as_e[pidx])
            else:
                B.w(t, "VS", vs_pe[pidx])

        def t_att(t):
            B.w(t, lsem, lc_hp)
            B.w(t, vsem, vc_hp)
            B.w(t, "AS", as_prior)
            B.w(t, "VS", vs_prior)

            def pv_part(j):
                qb, ci, cmax, kind = iters[j]
                par = qb % 2
                if kind == "plain":
                    B.w(t, "AS", as_e[j])
                else:
                    B.w(t, "VS", vs_m[j])
                if ci == 0 and qb >= 2:
                    B.w(t, "VS", vs_ob[qb - 2])
                pb = pbs[j % 3]
                t.matmul(pv[par][0][:, :], vt_hp[:, ci * 130:ci * 130 + 65],
                         pb[:, 0:TB], start=(ci == 0), stop=(ci == cmax - 1))
                i = t.matmul(pv[par][1][:, :],
                             vt_hp[:, ci * 130 + 65:ci * 130 + 130],
                             pb[:, TB:2 * TB], start=(ci == 0),
                             stop=(ci == cmax - 1))
                B.inc(i, "PS")

            for k, (tp_u, idx) in enumerate(units):
                slot_free(t, k)
                sp = sps[k % 2]
                if tp_u == "S":
                    g = idx
                    qb, ci, cmax, kind = iters[g]
                    k0 = ci * 128
                    t.matmul(sp[:, 0:TB], k_hp[0:64, k0:k0 + 128],
                             q_hp[0:64, qb * TB:(qb + 1) * TB],
                             start=True, stop=True, tile_position=(0, 0))
                    i = t.matmul(sp[:, TB:2 * TB],
                                 k_hp[64:128, k0:k0 + 128],
                                 q_hp[64:128, qb * TB:(qb + 1) * TB],
                                 start=True, stop=True,
                                 tile_position=(64, 0))
                    B.inc(i, "SS")
                    if g >= 2:
                        pv_part(g - 2)
                else:
                    j = idx
                    if j == 0:
                        B.w(t, l2sem, lc2v)
                    w_sb, dst, rhs_fn, width = jobs2[j]
                    for cc in range(CH):
                        i = t.matmul(sp[:, 0:width],
                                     w_sb[:, cc * 128:(cc + 1) * 128],
                                     rhs_fn(cc), start=(cc == 0),
                                     stop=(cc == CH - 1))
                    B.inc(i, "SS")
            if nit >= 2:
                pv_part(nit - 2)
            pv_part(nit - 1)
            if not causal:
                for kk in (len(units) - 2, len(units) - 1):
                    pt, pidx = units[kk]
                    if pt == "S":
                        B.w(t, "AS", as_e[pidx])
                    else:
                        B.w(t, "VS", vs_pe[pidx])
                for cc in range(KC):
                    t.matmul(sp0[:, cc:cc + 1],
                             k_hp[0:64, cc * 128:(cc + 1) * 128],
                             q_hp[0:64, SOUT:SOUT + 1],
                             start=True, stop=True, tile_position=(0, 0))
                    i = t.matmul(sp0[:, TB + cc:TB + cc + 1],
                                 k_hp[64:128, cc * 128:(cc + 1) * 128],
                                 q_hp[64:128, SOUT:SOUT + 1],
                                 start=True, stop=True,
                                 tile_position=(64, 0))
                B.inc(i, "SS")
                B.w(t, "VS", v_hm)
                B.w(t, "VS", vs_ob[NQ - 2])
                for cc in range(KC):
                    t.matmul(pv[0][0][:, 0:1],
                             vt_hp[:, cc * 130:cc * 130 + 65],
                             pbs[0][:, cc:cc + 1], start=(cc == 0),
                             stop=(cc == KC - 1))
                    i = t.matmul(pv[0][1][:, 0:1],
                                 vt_hp[:, cc * 130 + 65:cc * 130 + 130],
                                 pbs[0][:, TB + cc:TB + cc + 1],
                                 start=(cc == 0), stop=(cc == KC - 1))
                B.inc(i, "PS")
        block.tensor(t_att)

        def emit_evacA(s, qb):
            par = qb % 2
            B.w(s, "PS", ps_p[g_first[qb] + iters[g_first[qb]][2] - 1])
            if qb >= 2:
                B.w(s, "DO", do_rz[qb - 2])
            zr_r = zrP[64 * par:64 * par + 1, :]
            rz_r = rzP[64 * par:64 * par + 1, :]
            B.chain(s, s.activation(zr_r[:, 0:TB],
                                    pv[par][0][64:65, :], AF.Ln), "CA")
            B.chain(s, s.activation(zr_r[:, TB:2 * TB],
                                    pv[par][1][64:65, :], AF.Ln), "CA")
            i = s.activation(rz_r, zr_r, AF.Exp, scale=-1.0)
            B.inc(i, "AS")

        def s_att(s):
            B.entry(s, ent_h)
            for k, (tp_u, idx) in enumerate(units):
                if tp_u != "S":
                    continue
                g = idx
                qb, ci, cmax, kind = iters[g]
                if ci == 0 and qb >= 1:
                    emit_evacA(s, qb - 1)
                B.w(s, "SS", ss_u[k])
                B.w(s, "PS", ps_p[g - 3] if g >= 3 else ps_prior)
                i = s.activation(pbs[g % 3][:, :], sps[k % 2][:, :],
                                 AF.Exp, scale=SCALE)
                B.inc(i, "AS")
            emit_evacA(s, NQ - 1)
            if not causal:
                B.w(s, "SS", hs_ss)
                B.w(s, "PS", ps_p[nit - 1])
                i = s.activation(pbs[0][:, 0:KC], sp0[:, 0:KC], AF.Exp,
                                 scale=SCALE)
                B.inc(i, "AS")
                i = s.activation(pbs[0][:, TB:TB + KC], sp0[:, TB:TB + KC],
                                 AF.Exp, scale=SCALE)
                B.inc(i, "AS")
                B.w(s, "PS", hs_ps)
                B.w(s, "DI", ent_h["DI"])
                i = s.copy(cst["ohalo"][0:64, hp:hp + 1],
                           pv[0][0][0:64, 0:1])
                B.inc(i, "AS")
                i = s.copy(cst["hstage"][0:64, 0:1], pv[0][1][0:64, 0:1])
                B.inc(i, "AS")
                i = s.copy(cst["zhst"][64:65, hp:hp + 1],
                           pv[0][0][64:65, 0:1])
                B.inc(i, "AS")
                i = s.copy(cst["zhst"][64:65, 8 + hp:9 + hp],
                           pv[0][1][64:65, 0:1])
                B.inc(i, "AS")
        block.scalar(s_att)

        def emit_evacV(v, qb):
            par = qb % 2
            B.w(v, "PS", ps_p[g_first[qb] + iters[g_first[qb]][2] - 1])
            B.w(v, "DI", di_bc[qb])
            if qb >= 2:
                B.w(v, f"D{qb % 2}", do_ob[qb - 2])
            i = v.tensor_mul(ob[par][0][:, :], pv[par][0][0:64, :],
                             bcsP[64 * par:64 * par + 64, 0:TB])
            B.inc(i, "VS")
            i = v.tensor_mul(ob[par][1][:, :], pv[par][1][0:64, :],
                             bcsP[64 * par:64 * par + 64, TB:2 * TB])
            B.inc(i, "VS")

        def v_att(v):
            B.entry(v, ent_h)
            for k, (tp_u, idx) in enumerate(units):
                if tp_u == "P":
                    j = idx
                    w_sb, dst, rhs_fn, width = jobs2[j]
                    B.w(v, "SS", ss_u[k])
                    i = v.tensor_copy(dst, sps[k % 2][:, 0:width])
                    B.inc(i, "VS")
                    continue
                g = idx
                qb, ci, cmax, kind = iters[g]
                if ci == 4 and qb >= 1:
                    emit_evacV(v, qb - 1)
                if kind in ("mask", "bias"):
                    B.w(v, "AS", as_e[g])
                    if g >= 3:
                        B.w(v, "PS", ps_p[g - 3])
                    pb = pbs[g % 3]
                    if kind == "mask":
                        r = ci - 4 * qb
                        for e2 in range(2):
                            i = v.tensor_mul(
                                pb[:, e2 * TB:(e2 + 1) * TB],
                                pb[:, e2 * TB:(e2 + 1) * TB],
                                cst["masks"][:, r * TB:(r + 1) * TB])
                    else:
                        M0 = UOFF - qb * TB + ci * 128 + 1
                        for e2, u in ((0, u0), (1, u1)):
                            i = v.tensor_mul(
                                pb[:, e2 * TB:(e2 + 1) * TB],
                                pb[:, e2 * TB:(e2 + 1) * TB],
                                u[:, M0 - (TB - 1):M0 + 1][:, ::-1])
                    B.inc(i, "VS")
            emit_evacV(v, NQ - 1)
            if not causal:
                B.w(v, "AS", a_he)
                v.tensor_mul(pbs[0][:, 0:KC], pbs[0][:, 0:KC],
                             u0[:, UOFF + 1:UWB:128])
                i = v.tensor_mul(pbs[0][:, TB:TB + KC],
                                 pbs[0][:, TB:TB + KC],
                                 u1[:, UOFF + 1:UWB:128])
                B.inc(i, "VS")
        block.vector(v_att)

        def sy_att(sy):
            B.entry(sy, ent_h)
            for qb in range(NQ):
                par = qb % 2
                B.w(sy, "AS", as_rz[qb])
                B.dma(sy, rzD.ap()[par:par + 1, :],
                      rzP[64 * par:64 * par + 1, :], "DO")
                B.w(sy, "DO", do_rz[qb])
                if qb >= 2:
                    B.w(sy, "VS", vs_ob[qb - 2])
                B.dma(sy, bcsP[64 * par:64 * par + 64, :],
                      AP(rzD, par * 2 * TB, [[0, 64], [1, 2 * TB]]), "DI")
                B.w(sy, "VS", vs_ob[qb])
                r0 = hp * 128
                B.dma(sy, oaD.ap()[r0:r0 + 64, qb * TB:(qb + 1) * TB],
                      ob[par][0][:, :], f"D{par}")
                B.dma(sy, oaD.ap()[r0 + 64:r0 + 128, qb * TB:(qb + 1) * TB],
                      ob[par][1][:, :], f"D{par}")
            if not causal:
                B.w(sy, "AS", a_hc)
                B.dma(sy, cst["ohalo"][64:128, hp:hp + 1],
                      cst["hstage"][0:64, 0:1], "DI")
        block.sync(sy_att)

        for s_name in ("SS", "AS", "VS", "PS", "DO", "D0", "D1", "DI"):
            assert B.cnt[s_name] == c[s_name], (s_name, B.cnt[s_name],
                                                c[s_name], hp)
        if hp + 1 < HP:
            emit_u_loads(hp + 1)


def _attn_wo_residual(B, block, xF, oaD, wo_d):
    nc = B.nc
    ent_w = dict(B.cnt)
    with (
        nc.sbuf_tensor(B.nm("wo_sb"), [128, CH * D], BF16) as wo_sb,
        nc.sbuf_tensor(B.nm("wr_x0"), [128, CH * TB], F32) as wr_x0,
        nc.sbuf_tensor(B.nm("wr_o0"), [128, CH * TB], BF16) as wr_o0,
        nc.psum_tensor(B.nm("ps_w0"), [128, TB], F32) as ps_w0,
        nc.psum_tensor(B.nm("ps_w1"), [128, TB], F32) as ps_w1,
    ):
        def gw(g):
            B.entry(g, ent_w)
            B.dma(g, wo_sb[:, :].rearrange("p (cb j) -> p cb j", j=D),
                  wo_d.ap().rearrange("(cb p) j -> p cb j", p=128), "DG")
        block.gpsimd(gw)
        dg_w = B.cnt["DG"]
        pss = (ps_w0, ps_w1)
        di0, ts0, vs0, do0 = (B.cnt["DI"], B.cnt["SS"], B.cnt["VS"],
                              B.cnt["DO"])

        def syio(sy):
            B.entry(sy, ent_w)
            for tb in range(NTB):
                if tb > 0:
                    B.w(sy, "VS", vs0 + tb * CH)
                    B.dma(sy, xf_ap(xF)[:, :, (tb - 1) * TB:tb * TB],
                          wr_x0[:, :].rearrange("p (c t) -> p c t", t=TB),
                          "DO")
                    B.w(sy, "DO", B.cnt["DO"])
                B.dma(sy, wr_o0[:, :].rearrange("p (c t) -> p c t", t=TB),
                      oaD.ap().rearrange(
                          "(c p) s -> p c s",
                          p=128)[:, :, tb * TB:(tb + 1) * TB], "DI")
                B.dma(sy, wr_x0[:, :].rearrange("p (c t) -> p c t", t=TB),
                      xf_ap(xF)[:, :, tb * TB:(tb + 1) * TB], "DI")
            B.w(sy, "VS", vs0 + NTB * CH)
            B.dma(sy, xf_ap(xF)[:, :, (NTB - 1) * TB:NTB * TB],
                  wr_x0[:, :].rearrange("p (c t) -> p c t", t=TB), "DO")
        block.sync(syio)

        def tb_(t):
            B.entry(t, ent_w)
            B.w(t, "DG", dg_w)
            for tb in range(NTB):
                B.w(t, "DI", di0 + 32 * (tb + 1))
                for oc in range(CH):
                    ps = pss[oc % 2]
                    n = tb * CH + oc
                    B.w(t, "VS", vs0 + max(0, n - 1))
                    for cc in range(CH):
                        i = t.matmul(ps[:, :],
                                     wo_sb[:, cc * D + oc * 128:
                                           cc * D + (oc + 1) * 128],
                                     wr_o0[:, cc * TB:(cc + 1) * TB],
                                     start=(cc == 0), stop=(cc == CH - 1))
                    B.inc(i, "SS")
        block.tensor(tb_)

        def vb(v):
            B.entry(v, ent_w)
            for tb in range(NTB):
                B.w(v, "DI", di0 + 32 * (tb + 1))
                for oc in range(CH):
                    n = tb * CH + oc
                    B.w(v, "SS", ts0 + n + 1)
                    i = v.tensor_add(wr_x0[:, oc * TB:(oc + 1) * TB],
                                     wr_x0[:, oc * TB:(oc + 1) * TB],
                                     pss[oc % 2][:, :])
                    B.inc(i, "VS")
        block.vector(vb)


def _attn_plain_residual(B, block, xF, oaD):
    nc = B.nc
    ent_p = dict(B.cnt)
    with (
        nc.sbuf_tensor(B.nm("pr_x0"), [128, CH * TB], F32) as pr_x0,
        nc.sbuf_tensor(B.nm("pr_o0"), [128, CH * TB], BF16) as pr_o0,
    ):
        di0, vs0, do0 = B.cnt["DI"], B.cnt["VS"], B.cnt["DO"]

        def syio(sy):
            B.entry(sy, ent_p)
            for tb in range(NTB):
                if tb > 0:
                    B.w(sy, "VS", vs0 + tb)
                    B.dma(sy, xf_ap(xF)[:, :, (tb - 1) * TB:tb * TB],
                          pr_x0[:, :].rearrange("p (c t) -> p c t", t=TB),
                          "DO")
                    B.w(sy, "DO", B.cnt["DO"])
                B.dma(sy, pr_o0[:, :].rearrange("p (c t) -> p c t", t=TB),
                      oaD.ap().rearrange(
                          "(c p) s -> p c s",
                          p=128)[:, :, tb * TB:(tb + 1) * TB], "DI")
                B.dma(sy, pr_x0[:, :].rearrange("p (c t) -> p c t", t=TB),
                      xf_ap(xF)[:, :, tb * TB:(tb + 1) * TB], "DI")
            B.w(sy, "VS", vs0 + NTB)
            B.dma(sy, xf_ap(xF)[:, :, (NTB - 1) * TB:NTB * TB],
                  pr_x0[:, :].rearrange("p (c t) -> p c t", t=TB), "DO")
        block.sync(syio)

        def vb(v):
            B.entry(v, ent_p)
            for tb in range(NTB):
                B.w(v, "DI", di0 + 32 * (tb + 1))
                i = v.tensor_add(pr_x0[:, :], pr_x0[:, :], pr_o0[:, :])
                B.inc(i, "VS")
        block.vector(vb)


def emit_ffn(B, block, xF, cst, wrT_d, wk2T_d, wv2T_d, ntok=S):
    nc = B.nc
    NB = ntok // TB
    ent_f = dict(B.cnt)
    with (
        nc.sbuf_tensor(B.nm("fw_r"), [128, CH * D], BF16) as fw_r,
        nc.sbuf_tensor(B.nm("fw_k"), [128, CH * D], BF16) as fw_k,
        nc.sbuf_tensor(B.nm("fw_v"), [128, CH * D], BF16) as fw_v,
        nc.sbuf_tensor(B.nm("ff_xa"), [128, CH * TB], F32) as ff_xa,
        nc.sbuf_tensor(B.nm("ff_xb"), [128, CH * TB], F32) as ff_xb,
        nc.sbuf_tensor(B.nm("ff_xr0"), [128, CH * TB], BF16) as ff_xr0,
        nc.sbuf_tensor(B.nm("ff_xk0"), [128, CH * TB], BF16) as ff_xk0,
        nc.sbuf_tensor(B.nm("ff_xr1"), [128, CH * TB], BF16) as ff_xr1,
        nc.sbuf_tensor(B.nm("ff_xk1"), [128, CH * TB], BF16) as ff_xk1,
        nc.sbuf_tensor(B.nm("ff_r"), [128, CH * TB], F32) as ff_r,
        nc.sbuf_tensor(B.nm("ff_ka"), [128, CH * TB], BF16) as ff_ka,
        nc.sbuf_tensor(B.nm("ff_t2"), [128, TB], F32) as ff_t2,
        nc.psum_tensor(B.nm("ps_f0"), [128, TB], F32) as ps_f0,
        nc.psum_tensor(B.nm("ps_f1"), [128, TB], F32) as ps_f1,
        nc.psum_tensor(B.nm("ps_f2"), [128, TB], F32) as ps_f2,
        nc.psum_tensor(B.nm("ps_f3"), [128, TB], F32) as ps_f3,
    ):
        def gw(g):
            B.entry(g, ent_f)
            for w_sb, w_d in ((fw_r, wrT_d), (fw_k, wk2T_d), (fw_v, wv2T_d)):
                B.dma(g, w_sb[:, :].rearrange("p (cb j) -> p cb j", j=D),
                      w_d.ap().rearrange("(cb p) j -> p cb j", p=128), "DG")
        block.gpsimd(gw)
        dg_w = B.cnt["DG"]
        xn_vs = B.cnt["VS"]
        pss_sets = ((ps_f0, ps_f1), (ps_f2, ps_f3))
        fxs = (ff_xa, ff_xb)
        xrs = (ff_xr0, ff_xr1)
        xks = (ff_xk0, ff_xk1)
        ssp = {}
        vsm = {}
        prev_vs3 = 0

        def emit_vmix(tbx, first):
            war = ssp.get(tbx - 2)

            def vmix(v, tbx=tbx, first=first, war=war):
                if first:
                    B.entry(v, ent_f)
                    B.w(v, "VS", xn_vs)
                if war is not None:
                    B.w(v, "SS", war)
                for mu1, dst in ((cst["mur1"], xrs[tbx % 2]),
                                 (cst["muk1"], xks[tbx % 2])):
                    for c in range(CH):
                        base = c * (S + 1) + 1 + tbx * TB
                        xp_c = cst["xnb"][:, base - 1:base - 1 + TB]
                        i = v.tensor_scalar(dst[:, c * TB:(c + 1) * TB],
                                            xp_c, mu1[:, c:c + 1], None,
                                            ALU.mult)
                B.chain(v, i, "CV")
                for mu_sb, dst in ((cst["mur"], xrs[tbx % 2]),
                                   (cst["muk"], xks[tbx % 2])):
                    for c in range(CH):
                        base = c * (S + 1) + 1 + tbx * TB
                        x_c = cst["xnb"][:, base:base + TB]
                        i = v.scalar_tensor_tensor(
                            dst[:, c * TB:(c + 1) * TB],
                            x_c, mu_sb[:, c:c + 1],
                            dst[:, c * TB:(c + 1) * TB],
                            ALU.mult, ALU.add)
                B.inc(i, "VS")
            block.vector(vmix)
            vsm[tbx] = B.cnt["VS"]

        emit_vmix(0, True)

        def sy0(sy):
            B.entry(sy, ent_f)
            B.dma(sy, ff_xa[:, :].rearrange("p (c t) -> p c t", t=TB),
                  xf_ap(xF)[:, :, 0:TB], "V0")
        block.sync(sy0)
        di_ready0 = B.cnt["V0"]

        for tb in range(NB):
            ff_xr, ff_xk = xrs[tb % 2], xks[tb % 2]
            ff_x = fxs[tb % 2]
            pss = pss_sets[tb % 2]
            if tb == 0:
                di_all = di_ready0
            vs_mix = vsm[tb]
            ts0 = B.cnt["SS"]
            as0 = B.cnt["AS"]
            vs0 = B.cnt["VS"]

            vadd = 1 if tb + 1 < NB else 0

            def tproj(t, tb=tb, ts0=ts0, as0=as0, vs0=vs0, vs_mix=vs_mix,
                      ff_xr=ff_xr, ff_xk=ff_xk, pss=pss, vadd=vadd):
                if tb == 0:
                    B.entry(t, ent_f)
                    B.w(t, "DG", dg_w)
                B.w(t, "VS", vs_mix)
                n = 0
                for w_sb, src in ((fw_r, ff_xr), (fw_k, ff_xk)):
                    for oc in range(CH):
                        ps = pss[n % 2]
                        if n >= 2:
                            if n - 2 < CH:
                                B.w(t, "AS", as0 + (n - 2) + 1)
                            else:
                                B.w(t, "VS", vs0 + vadd + (n - 2 - CH) + 1)
                        for cc in range(CH):
                            i = t.matmul(ps[:, :],
                                         w_sb[:, cc * D + oc * 128:
                                              cc * D + (oc + 1) * 128],
                                         src[:, cc * TB:(cc + 1) * TB],
                                         start=(cc == 0),
                                         stop=(cc == CH - 1))
                        B.inc(i, "SS")
                        n += 1
            block.tensor(tproj)
            ssp[tb] = B.cnt["SS"]

            def sact(s, ts0=ts0, tb=tb, pss=pss, pv3=prev_vs3):
                if tb == 0:
                    B.entry(s, ent_f)
                for oc in range(CH):
                    if pv3:
                        B.w(s, "VS", pv3 + oc + 1)
                    B.w(s, "SS", ts0 + oc + 1)
                    i = s.activation(ff_r[:, oc * TB:(oc + 1) * TB],
                                     pss[oc % 2][:, :], AF.Sigmoid)
                    B.inc(i, "AS")
            block.scalar(sact)

            if tb + 1 < NB:
                emit_vmix(tb + 1, False)
            vs_emis2 = B.cnt["VS"]

            def vact(v, ts0=ts0, vs_emis2=vs_emis2, pss=pss):
                B.w(v, "VS", vs_emis2)
                for oc in range(CH):
                    n = CH + oc
                    if oc:
                        B.w(v, "VS", vs_emis2 + oc)
                    B.w(v, "SS", ts0 + n + 1)
                    B.chain(v, v.tensor_scalar_max(ff_t2[:, :],
                                                   pss[n % 2][:, :], 0.0),
                            "CV")
                    i = v.tensor_mul(ff_ka[:, oc * TB:(oc + 1) * TB],
                                     ff_t2[:, :], ff_t2[:, :])
                    B.inc(i, "VS")
            block.vector(vact)
            as_r = B.cnt["AS"]
            vs_k = B.cnt["VS"]

            ts1 = B.cnt["SS"]
            vs1 = B.cnt["VS"]

            def tkv(t, ts1=ts1, vs1=vs1, pss=pss):
                B.w(t, "VS", vs_k)
                for oc in range(CH):
                    ps = pss[oc % 2]
                    if oc >= 2:
                        B.w(t, "VS", vs1 + (oc - 2) + 1)
                    for cc in range(CH):
                        i = t.matmul(ps[:, :],
                                     fw_v[:, cc * D + oc * 128:
                                          cc * D + (oc + 1) * 128],
                                     ff_ka[:, cc * TB:(cc + 1) * TB],
                                     start=(cc == 0), stop=(cc == CH - 1))
                    B.inc(i, "SS")
            block.tensor(tkv)

            vs_emis3 = B.cnt["VS"]

            def vkv(v, ts1=ts1, di_all=di_all, vs_emis3=vs_emis3,
                    pss=pss, ff_x=ff_x, tb=tb):
                B.w(v, "VS", vs_emis3)
                B.w(v, "AS", as_r)
                B.w(v, f"V{tb % 2}", di_all)
                for oc in range(CH):
                    if oc:
                        B.w(v, "VS", vs_emis3 + oc)
                    B.w(v, "SS", ts1 + oc + 1)
                    B.chain(v, v.tensor_mul(ff_t2[:, :], pss[oc % 2][:, :],
                                            ff_r[:, oc * TB:(oc + 1) * TB]),
                            "CV")
                    i = v.tensor_add(ff_x[:, oc * TB:(oc + 1) * TB],
                                     ff_x[:, oc * TB:(oc + 1) * TB],
                                     ff_t2[:, :])
                    B.inc(i, "VS")
            block.vector(vkv)
            prev_vs3 = vs_emis3

            def syio(sy, tb=tb, vs_emis3=vs_emis3, ff_x=ff_x,
                     dpar=(B.cnt["D0"], B.cnt["D1"])):
                if tb + 1 < NB:
                    B.w(sy, "D0", dpar[0])
                    B.w(sy, "D1", dpar[1])
                    B.dma(sy, fxs[(tb + 1) % 2][:, :].rearrange(
                              "p (c t) -> p c t", t=TB),
                          xf_ap(xF)[:, :, (tb + 1) * TB:(tb + 2) * TB],
                          f"V{(tb + 1) % 2}")
                for c in range(CH):
                    B.w(sy, "VS", vs_emis3 + c + 1)
                    B.dma(sy, xf_ap(xF)[:, c:c + 1,
                                        tb * TB:(tb + 1) * TB],
                          ff_x[:, c * TB:(c + 1) * TB].rearrange(
                              "p (o t) -> p o t", o=1), f"D{c % 2}")
            block.sync(syio)
            if tb + 1 < NB:
                di_all = B.cnt[f"V{(tb + 1) % 2}"]


def emit_output(B, block, xF, ident, y_d, cst, dyn=False):
    nc = B.nc
    ntt = SOUT // 128
    with (
        nc.sbuf_tensor(B.nm("yo0"), [128, D], F32) as yo0,
        nc.sbuf_tensor(B.nm("yo1"), [128, D], F32) as yo1,
        nc.sbuf_tensor(B.nm("yt"), [128, D], F32) as yt,
        nc.psum_tensor(B.nm("ps_y"), [128, D], F32) as ps_y,
    ):
        yos = (yo0, yo1)
        ent_o = dict(B.cnt)
        dp = (B.cnt["D0"], B.cnt["D1"])
        ts0, as0, do0 = B.cnt["SS"], B.cnt["AS"], B.cnt["DO"]

        def syio(sy):
            B.entry(sy, ent_o)
            if dyn:
                sv = sy.value_load(cst["ridx"][0:1, 0:1])
            for tt in range(ntt):
                B.w(sy, "SS", ts0 + max(0, tt - 1))
                if dyn:
                    srca = AP(xF, sv + tt * 128,
                              [[S, 128], [128 * S, CH], [1, 128]])
                else:
                    srca = xf_ap(xF)[:, :, tt * 128:(tt + 1) * 128]
                B.dma(sy, yos[tt % 2][:, :].rearrange("p (c j) -> p c j",
                                                      j=128),
                      srca, f"D{tt % 2}")
                if tt >= 1:
                    B.w(sy, "AS", as0 + tt)
                    B.dma(sy, y_d[(tt - 1) * 128:tt * 128, :], yt[:, :],
                          "DO")
            B.w(sy, "AS", as0 + ntt)
            B.dma(sy, y_d[(ntt - 1) * 128:ntt * 128, :], yt[:, :], "DO")
            B.w(sy, "DO", B.cnt["DO"])
        block.sync(syio)

        def tb_(t):
            B.entry(t, ent_o)
            for tt in range(ntt):
                B.w(t, f"D{tt % 2}", dp[tt % 2] + 16 * (tt // 2 + 1))
                B.w(t, "AS", as0 + tt)
                for c in range(CH):
                    i = t.transpose(ps_y[:, c * 128:(c + 1) * 128],
                                    yos[tt % 2][:, c * 128:(c + 1) * 128],
                                    ident[:, :])
                B.inc(i, "SS")
        block.tensor(tb_)

        def sb(s):
            B.entry(s, ent_o)
            for tt in range(ntt):
                B.w(s, "SS", ts0 + tt + 1)
                B.w(s, "DO", do0 + 16 * tt)
                i = s.copy(yt[:, :], ps_y[:, :])
                B.inc(i, "AS")
        block.scalar(sb)


_prog_cache = {}


def _get_prog():
    if "nc" not in _prog_cache:
        _prog_cache["nc"] = build_program()
    return _prog_cache["nc"]


def kernel(L0_attn_a_x, L0_attn_a_W_Q, L0_attn_a_W_K, L0_attn_a_W_V,
           L0_attn_a_W_o, L0_attn_b_w_in, L0_attn_b_kernel_offsets,
           L0_attn_b_kernel_amplitudes, L0_attn_b_kernel_sharpness,
           L0_ffn_W_r, L0_ffn_W_k, L0_ffn_W_v, L0_ffn_mu_r, L0_ffn_mu_k):
    def f32(a):
        return np.ascontiguousarray(np.asarray(a, dtype=np.float32))

    x = f32(L0_attn_a_x)
    wq, wk, wv, wo = (f32(L0_attn_a_W_Q), f32(L0_attn_a_W_K),
                      f32(L0_attn_a_W_V), f32(L0_attn_a_W_o))
    winT = np.ascontiguousarray(f32(L0_attn_b_w_in).T)
    wrT = np.ascontiguousarray(f32(L0_ffn_W_r).T)
    wk2T = np.ascontiguousarray(f32(L0_ffn_W_k).T)
    wv2T = np.ascontiguousarray(f32(L0_ffn_W_v).T)
    mur = np.ascontiguousarray(f32(L0_ffn_mu_r).reshape(CH, 128).T)
    muk = np.ascontiguousarray(f32(L0_ffn_mu_k).reshape(CH, 128).T)
    off = f32(L0_attn_b_kernel_offsets)
    amp = f32(L0_attn_b_kernel_amplitudes)
    shp = f32(L0_attn_b_kernel_sharpness)
    toff = np.zeros((128, 2), np.float32)
    tshp = np.zeros((128, 2), np.float32)
    tamp = np.zeros((256, 16), np.float32)
    for h in range(H):
        for kap in range(8):
            for half in range(2):
                toff[h * 8 + kap, half] = off[h, kap + 8 * half]
                tshp[h * 8 + kap, half] = shp[h, kap + 8 * half]
                tamp[half * 128 + h * 8 + kap, h] = amp[h, kap + 8 * half]

    nc = _get_prog()
    in_maps = []
    for cid in range(8):
        seq = cid // 2
        role = cid % 2
        delta = SOUT * role
        rolec = np.array([[float(delta), float(role), 0.0, 0.0]], np.float32)
        ridx = np.array([[delta, 3968 if role == 0 else 1920, 0, 0]],
                        np.uint32)
        in_maps.append({
            "xin": x[seq], "wq": wq, "wk": wk, "wv": wv, "wo": wo,
            "winT": winT, "wrT": wrT, "wk2T": wk2T, "wv2T": wv2T,
            "mur": mur, "muk": muk, "toff": toff, "tshp": tshp,
            "tampbd": tamp, "rolec": rolec, "ridx": ridx,
        })
    res = run_bass_kernel_spmd(nc, in_maps, core_ids=list(range(8)))
    out = np.empty((4, S, D), np.float32)
    for cid in range(8):
        seq, role = cid // 2, cid % 2
        out[seq, SOUT * role:SOUT * (role + 1)] = np.asarray(
            res.results[cid]["y"], dtype=np.float32)
    return out


# version 18
# speedup vs baseline: 1.2090x; 1.2090x over previous
import numpy as np
import concourse.bass as bass
import concourse.mybir as mybir
from concourse.ap import AP
from concourse.bass_utils import run_bass_kernel_spmd

F32 = mybir.dt.float32
BF16 = mybir.dt.bfloat16
AF = mybir.ActivationFunctionType
ALU = mybir.AluOpType

S = 4096
D = 1024
CH = 8
TB = 512
NTB = S // TB
H = 16
HP = 8
DH = 64
KC = S // 128
VTW = H * 65
SW = 2 * S
UW = 8064
SOUT = 2048
NQB = SOUT // TB
UOFF = 2048
UWB = 6018
EPS = 1e-5
SCALE = 0.125
import os as _os
NSUB = int(_os.environ.get("KNSUB", "4"))


class Bld:
    def __init__(self, nc):
        self.nc = nc
        self.sem = {}
        self.cnt = {}
        for name in ("DI", "DO", "DG", "SS", "PS", "AS", "VS", "GS",
                     "CA", "CV", "CG", "D0", "D1", "L0", "L1", "U0", "U1",
                     "V0", "V1"):
            self.sem[name] = nc.alloc_semaphore(name)
            self.cnt[name] = 0

    def inc(self, instr, s, n=1):
        self.cnt[s] += n
        instr.then_inc(self.sem[s], n)
        return self.cnt[s]

    def w(self, eng, s, val):
        if val > 0:
            eng.wait_ge(self.sem[s], val)

    ENTRY = ("SS", "PS", "AS", "VS", "DO", "DG", "D0", "D1", "DI",
             "L0", "L1", "U0", "U1", "V0", "V1")

    def entry(self, eng, ent):
        for s in self.ENTRY:
            self.w(eng, s, ent[s])

    def guard(self, eng, *sems):
        for s in sems:
            self.w(eng, s, self.cnt[s])

    def wi(self, instr, s, val):
        if val > 0:
            instr._wait_ge(self.sem[s], val)

    def dma(self, eng, out, in_, s):
        i = eng.dma_start(out=out, in_=in_)
        return self.inc(i, s, 16)

    def chain(self, eng, instr, s):
        v = self.inc(instr, s, 1)
        eng.wait_ge(self.sem[s], v)

    _uid = 0

    def nm(self, base):
        Bld._uid += 1
        return f"{base}_{Bld._uid}" 


def xf_ap(xf):
    return xf.ap().rearrange("(c p) s -> p c s", p=128)


def build_program():
    nc = bass.Bass("TRN2", target_bir_lowering=False)
    B = Bld(nc)

    xin = nc.dram_tensor("xin", [S, D], F32, kind="ExternalInput")
    wq_d = nc.dram_tensor("wq", [D, D], F32, kind="ExternalInput")
    wk_d = nc.dram_tensor("wk", [D, D], F32, kind="ExternalInput")
    wv_d = nc.dram_tensor("wv", [D, D], F32, kind="ExternalInput")
    wo_d = nc.dram_tensor("wo", [D, D], F32, kind="ExternalInput")
    winT_d = nc.dram_tensor("winT", [D, 3 * D], F32, kind="ExternalInput")
    wrT_d = nc.dram_tensor("wrT", [D, D], F32, kind="ExternalInput")
    wk2T_d = nc.dram_tensor("wk2T", [D, D], F32, kind="ExternalInput")
    wv2T_d = nc.dram_tensor("wv2T", [D, D], F32, kind="ExternalInput")
    mur_d = nc.dram_tensor("mur", [128, CH], F32, kind="ExternalInput")
    muk_d = nc.dram_tensor("muk", [128, CH], F32, kind="ExternalInput")
    toff_d = nc.dram_tensor("toff", [128, 2], F32, kind="ExternalInput")
    tshp_d = nc.dram_tensor("tshp", [128, 2], F32, kind="ExternalInput")
    tamp_d = nc.dram_tensor("tampbd", [256, 16], F32, kind="ExternalInput")
    rolec_d = nc.dram_tensor("rolec", [1, 4], F32, kind="ExternalInput")
    ridx_d = nc.dram_tensor("ridx", [1, 4], mybir.dt.uint32,
                            kind="ExternalInput")
    y_d = nc.dram_tensor("y", [SOUT, D], F32, kind="ExternalOutput")

    xF = nc.dram_tensor("xF", [CH * 128, S], F32)
    xF2 = nc.dram_tensor("xF2", [CH * 128, SOUT], F32)
    xnD = nc.dram_tensor("xnD", [CH * 128, S], BF16)
    zD = nc.dram_tensor("zD", [1, 16], F32)
    oaD = nc.dram_tensor("oaD", [CH * 128, S], BF16)
    vtD = nc.dram_tensor("vtD", [KC * 128, VTW], BF16)
    sexpD = nc.dram_tensor("sexpD", [H, SW], F32)

    ones128 = nc.alloc_sbuf_tensor("ones128", [128, 1], F32)
    ones1 = nc.alloc_sbuf_tensor("ones1", [1, 128], F32)
    ident = nc.alloc_sbuf_tensor("ident", [128, 128], F32)
    masks = nc.alloc_sbuf_tensor("masks", [128, 4 * TB], BF16)
    mur_sb = nc.alloc_sbuf_tensor("mur_sb", [128, CH], F32)
    muk_sb = nc.alloc_sbuf_tensor("muk_sb", [128, CH], F32)
    mur1_sb = nc.alloc_sbuf_tensor("mur1_sb", [128, CH], F32)
    muk1_sb = nc.alloc_sbuf_tensor("muk1_sb", [128, CH], F32)
    deltaB = nc.alloc_sbuf_tensor("deltaB", [128, 1], F32)
    hmB = nc.alloc_sbuf_tensor("hmB", [128, 1], F32)
    ridx_sb = nc.alloc_sbuf_tensor("ridx_sb", [1, 4], mybir.dt.uint32)
    xnb = nc.alloc_sbuf_tensor("xnb", [128, CH * (S + 1)], BF16)

    cst = dict(ones128=ones128, ones1=ones1, ident=ident, masks=masks,
               mur=mur_sb, muk=muk_sb, mur1=mur1_sb, muk1=muk1_sb,
               xnb=xnb, deltaB=deltaB, hmB=hmB, ridx=ridx_sb)

    with nc.Block() as block:
        emit_setup(B, block, xin, mur_d, muk_d, toff_d, tshp_d, tamp_d,
                   rolec_d, ridx_d, xF, sexpD, cst)
        for sub in range(NSUB):
            xf_src = xF2 if sub == 3 else xF
            ntok = SOUT if sub == 3 else S
            emit_statsxn(B, block, xf_src, cst, ntok,
                         mirror=(xnD if sub == 2 else None))
            if sub % 2 == 0:
                causal = (sub == 0)
                emit_attn(B, block, xF, xF2, oaD, vtD, sexpD, cst,
                          wq_d, wk_d, wv_d, wo_d, winT_d, causal, zD, xnD)
            else:
                emit_ffn(B, block, xf_src, cst, wrT_d, wk2T_d, wv2T_d, ntok)
        emit_output(B, block, xF2 if NSUB >= 3 else xF, ident, y_d, cst,
                    dyn=(NSUB < 3))
    return nc


def emit_setup(B, block, xin, mur_d, muk_d, toff_d, tshp_d, tamp_d,
               rolec_d, ridx_d, xF, sexpD, cst):
    nc = B.nc
    with (
        nc.sbuf_tensor(B.nm("toff_sb"), [128, 2], F32) as toff_sb,
        nc.sbuf_tensor(B.nm("tshp_sb"), [128, 2], F32) as tshp_sb,
        nc.sbuf_tensor(B.nm("tnsh_sb"), [128, 2], F32) as tnsh_sb,
        nc.sbuf_tensor(B.nm("tamp_sb"), [128, 2 * 16], F32) as tamp_sb,
        nc.sbuf_tensor(B.nm("iotb"), [128, 2048], F32) as iotb,
        nc.sbuf_tensor(B.nm("tk0b"), [128, 2048], F32) as tk0b,
        nc.sbuf_tensor(B.nm("tk1b"), [128, 2048], F32) as tk1b,
        nc.sbuf_tensor(B.nm("sexpb"), [16, 2 * TB], F32) as sexpb,
        nc.psum_tensor(B.nm("ps_s"), [16, 2 * TB], F32) as ps_s,
    ):
        def g_body(g):
            B.dma(g, cst["mur"][:, :], mur_d[:, :], "DG")
            B.dma(g, cst["muk"][:, :], muk_d[:, :], "DG")
            B.dma(g, toff_sb[:, :], toff_d[:, :], "DG")
            B.dma(g, tshp_sb[:, :], tshp_d[:, :], "DG")
            B.dma(g, tamp_sb[:, :].rearrange("p (u k) -> p u k", k=16),
                  tamp_d.ap().rearrange("(u p) k -> p u k", p=128), "DG")
            B.dma(g, cst["deltaB"][:, :],
                  AP(rolec_d, 0, [[0, 128], [1, 1]]), "DG")
            B.dma(g, cst["hmB"][:, :],
                  AP(rolec_d, 1, [[0, 128], [1, 1]]), "DG")
            B.dma(g, cst["ridx"][:, :], ridx_d[:, :], "DG")
            g.memset(cst["ones128"][:, :], 1.0)
            g.memset(cst["ones1"][:, :], 1.0)
            B.chain(g, g.memset(cst["ident"][:, :], 0.0), "CG")
            g.affine_select(out=cst["ident"][:, :], in_=cst["ident"][:, :],
                            pattern=[[-1, 128]], compare_op=ALU.not_equal,
                            fill=1.0, base=0, channel_multiplier=1)
            B.chain(g, g.memset(cst["masks"][:, :], 1.0), "CG")
            for r in range(4):
                g.affine_select(out=cst["masks"][:, r * TB:(r + 1) * TB],
                                in_=cst["masks"][:, r * TB:(r + 1) * TB],
                                pattern=[[1, TB]], compare_op=ALU.is_ge,
                                fill=0.0, base=-128 * r,
                                channel_multiplier=-1)
            i = g.memset(cst["xnb"][:, 0:CH * (S + 1):(S + 1)], 0.0)
            B.inc(i, "GS")
        block.gpsimd(g_body)
        gs_ready = B.cnt["GS"]
        dg_ready = B.cnt["DG"]

        def v_mu1(v):
            B.w(v, "DG", dg_ready)
            i = v.tensor_scalar(cst["mur1"][:, :], cst["mur"][:, :],
                                -1.0, 1.0, ALU.mult, ALU.add)
            B.inc(i, "VS")
            i = v.tensor_scalar(cst["muk1"][:, :], cst["muk"][:, :],
                                -1.0, 1.0, ALU.mult, ALU.add)
            B.inc(i, "VS")
        block.vector(v_mu1)

        def s_abs(s):
            B.w(s, "DG", dg_ready)
            i = s.activation(tnsh_sb[:, :], tshp_sb[:, :], AF.Abs)
            B.inc(i, "AS")
        block.scalar(s_abs)
        as_abs = B.cnt["AS"]

        def v_neg(v):
            B.w(v, "AS", as_abs)
            i = v.tensor_scalar_mul(tnsh_sb[:, :], tnsh_sb[:, :], -1.0)
            B.inc(i, "VS")
        block.vector(v_neg)
        vs_neg = B.cnt["VS"]

        for blk in range(SW // 2048):
            i0 = blk * 2048

            def g_io(g, i0=i0):
                B.guard(g, "VS")
                i = g.iota(iotb[:, :], pattern=[[-1, 2048]],
                           base=S - 1 - i0, channel_multiplier=0,
                           allow_small_or_imprecise_dtypes=True)
                B.inc(i, "GS")
            block.gpsimd(g_io)
            gs_io = B.cnt["GS"]

            def v_tk(v, gs_io=gs_io):
                B.w(v, "GS", gs_io)
                B.w(v, "VS", vs_neg)
                B.guard(v, "AS")
                B.chain(v, v.tensor_scalar(iotb[:, :], iotb[:, :],
                                           cst["deltaB"][:, 0:1], None,
                                           ALU.add), "CV")
                for tk, half in ((tk0b, 0), (tk1b, 1)):
                    B.chain(v, v.tensor_scalar(tk[:, :], iotb[:, :],
                                               toff_sb[:, half:half + 1],
                                               None, ALU.subtract), "CV")
                    i = v.scalar_tensor_tensor(tk[:, :], tk[:, :],
                                               tnsh_sb[:, half:half + 1],
                                               tk[:, :], ALU.mult, ALU.mult)
                    B.inc(i, "VS")
            block.vector(v_tk)
            vs_tk = B.cnt["VS"]

            def s_ek(s, vs_tk=vs_tk):
                B.w(s, "VS", vs_tk)
                B.guard(s, "SS")
                s.activation(tk0b[:, :], tk0b[:, :], AF.Exp)
                i = s.activation(tk1b[:, :], tk1b[:, :], AF.Exp)
                B.inc(i, "AS")
            block.scalar(s_ek)
            as_ek = B.cnt["AS"]

            ss_mm0 = B.cnt["SS"]
            as_se0 = B.cnt["AS"]

            def t_amp(t, as_ek=as_ek, ss_mm0=ss_mm0, as_se0=as_se0):
                B.w(t, "AS", as_ek)
                for ch in range(4):
                    c0 = ch * TB
                    bank = ch % 2
                    B.w(t, "AS", as_se0 + max(0, ch - 1))
                    t.matmul(ps_s[0:16, bank * TB:(bank + 1) * TB],
                             tamp_sb[:, 0:16], tk0b[:, c0:c0 + TB],
                             start=True, stop=False)
                    i = t.matmul(ps_s[0:16, bank * TB:(bank + 1) * TB],
                                 tamp_sb[:, 16:32], tk1b[:, c0:c0 + TB],
                                 start=False, stop=True)
                    B.inc(i, "SS")
            block.tensor(t_amp)

            do_se0 = B.cnt["DO"]

            def s_se(s, ss_mm0=ss_mm0, do_se0=do_se0):
                for ch in range(4):
                    bank = ch % 2
                    B.w(s, "SS", ss_mm0 + ch + 1)
                    B.w(s, "DO", do_se0 + 16 * max(0, ch - 1))
                    i = s.activation(sexpb[0:16, bank * TB:(bank + 1) * TB],
                                     ps_s[0:16, bank * TB:(bank + 1) * TB],
                                     AF.Exp)
                    B.inc(i, "AS")
            block.scalar(s_se)
            as_se1 = B.cnt["AS"]

            do_se1 = B.cnt["DO"]

            def sy_se(sy, i0=i0, as_se0=as_se0, do_se1=do_se1):
                for ch in range(4):
                    bank = ch % 2
                    B.w(sy, "AS", as_se0 + ch + 1)
                    B.w(sy, "DO", do_se1 + 16 * ch)
                    B.dma(sy, sexpD.ap()[:, i0 + ch * TB:i0 + (ch + 1) * TB],
                          sexpb[0:16, bank * TB:(bank + 1) * TB], "DO")
            block.sync(sy_se)

    if True:
        do_sexp = B.cnt["DO"]
        with (
            nc.sbuf_tensor(B.nm("xt0"), [128, D], F32) as xt0,
            nc.sbuf_tensor(B.nm("xt1"), [128, D], F32) as xt1,
            nc.sbuf_tensor(B.nm("xt2"), [128, D], F32) as xt2,
            nc.psum_tensor(B.nm("ps_t"), [128, D], F32) as ps_t,
        ):
            xts = (xt0, xt1)
            ent_t = dict(B.cnt)
            dp = (B.cnt["D0"], B.cnt["D1"])
            ts0, as1, do0 = B.cnt["SS"], B.cnt["AS"], B.cnt["DO"]

            def syio(sy):
                B.entry(sy, ent_t)
                B.w(sy, "GS", ent_t["GS"])
                B.w(sy, "DO", do_sexp)
                for tt in range(KC):
                    B.w(sy, "SS", ts0 + max(0, tt - 1))
                    B.dma(sy, xts[tt % 2][:, :],
                          xin[tt * 128:(tt + 1) * 128, :], f"D{tt % 2}")
                    if tt >= 1:
                        B.w(sy, "AS", as1 + tt)
                        dst = xf_ap(xF)[:, :, (tt - 1) * 128:tt * 128]
                        B.dma(sy, dst,
                              xt2[:, :].rearrange("p (c j) -> p c j", j=128),
                              "DO")
                B.w(sy, "AS", as1 + KC)
                dst = xf_ap(xF)[:, :, (KC - 1) * 128:KC * 128]
                B.dma(sy, dst,
                      xt2[:, :].rearrange("p (c j) -> p c j", j=128), "DO")
            block.sync(syio)

            def tb_(t):
                B.entry(t, ent_t)
                B.w(t, "GS", gs_ready)
                for tt in range(KC):
                    B.w(t, f"D{tt % 2}", dp[tt % 2] + 16 * (tt // 2 + 1))
                    B.w(t, "AS", as1 + tt)
                    for c in range(CH):
                        i = t.transpose(ps_t[:, c * 128:(c + 1) * 128],
                                        xts[tt % 2][:, c * 128:(c + 1) * 128],
                                        cst["ident"][:, :])
                    B.inc(i, "SS")
            block.tensor(tb_)

            def sb_(s):
                B.entry(s, ent_t)
                for tt in range(KC):
                    B.w(s, "SS", ts0 + tt + 1)
                    B.w(s, "DO", do0 + 16 * tt)
                    i = s.copy(xt2[:, :], ps_t[:, :])
                    B.inc(i, "AS")
            block.scalar(sb_)




def emit_statsxn(B, block, xF, cst, ntok=S, mirror=None):
    NB = ntok // TB
    nc = B.nc
    with (
        nc.sbuf_tensor(B.nm("sx_x0"), [128, CH * TB], F32) as sx_x0,
        nc.sbuf_tensor(B.nm("sx_x1"), [128, CH * TB], F32) as sx_x1,
        nc.sbuf_tensor(B.nm("sx_q0"), [128, CH * TB], F32) as sx_q0,
        nc.sbuf_tensor(B.nm("sx_q1"), [128, CH * TB], F32) as sx_q1,
        nc.sbuf_tensor(B.nm("sx_m"), [1, 2 * TB], F32) as sx_m,
        nc.sbuf_tensor(B.nm("sx_s"), [1, 2 * TB], F32) as sx_s,
        nc.sbuf_tensor(B.nm("sx_t"), [1, 2 * TB], F32) as sx_t,
        nc.psum_tensor(B.nm("ps_m0"), [1, 2 * TB], F32) as ps_m0,
        nc.psum_tensor(B.nm("ps_m1"), [1, 2 * TB], F32) as ps_m1,
        nc.psum_tensor(B.nm("ps_b0"), [128, 2 * TB], F32) as ps_b0,
        nc.psum_tensor(B.nm("ps_b1"), [128, 2 * TB], F32) as ps_b1,
    ):
        xbs = (sx_x0, sx_x1)
        sqs = (sx_q0, sx_q1)
        pms = (ps_m0, ps_m1)
        pbs = (ps_b0, ps_b1)
        ent = dict(B.cnt)
        dp = (B.cnt["D0"], B.cnt["D1"])
        vs0 = B.cnt["VS"]
        ss0 = B.cnt["SS"]
        as0 = B.cnt["AS"]
        NVT = 2 + CH

        def vs_tb(tb):
            return vs0 + NVT * (tb + 1)

        def syi(sy):
            B.entry(sy, ent)
            for tb in range(NB):
                if tb >= 2:
                    B.w(sy, "VS", vs_tb(tb - 2))
                B.dma(sy, xbs[tb % 2][:, :].rearrange(
                    "p (c t) -> p c t", t=TB),
                    xf_ap(xF)[:, :, tb * TB:(tb + 1) * TB], f"D{tb % 2}")
        block.sync(syi)

        def vb(v):
            B.entry(v, ent)
            for tb in range(NB):
                B.w(v, f"D{tb % 2}", dp[tb % 2] + 16 * (tb // 2 + 1))
                if tb >= 2:
                    B.w(v, "SS", ss0 + 3 * (tb - 2) + 2)
                i = v.tensor_mul(sqs[tb % 2][:, :], xbs[tb % 2][:, :],
                                 xbs[tb % 2][:, :])
                B.inc(i, "VS")
                B.w(v, "SS", ss0 + 3 * tb + 2)
                pm = pms[tb % 2]
                B.chain(v, v.tensor_scalar_mul(sx_m[0:1, :], pm[0:1, :],
                                               1.0 / D), "CV")
                B.chain(v, v.tensor_mul(sx_t[0:1, 0:TB], sx_m[0:1, 0:TB],
                                        sx_m[0:1, 0:TB]), "CV")
                B.chain(v, v.tensor_sub(sx_m[0:1, TB:2 * TB],
                                        sx_m[0:1, TB:2 * TB],
                                        sx_t[0:1, 0:TB]), "CV")
                i = v.tensor_scalar_add(sx_m[0:1, TB:2 * TB],
                                        sx_m[0:1, TB:2 * TB], EPS)
                B.inc(i, "VS")
                B.w(v, "SS", ss0 + 3 * tb + 3)
                pb = pbs[tb % 2]
                tmp = sqs[tb % 2]
                for c in range(CH):
                    i = v.tensor_sub(tmp[:, c * TB:(c + 1) * TB],
                                     xbs[tb % 2][:, c * TB:(c + 1) * TB],
                                     pb[:, 0:TB])
                B.chain(v, i, "CV")
                for c in range(CH):
                    base = c * (S + 1) + 1 + tb * TB
                    i = v.tensor_mul(cst["xnb"][:, base:base + TB],
                                     tmp[:, c * TB:(c + 1) * TB],
                                     pb[:, TB:2 * TB])
                    B.inc(i, "VS")
        block.vector(vb)

        def tb_(t):
            B.entry(t, ent)
            for tb in range(NB):
                pm = pms[tb % 2]
                B.w(t, f"D{tb % 2}", dp[tb % 2] + 16 * (tb // 2 + 1))
                if tb >= 2:
                    B.w(t, "VS", vs0 + NVT * (tb - 2) + 2)
                for c in range(CH):
                    i = t.matmul(pm[0:1, 0:TB], cst["ones128"][:, :],
                                 xbs[tb % 2][:, c * TB:(c + 1) * TB],
                                 start=(c == 0), stop=(c == CH - 1))
                B.inc(i, "SS")
                B.w(t, "VS", vs0 + NVT * tb + 1)

                for c in range(CH):
                    i = t.matmul(pm[0:1, TB:2 * TB], cst["ones128"][:, :],
                                 sqs[tb % 2][:, c * TB:(c + 1) * TB],
                                 start=(c == 0), stop=(c == CH - 1))
                B.inc(i, "SS")
                B.w(t, "AS", as0 + tb + 1)
                if tb >= 2:
                    B.w(t, "VS", vs_tb(tb - 2))
                pb = pbs[tb % 2]
                t.matmul(pb[:, 0:TB], cst["ones1"][0:1, :],
                         sx_m[0:1, 0:TB], start=True, stop=True)
                i = t.matmul(pb[:, TB:2 * TB], cst["ones1"][0:1, :],
                             sx_s[0:1, 0:TB], start=True, stop=True)
                B.inc(i, "SS")
        block.tensor(tb_)

        def sb(s):
            B.entry(s, ent)
            for tb in range(NB):
                B.w(s, "VS", vs0 + NVT * tb + 2)
                B.chain(s, s.activation(sx_t[0:1, TB:2 * TB],
                                        sx_m[0:1, TB:2 * TB], AF.Ln), "CA")
                i = s.activation(sx_s[0:1, 0:TB], sx_t[0:1, TB:2 * TB],
                                 AF.Exp, scale=-0.5)
                B.inc(i, "AS")
        block.scalar(sb)

        if mirror is not None:
            def sym(sy):
                for tb in range(NB):
                    B.w(sy, "VS", vs_tb(tb))
                    src_ap = AP(cst["xnb"].tensor
                                if hasattr(cst["xnb"], "tensor")
                                else cst["xnb"], 1 + tb * TB,
                                [[CH * (S + 1), 128], [S + 1, CH], [1, TB]])
                    B.dma(sy, xf_ap(mirror)[:, :, tb * TB:(tb + 1) * TB],
                          src_ap, "DO")
            block.sync(sym)


def xn_sl(cst, c, t0, tl):
    return cst["xnb"][:, c * (S + 1) + 1 + t0: c * (S + 1) + 1 + t0 + tl]


def emit_attn(B, block, xF, xF2, oaD, vtD, sexpD, cst,
              wq_d, wk_d, wv_d, wo_d, winT_d, causal, zD, xnD):
    nc = B.nc
    from contextlib import ExitStack
    with ExitStack() as actx:
        if not causal:
            ae = actx.enter_context
            cst = dict(cst)
            cst["xn2own"] = ae(nc.sbuf_tensor(B.nm("xn2own"),
                                              [128, CH * SOUT], BF16))
            cst["xnqh"] = ae(nc.sbuf_tensor(B.nm("xnqh"), [128, CH * 128],
                                            BF16))
            cst["x2hp"] = ae(nc.sbuf_tensor(B.nm("x2hp"), [128, CH * 128],
                                            F32))
            cst["ohalo"] = ae(nc.sbuf_tensor(B.nm("ohalo"), [128, CH], F32))
            cst["zhst"] = ae(nc.sbuf_tensor(B.nm("zhst"), [65, 16], F32))
            cst["hstage"] = ae(nc.sbuf_tensor(B.nm("hstage"), [65, 1], F32))
            cst["zrow"] = ae(nc.sbuf_tensor(B.nm("zrow"), [1, 16], F32))
            cst["zcol"] = ae(nc.sbuf_tensor(B.nm("zcol"), [128, CH], F32))
        _emit_attn_inner(B, block, xF, xF2, oaD, vtD, sexpD, cst,
                         wq_d, wk_d, wv_d, wo_d, winT_d, causal, zD, xnD)


def _emit_attn_inner(B, block, xF, xF2, oaD, vtD, sexpD, cst,
                     wq_d, wk_d, wv_d, wo_d, winT_d, causal, zD, xnD):
    nc = B.nc
    ent_a = dict(B.cnt)
    with (
        nc.sbuf_tensor(B.nm("wv_sb"), [128, CH * D], BF16) as wv_sb,
        nc.sbuf_tensor(B.nm("vt_st"), [128, VTW], BF16) as vt_st,
        nc.psum_tensor(B.nm("ps_v0"), [128, D], F32) as ps_v0,
        nc.psum_tensor(B.nm("ps_v1"), [128, D], F32) as ps_v1,
    ):
        def gld(g):
            B.entry(g, ent_a)
            src = wv_d.ap() if causal else winT_d.ap()[:, D:2 * D]
            B.dma(g, wv_sb[:, :].rearrange("p (cb j) -> p cb j", j=D),
                  src.rearrange("(cb p) j -> p cb j", p=128), "DG")
            i = g.memset(vt_st[:, 64:VTW:65], 1.0)
            B.inc(i, "GS")
        block.gpsimd(gld)
        dg_w = B.cnt["DG"]
        gs_st = B.cnt["GS"]
        xn_vs = B.cnt["VS"]

        psv = (ps_v0, ps_v1)
        ts0, vs0, do0 = B.cnt["SS"], B.cnt["VS"], B.cnt["DO"]

        def tb_(t):
            B.entry(t, ent_a)
            B.w(t, "DG", dg_w)
            B.w(t, "VS", xn_vs)
            for tt in range(KC):
                ps = psv[tt % 2]
                B.w(t, "VS", vs0 + max(0, tt - 1))
                for c in range(CH):
                    t.matmul(ps[:, 0:TB],
                             xn_sl(cst, c, tt * 128, 128),
                             wv_sb[:, c * D:c * D + TB],
                             start=(c == 0), stop=(c == CH - 1))
                    i = t.matmul(ps[:, TB:D],
                                 xn_sl(cst, c, tt * 128, 128),
                                 wv_sb[:, c * D + TB:(c + 1) * D],
                                 start=(c == 0), stop=(c == CH - 1))
                B.inc(i, "SS")
        block.tensor(tb_)

        def vb(v):
            B.entry(v, ent_a)
            B.w(v, "GS", gs_st)
            for tt in range(KC):
                ps = psv[tt % 2]
                B.w(v, "SS", ts0 + tt + 1)
                B.w(v, "DO", do0 + 16 * tt)
                for half in range(2):
                    src = ps[:, half * TB:(half + 1) * TB].rearrange(
                        "p (h d) -> p h d", d=64)
                    dst = AP(vt_st, half * 8 * 65,
                             [[VTW, 128], [65, 8], [1, 64]])
                    i = v.tensor_copy(dst, src)
                B.inc(i, "VS")
        block.vector(vb)

        def syo(sy):
            B.entry(sy, ent_a)
            for tt in range(KC):
                B.w(sy, "VS", vs0 + tt + 1)
                dst = vtD.ap().rearrange("(tt p) w -> p tt w",
                                         p=128)[:, tt:tt + 1, :]
                B.dma(sy, dst,
                      vt_st[:, :].rearrange("p (o w) -> p o w", o=1), "DO")
        block.sync(syo)

    if not causal:
        ent_q = dict(B.cnt)

        def syq(sy):
            B.entry(sy, ent_q)
            sv = sy.value_load(cst["ridx"][0:1, 0:1])
            svh = sy.value_load(cst["ridx"][0:1, 1:2])
            B.dma(sy, cst["xn2own"][:, :].rearrange("p (c t) -> p c t",
                                                    t=SOUT),
                  AP(xnD, sv, [[S, 128], [128 * S, CH], [1, SOUT]]), "DI")
            B.dma(sy, cst["xnqh"][:, :].rearrange("p (c t) -> p c t", t=128),
                  AP(xnD, svh, [[S, 128], [128 * S, CH], [1, 128]]), "DI")
            B.dma(sy, cst["x2hp"][:, :].rearrange("p (c t) -> p c t", t=128),
                  AP(xF, svh, [[S, 128], [128 * S, CH], [1, 128]]), "DI")
        block.sync(syq)

    uw = UWB if not causal else 2
    from contextlib import ExitStack as _ES
    with _ES() as lctx:
        le = lctx.enter_context
        u0 = le(nc.sbuf_tensor(B.nm("ldu0"), [128, uw], BF16))
        u1 = le(nc.sbuf_tensor(B.nm("ldu1"), [128, uw], BF16))
        ldsets = []
        for _b in range(2):
            ldsets.append({
                "wA": le(nc.sbuf_tensor(B.nm("ldwA"), [128, CH * 128],
                                        BF16)),
                "wB": le(nc.sbuf_tensor(B.nm("ldwB"), [128, CH * 128],
                                        BF16)),
                "vt": le(nc.sbuf_tensor(B.nm("ldvt"), [128, KC * 130],
                                        BF16)),
                "u0": u0,
                "u1": u1,
            })
        lc = {}

        def emit_hp_loads(hpx, ent):
            ld = ldsets[hpx % 2]
            sname = f"L{hpx % 2}"

            def gl(g, hpx=hpx, ld=ld, ent=ent, sname=sname):
                B.entry(g, ent)
                if causal:
                    sq, sk = wq_d.ap(), wk_d.ap()
                else:
                    sk = winT_d.ap()[:, 0:D]
                    sq = winT_d.ap()[:, 2 * D:3 * D]
                for w_sb, srcw in ((ld["wA"], sq), (ld["wB"], sk)):
                    B.dma(g, w_sb[:, :].rearrange("p (cb j) -> p cb j",
                                                  j=128),
                          srcw[:, hpx * 128:(hpx + 1) * 128].rearrange(
                              "(cb p) j -> p cb j", p=128), sname)
            block.gpsimd(gl)

            vname = f"V{hpx % 2}"

            def syl(sy, hpx=hpx, ld=ld, ent=ent, vname=vname):
                B.entry(sy, ent)
                srcv = vtD.ap().rearrange(
                    "(tt p) w -> p tt w",
                    p=128)[:, :, hpx * 130:(hpx + 1) * 130]
                B.dma(sy, ld["vt"][:, :].rearrange("p (tt w) -> p tt w",
                                                   w=130), srcv, vname)
            block.sync(syl)
            lc[hpx] = (B.cnt[sname], B.cnt[vname])

        def emit_u_loads(hpx):
            if causal:
                return
            uname = f"U{hpx % 2}"
            vs_tgt = B.cnt["VS"]

            def gu(g, hpx=hpx, uname=uname, vs_tgt=vs_tgt):
                B.w(g, "VS", vs_tgt)
                B.dma(g, u0[:, 1:UWB],
                      AP(sexpD, (2 * hpx) * SW + UOFF,
                         [[1, 128], [1, UWB - 1]]), uname)
                B.dma(g, u1[:, 1:UWB],
                      AP(sexpD, (2 * hpx + 1) * SW + UOFF,
                         [[1, 128], [1, UWB - 1]]), uname)
            block.gpsimd(gu)

        emit_hp_loads(0, dict(B.cnt))
        emit_u_loads(0)
        qw = S if causal else SOUT + 128
        khb = [le(nc.sbuf_tensor(B.nm("khb"), [128, S], BF16))
               for _ in range(2)]
        qhb = [le(nc.sbuf_tensor(B.nm("qhb"), [128, qw], BF16))
               for _ in range(2)]

        with (
            nc.psum_tensor(B.nm("pp0"), [128, TB], F32) as pp0,
            nc.psum_tensor(B.nm("pp1"), [128, TB], F32) as pp1,
        ):
            ent_p0 = dict(B.cnt)
            jobs0 = _attn_jobs(cst, causal, ldsets[0]["wA"], ldsets[0]["wB"],
                               khb[0], qhb[0])
            pts0, pas0 = B.cnt["SS"], B.cnt["AS"]
            pps = (pp0, pp1)

            def tp0(t):
                B.entry(t, ent_p0)
                B.w(t, "L0", lc[0][0])
                for n, (w_sb, dst, rhs_fn, width) in enumerate(jobs0):
                    ps = pps[n % 2]
                    B.w(t, "AS", pas0 + max(0, n - 1))
                    for cc in range(CH):
                        i = t.matmul(ps[:, 0:width],
                                     w_sb[:, cc * 128:(cc + 1) * 128],
                                     rhs_fn(cc), start=(cc == 0),
                                     stop=(cc == CH - 1))
                    B.inc(i, "SS")
            block.tensor(tp0)

            def sp0_(s):
                B.entry(s, ent_p0)
                for n, (w_sb, dst, rhs_fn, width) in enumerate(jobs0):
                    B.w(s, "SS", pts0 + n + 1)
                    i = s.copy(dst, pps[n % 2][:, 0:width])
                    B.inc(i, "AS")
            block.scalar(sp0_)

        for hp in range(HP):
            _attn_head_pair(B, block, oaD, vtD, sexpD, cst, hp, causal,
                            wq_d, wk_d, winT_d, ldsets[hp % 2], lc,
                            emit_hp_loads, emit_u_loads, khb, qhb, ldsets)

    if causal:
        _attn_wo_residual(B, block, xF, oaD, wo_d)
    else:
        _halo_finalize(B, block, cst, zD, xF2)
        _attn_split_residual(B, block, xF, xF2, oaD, cst)


def _halo_finalize(B, block, cst, zD, xF2):
    nc = B.nc
    ent = dict(B.cnt)
    with (
        nc.sbuf_tensor(B.nm("hx3"), [128, CH], F32) as hx3,
        nc.sbuf_tensor(B.nm("hsq"), [128, CH], F32) as hsq,
        nc.sbuf_tensor(B.nm("hfold"), [128, CH], F32) as hfold,
        nc.sbuf_tensor(B.nm("hrow"), [1, CH], F32) as hrow,
        nc.sbuf_tensor(B.nm("hsc"), [1, 2], F32) as hsc,
        nc.psum_tensor(B.nm("ps_h"), [1, 2 * TB], F32) as ps_h,
        nc.psum_tensor(B.nm("ps_hb"), [128, 2], F32) as ps_hb,
    ):
        def sy1(sy):
            B.entry(sy, ent)
            B.dma(sy, cst["zrow"][0:1, :], cst["zhst"][64:65, :], "DI")
        block.sync(sy1)
        di_z = B.cnt["DI"]

        def s0(s):
            B.entry(s, ent)
            B.w(s, "DI", di_z)
            B.chain(s, s.activation(cst["zrow"][0:1, :], cst["zrow"][0:1, :],
                                    AF.Ln), "CA")
            i = s.activation(cst["zrow"][0:1, :], cst["zrow"][0:1, :],
                             AF.Exp, scale=-1.0)
            B.inc(i, "AS")
        block.scalar(s0)
        vs_rz = B.cnt["AS"]

        def sy2(sy):
            B.w(sy, "AS", vs_rz)
            B.dma(sy, zD[0:1, :], cst["zrow"][0:1, :], "DO")
            B.w(sy, "DO", B.cnt["DO"])
            B.dma(sy, cst["zcol"][0:64, :],
                  AP(zD, 0, [[0, 64], [1, CH]]), "DI")
            B.dma(sy, cst["zcol"][64:128, :],
                  AP(zD, CH, [[0, 64], [1, CH]]), "DI")
        block.sync(sy2)
        di_zc = B.cnt["DI"]

        def v2(v):
            B.w(v, "DI", di_zc)
            B.chain(v, v.tensor_mul(hx3[:, :], cst["ohalo"][:, :],
                                    cst["zcol"][:, :]), "CV")
            B.chain(v, v.tensor_add(
                hx3[:, :], hx3[:, :],
                AP(cst["x2hp"], 127, [[CH * 128, 128], [128, CH]])), "CV")
            B.chain(v, v.tensor_mul(hsq[:, :], hx3[:, :], hx3[:, :]), "CV")
            B.chain(v, v.tensor_add(hfold[:, 0:4], hx3[:, 0:4],
                                    hx3[:, 4:8]), "CV")
            B.chain(v, v.tensor_add(hfold[:, 0:2], hfold[:, 0:2],
                                    hfold[:, 2:4]), "CV")
            B.chain(v, v.tensor_add(hfold[:, 0:1], hfold[:, 0:1],
                                    hfold[:, 1:2]), "CV")
            B.chain(v, v.tensor_add(hfold[:, 4:8], hsq[:, 0:4],
                                    hsq[:, 4:8]), "CV")
            B.chain(v, v.tensor_add(hfold[:, 4:6], hfold[:, 4:6],
                                    hfold[:, 6:8]), "CV")
            i = v.tensor_add(hfold[:, 4:5], hfold[:, 4:5], hfold[:, 5:6])
            B.inc(i, "VS")
        block.vector(v2)
        vs_x3 = B.cnt["VS"]

        def t1(t):
            B.entry(t, ent)
            B.w(t, "VS", vs_x3)
            i = t.matmul(ps_h[0:1, 0:1], cst["ones128"][:, :],
                         hfold[:, 0:1], start=True, stop=True)
            B.inc(i, "SS")
            i = t.matmul(ps_h[0:1, 1:2], cst["ones128"][:, :],
                         hfold[:, 4:5], start=True, stop=True)
            B.inc(i, "SS")
        block.tensor(t1)
        ss_h = B.cnt["SS"]

        def v3(v):
            B.w(v, "SS", ss_h)
            B.chain(v, v.tensor_scalar_mul(hsc[0:1, 0:2], ps_h[0:1, 0:2],
                                           1.0 / D), "CV")
            B.chain(v, v.tensor_mul(hrow[0:1, 0:1], hsc[0:1, 0:1],
                                    hsc[0:1, 0:1]), "CV")
            B.chain(v, v.tensor_sub(hsc[0:1, 1:2], hsc[0:1, 1:2],
                                    hrow[0:1, 0:1]), "CV")
            i = v.tensor_scalar_add(hsc[0:1, 1:2], hsc[0:1, 1:2], EPS)
            B.inc(i, "VS")
        block.vector(v3)
        vs_st = B.cnt["VS"]

        def s1(s):
            B.entry(s, ent)
            B.w(s, "VS", vs_st)
            B.chain(s, s.activation(hrow[0:1, 1:2], hsc[0:1, 1:2], AF.Ln),
                    "CA")
            i = s.activation(hsc[0:1, 1:2], hrow[0:1, 1:2], AF.Exp,
                             scale=-0.5)
            B.inc(i, "AS")
        block.scalar(s1)
        as_r = B.cnt["AS"]

        def t2(t):
            B.w(t, "AS", as_r)
            i = t.matmul(ps_hb[:, 0:2], cst["ones1"][0:1, :], hsc[0:1, 0:2],
                         start=True, stop=True)
            B.inc(i, "SS")
        block.tensor(t2)
        ss_b = B.cnt["SS"]

        def v4(v):
            B.w(v, "SS", ss_b)
            B.chain(v, v.tensor_scalar(hx3[:, :], hx3[:, :],
                                       ps_hb[:, 0:1], None, ALU.subtract),
                    "CV")
            B.chain(v, v.tensor_scalar(hx3[:, :], hx3[:, :],
                                       ps_hb[:, 1:2], None, ALU.mult), "CV")
            B.chain(v, v.tensor_scalar(hx3[:, :], hx3[:, :],
                                       cst["hmB"][:, 0:1], None, ALU.mult),
                    "CV")
            i = v.tensor_copy(
                AP(cst["xnb"], 0, [[CH * (S + 1), 128], [S + 1, CH]]),
                hx3[:, :])
            B.inc(i, "VS")
        block.vector(v4)


def _attn_split_residual(B, block, xF, xF2, oaD, cst):
    nc = B.nc
    ent_p = dict(B.cnt)
    with (
        nc.sbuf_tensor(B.nm("pr_x0"), [128, CH * TB], F32) as pr_x0,
        nc.sbuf_tensor(B.nm("pr_x1"), [128, CH * TB], F32) as pr_x1,
        nc.sbuf_tensor(B.nm("pr_o0"), [128, CH * TB], BF16) as pr_o0,
        nc.sbuf_tensor(B.nm("pr_o1"), [128, CH * TB], BF16) as pr_o1,
    ):
        xs = (pr_x0, pr_x1)
        os_ = (pr_o0, pr_o1)
        dp = (B.cnt["D0"], B.cnt["D1"])
        vs0, do0 = B.cnt["VS"], B.cnt["DO"]

        def syio(sy):
            B.entry(sy, ent_p)
            sv = sy.value_load(cst["ridx"][0:1, 0:1])
            for tb in range(NQB):
                B.w(sy, "VS", vs0 + max(0, tb - 1))
                if tb >= 2:
                    B.w(sy, "DO", do0 + 16 * (tb - 1))
                B.dma(sy, os_[tb % 2][:, :].rearrange("p (c t) -> p c t",
                                                      t=TB),
                      oaD.ap().rearrange(
                          "(c p) s -> p c s",
                          p=128)[:, :, tb * TB:(tb + 1) * TB], f"D{tb % 2}")
                B.dma(sy, xs[tb % 2][:, :].rearrange("p (c t) -> p c t",
                                                     t=TB),
                      AP(xF, sv + tb * TB,
                         [[S, 128], [128 * S, CH], [1, TB]]), f"D{tb % 2}")
                if tb >= 1:
                    B.w(sy, "VS", vs0 + tb)
                    B.dma(sy, xf_ap(xF2)[:, :, (tb - 1) * TB:tb * TB],
                          xs[(tb - 1) % 2][:, :].rearrange(
                              "p (c t) -> p c t", t=TB), "DO")
            B.w(sy, "VS", vs0 + NQB)
            B.dma(sy, xf_ap(xF2)[:, :, (NQB - 1) * TB:NQB * TB],
                  xs[(NQB - 1) % 2][:, :].rearrange("p (c t) -> p c t",
                                                    t=TB), "DO")
        block.sync(syio)

        def vb(v):
            B.entry(v, ent_p)
            for tb in range(NQB):
                B.w(v, f"D{tb % 2}", dp[tb % 2] + 32 * (tb // 2 + 1))
                i = v.tensor_add(xs[tb % 2][:, :], xs[tb % 2][:, :],
                                 os_[tb % 2][:, :])
                B.inc(i, "VS")
        block.vector(vb)


def _attn_jobs(cst, causal, wA, wB, k_dst, q_dst):
    jobs = []
    for tb in range(NTB):
        jobs.append((wB, k_dst[:, tb * TB:(tb + 1) * TB],
                     (lambda tb: lambda c: xn_sl(cst, c, tb * TB,
                                                 TB))(tb), TB))
    if causal:
        for tb in range(NTB):
            jobs.append((wA, q_dst[:, tb * TB:(tb + 1) * TB],
                         (lambda tb: lambda c: xn_sl(cst, c, tb * TB,
                                                     TB))(tb), TB))
    else:
        xo = cst["xn2own"]
        for qb in range(NQB):
            jobs.append((wA, q_dst[:, qb * TB:(qb + 1) * TB],
                         (lambda qb: lambda c: xo[
                             :, c * SOUT + qb * TB:
                             c * SOUT + (qb + 1) * TB])(qb), TB))
        jobs.append((wA, q_dst[:, SOUT:SOUT + 1],
                     lambda c: cst["xnqh"][:, c * 128 + 127:
                                           c * 128 + 128], 1))
    return jobs


def _attn_head_pair(B, block, oaD, vtD, sexpD, cst, hp, causal,
                    wq_d, wk_d, winT_d, ld, lc, emit_hp_loads,
                    emit_u_loads, khb, qhb, ldsets):
    nc = B.nc
    ent_h = dict(B.cnt)
    if hp + 1 < HP:
        emit_hp_loads(hp + 1, ent_h)
    NQ = NTB if causal else NQB
    wA, wB, vt_hp = ld["wA"], ld["wB"], ld["vt"]
    u0, u1 = ld["u0"], ld["u1"]
    lsem, vsem = f"L{hp % 2}", f"V{hp % 2}"
    lc_hp, vc_hp = lc[hp]
    k_hp, q_hp = khb[hp % 2], qhb[hp % 2]
    if hp + 1 < HP:
        ld2 = ldsets[(hp + 1) % 2]
        jobs2 = _attn_jobs(cst, causal, ld2["wA"], ld2["wB"],
                           khb[(hp + 1) % 2], qhb[(hp + 1) % 2])
        l2sem, lc2 = f"L{(hp + 1) % 2}", None
    else:
        jobs2 = []
    from contextlib import ExitStack
    with ExitStack() as ctx:
        e = ctx.enter_context
        psb0 = e(nc.sbuf_tensor(B.nm(f"psb0_{hp}"), [128, 2 * TB], BF16))
        psb1 = e(nc.sbuf_tensor(B.nm(f"psb1_{hp}"), [128, 2 * TB], BF16))
        psb2 = e(nc.sbuf_tensor(B.nm(f"psb2_{hp}"), [128, 2 * TB], BF16))
        ob = [[e(nc.sbuf_tensor(B.nm(f"ob{hp}"), [64, TB], BF16))
               for _ in range(2)] for _ in range(2)]
        zrP = e(nc.sbuf_tensor(B.nm(f"zrP{hp}"), [65, 2 * TB], F32))
        rzP = e(nc.sbuf_tensor(B.nm(f"rzP{hp}"), [65, 2 * TB], F32))
        bcsP = e(nc.sbuf_tensor(B.nm(f"bcP{hp}"), [128, 2 * TB], F32))
        sp0 = e(nc.psum_tensor(B.nm(f"sp0_{hp}"), [128, 2 * TB], F32))
        sp1 = e(nc.psum_tensor(B.nm(f"sp1_{hp}"), [128, 2 * TB], F32))
        pv = [[e(nc.psum_tensor(B.nm(f"pv{hp}"), [65, TB], F32))
               for _ in range(2)] for _ in range(2)]
        rzD = nc.dram_tensor(B.nm(f"rzD{hp}"), [2, 2 * TB], F32)

        iters = []
        g_first = {}
        for qb in range(NQ):
            cmax = 4 * qb + 4 if causal else KC
            g_first[qb] = len(iters)
            for ci in range(cmax):
                if causal:
                    kind = "mask" if ci >= 4 * qb else "plain"
                else:
                    kind = "bias"
                iters.append((qb, ci, cmax, kind))
        nit = len(iters)
        units = []
        step = max(1, nit // len(jobs2)) if jobs2 else 0
        nins = 0
        for g in range(nit):
            units.append(("S", g))
            if jobs2 and (g + 1) % step == 0 and nins < len(jobs2):
                units.append(("P", nins))
                nins += 1
        while nins < len(jobs2):
            units.append(("P", nins))
            nins += 1
        s2k = {}
        for k, (tp_u, idx) in enumerate(units):
            if tp_u == "S":
                s2k[idx] = k

        c = dict(B.cnt)
        ss_u = [0] * len(units)
        as_e = [0] * nit
        vs_m = [None] * nit
        ps_p = [0] * nit
        vs_pe = [0] * len(jobs2)
        as_rz, vs_ob = {}, {}
        do_rz, di_bc, do_ob = {}, {}, {}

        def sim_evacA(qb):
            c["AS"] += 1
            as_rz[qb] = c["AS"]

        def sim_evacV(qb):
            c["VS"] += 2
            vs_ob[qb] = c["VS"]

        def sim_evacS(qb):
            c["DO"] += 16
            do_rz[qb] = c["DO"]
            c["DI"] += 16
            di_bc[qb] = c["DI"]
            c[f"D{qb % 2}"] += 32
            do_ob[qb] = c[f"D{qb % 2}"]

        for k, (tp_u, idx) in enumerate(units):
            if tp_u == "S":
                g = idx
                qb, ci, cmax, kind = iters[g]
                if ci == 0 and qb >= 1:
                    sim_evacA(qb - 1)
                c["SS"] += 1
                ss_u[k] = c["SS"]
                c["AS"] += 1
                as_e[g] = c["AS"]
                if ci == 4 and qb >= 1:
                    sim_evacV(qb - 1)
                if kind in ("mask", "bias"):
                    c["VS"] += 1
                    vs_m[g] = c["VS"]
                if g >= 2:
                    c["PS"] += 1
                    ps_p[g - 2] = c["PS"]
            else:
                c["SS"] += 1
                ss_u[k] = c["SS"]
                c["VS"] += 1
                vs_pe[idx] = c["VS"]
        for gg in (nit - 2, nit - 1):
            c["PS"] += 1
            ps_p[gg] = c["PS"]
        sim_evacA(NQ - 1)
        sim_evacV(NQ - 1)
        for qb in range(NQ):
            sim_evacS(qb)
        if not causal:
            c["SS"] += 1
            hs_ss = c["SS"]
            c["AS"] += 2
            a_he = c["AS"]
            c["VS"] += 1
            v_hm = c["VS"]
            c["PS"] += 1
            hs_ps = c["PS"]
            c["AS"] += 4
            a_hc = c["AS"]
            c["DI"] += 16
            d_hm = c["DI"]

        sps = (sp0, sp1)
        pbs = (psb0, psb1, psb2)
        as_prior = B.cnt["AS"]
        ps_prior = B.cnt["PS"]
        vs_prior = B.cnt["VS"]
        lc2v = lc[hp + 1][0] if jobs2 else 0

        def t_att(t):
            B.w(t, lsem, lc_hp)
            B.w(t, vsem, vc_hp)
            B.w(t, "AS", as_prior)
            B.w(t, "VS", vs_prior)

            def pv_part(j):
                qb, ci, cmax, kind = iters[j]
                par = qb % 2
                if ci == 0 and qb >= 2:
                    B.w(t, "VS", vs_ob[qb - 2])
                pb = pbs[j % 3]
                i0 = t.matmul(pv[par][0][:, :],
                              vt_hp[:, ci * 130:ci * 130 + 65],
                              pb[:, 0:TB], start=(ci == 0),
                              stop=(ci == cmax - 1))
                if kind == "plain":
                    B.wi(i0, "AS", as_e[j])
                else:
                    B.wi(i0, "VS", vs_m[j])
                i = t.matmul(pv[par][1][:, :],
                             vt_hp[:, ci * 130 + 65:ci * 130 + 130],
                             pb[:, TB:2 * TB], start=(ci == 0),
                             stop=(ci == cmax - 1))
                B.inc(i, "PS")

            def slot_wait(i0, k):
                if k < 2:
                    return
                pt, pidx = units[k - 2]
                if pt == "S":
                    B.wi(i0, "AS", as_e[pidx])
                else:
                    B.wi(i0, "VS", vs_pe[pidx])

            for k, (tp_u, idx) in enumerate(units):
                sp = sps[k % 2]
                if tp_u == "S":
                    g = idx
                    qb, ci, cmax, kind = iters[g]
                    k0 = ci * 128
                    i0 = t.matmul(sp[:, 0:TB], k_hp[0:64, k0:k0 + 128],
                                  q_hp[0:64, qb * TB:(qb + 1) * TB],
                                  start=True, stop=True,
                                  tile_position=(0, 0))
                    slot_wait(i0, k)
                    i = t.matmul(sp[:, TB:2 * TB],
                                 k_hp[64:128, k0:k0 + 128],
                                 q_hp[64:128, qb * TB:(qb + 1) * TB],
                                 start=True, stop=True,
                                 tile_position=(64, 0))
                    B.inc(i, "SS")
                    if g >= 2:
                        pv_part(g - 2)
                else:
                    j = idx
                    if j == 0:
                        B.w(t, l2sem, lc2v)
                    w_sb, dst, rhs_fn, width = jobs2[j]
                    for cc in range(CH):
                        i = t.matmul(sp[:, 0:width],
                                     w_sb[:, cc * 128:(cc + 1) * 128],
                                     rhs_fn(cc), start=(cc == 0),
                                     stop=(cc == CH - 1))
                        if cc == 0:
                            slot_wait(i, k)
                    B.inc(i, "SS")
            if nit >= 2:
                pv_part(nit - 2)
            pv_part(nit - 1)
            if not causal:
                for kk in (len(units) - 2, len(units) - 1):
                    pt, pidx = units[kk]
                    if pt == "S":
                        B.w(t, "AS", as_e[pidx])
                    else:
                        B.w(t, "VS", vs_pe[pidx])
                for cc in range(KC):
                    t.matmul(sp0[:, cc:cc + 1],
                             k_hp[0:64, cc * 128:(cc + 1) * 128],
                             q_hp[0:64, SOUT:SOUT + 1],
                             start=True, stop=True, tile_position=(0, 0))
                    i = t.matmul(sp0[:, TB + cc:TB + cc + 1],
                                 k_hp[64:128, cc * 128:(cc + 1) * 128],
                                 q_hp[64:128, SOUT:SOUT + 1],
                                 start=True, stop=True,
                                 tile_position=(64, 0))
                B.inc(i, "SS")
                B.w(t, "VS", v_hm)
                B.w(t, "VS", vs_ob[NQ - 2])
                for cc in range(KC):
                    t.matmul(pv[0][0][:, 0:1],
                             vt_hp[:, cc * 130:cc * 130 + 65],
                             pbs[0][:, cc:cc + 1], start=(cc == 0),
                             stop=(cc == KC - 1))
                    i = t.matmul(pv[0][1][:, 0:1],
                                 vt_hp[:, cc * 130 + 65:cc * 130 + 130],
                                 pbs[0][:, TB + cc:TB + cc + 1],
                                 start=(cc == 0), stop=(cc == KC - 1))
                B.inc(i, "PS")
        block.tensor(t_att)

        def emit_evacA(s, qb):
            par = qb % 2
            B.w(s, "PS", ps_p[g_first[qb] + iters[g_first[qb]][2] - 1])
            if qb >= 2:
                B.w(s, "DO", do_rz[qb - 2])
            zr_r = zrP[64 * par:64 * par + 1, :]
            rz_r = rzP[64 * par:64 * par + 1, :]
            B.chain(s, s.activation(zr_r[:, 0:TB],
                                    pv[par][0][64:65, :], AF.Ln), "CA")
            B.chain(s, s.activation(zr_r[:, TB:2 * TB],
                                    pv[par][1][64:65, :], AF.Ln), "CA")
            i = s.activation(rz_r, zr_r, AF.Exp, scale=-1.0)
            B.inc(i, "AS")

        def s_att(s):
            B.entry(s, ent_h)
            for k, (tp_u, idx) in enumerate(units):
                if tp_u != "S":
                    continue
                g = idx
                qb, ci, cmax, kind = iters[g]
                if ci == 0 and qb >= 1:
                    emit_evacA(s, qb - 1)
                B.w(s, "PS", ps_p[g - 3] if g >= 3 else ps_prior)
                i = s.activation(pbs[g % 3][:, :], sps[k % 2][:, :],
                                 AF.Exp, scale=SCALE)
                B.wi(i, "SS", ss_u[k])
                B.inc(i, "AS")
            emit_evacA(s, NQ - 1)
            if not causal:
                B.w(s, "SS", hs_ss)
                B.w(s, "PS", ps_p[nit - 1])
                i = s.activation(pbs[0][:, 0:KC], sp0[:, 0:KC], AF.Exp,
                                 scale=SCALE)
                B.inc(i, "AS")
                i = s.activation(pbs[0][:, TB:TB + KC], sp0[:, TB:TB + KC],
                                 AF.Exp, scale=SCALE)
                B.inc(i, "AS")
                B.w(s, "PS", hs_ps)
                B.w(s, "DI", ent_h["DI"])
                i = s.copy(cst["ohalo"][0:64, hp:hp + 1],
                           pv[0][0][0:64, 0:1])
                B.inc(i, "AS")
                i = s.copy(cst["hstage"][0:64, 0:1], pv[0][1][0:64, 0:1])
                B.inc(i, "AS")
                i = s.copy(cst["zhst"][64:65, hp:hp + 1],
                           pv[0][0][64:65, 0:1])
                B.inc(i, "AS")
                i = s.copy(cst["zhst"][64:65, 8 + hp:9 + hp],
                           pv[0][1][64:65, 0:1])
                B.inc(i, "AS")
        block.scalar(s_att)

        def emit_evacV(v, qb):
            par = qb % 2
            B.w(v, "PS", ps_p[g_first[qb] + iters[g_first[qb]][2] - 1])
            B.w(v, "DI", di_bc[qb])
            if qb >= 2:
                B.w(v, f"D{qb % 2}", do_ob[qb - 2])
            i = v.tensor_mul(ob[par][0][:, :], pv[par][0][0:64, :],
                             bcsP[64 * par:64 * par + 64, 0:TB])
            B.inc(i, "VS")
            i = v.tensor_mul(ob[par][1][:, :], pv[par][1][0:64, :],
                             bcsP[64 * par:64 * par + 64, TB:2 * TB])
            B.inc(i, "VS")

        def v_att(v):
            B.entry(v, ent_h)
            for k, (tp_u, idx) in enumerate(units):
                if tp_u == "P":
                    j = idx
                    w_sb, dst, rhs_fn, width = jobs2[j]
                    i = v.tensor_copy(dst, sps[k % 2][:, 0:width])
                    B.wi(i, "SS", ss_u[k])
                    B.inc(i, "VS")
                    continue
                g = idx
                qb, ci, cmax, kind = iters[g]
                if ci == 4 and qb >= 1:
                    emit_evacV(v, qb - 1)
                if kind in ("mask", "bias"):
                    if g >= 3:
                        B.w(v, "PS", ps_p[g - 3])
                    pb = pbs[g % 3]
                    first = True
                    if kind == "mask":
                        r = ci - 4 * qb
                        for e2 in range(2):
                            i = v.tensor_mul(
                                pb[:, e2 * TB:(e2 + 1) * TB],
                                pb[:, e2 * TB:(e2 + 1) * TB],
                                cst["masks"][:, r * TB:(r + 1) * TB])
                            if first:
                                B.wi(i, "AS", as_e[g])
                                first = False
                    else:
                        M0 = UOFF - qb * TB + ci * 128 + 1
                        for e2, u in ((0, u0), (1, u1)):
                            i = v.tensor_mul(
                                pb[:, e2 * TB:(e2 + 1) * TB],
                                pb[:, e2 * TB:(e2 + 1) * TB],
                                u[:, M0 - (TB - 1):M0 + 1][:, ::-1])
                            if first:
                                B.wi(i, "AS", as_e[g])
                                first = False
                    B.inc(i, "VS")
            emit_evacV(v, NQ - 1)
            if not causal:
                B.w(v, "AS", a_he)
                v.tensor_mul(pbs[0][:, 0:KC], pbs[0][:, 0:KC],
                             u0[:, UOFF + 1:UWB:128])
                i = v.tensor_mul(pbs[0][:, TB:TB + KC],
                                 pbs[0][:, TB:TB + KC],
                                 u1[:, UOFF + 1:UWB:128])
                B.inc(i, "VS")
        block.vector(v_att)

        def sy_att(sy):
            B.entry(sy, ent_h)
            for qb in range(NQ):
                par = qb % 2
                B.w(sy, "AS", as_rz[qb])
                B.dma(sy, rzD.ap()[par:par + 1, :],
                      rzP[64 * par:64 * par + 1, :], "DO")
                B.w(sy, "DO", do_rz[qb])
                if qb >= 2:
                    B.w(sy, "VS", vs_ob[qb - 2])
                B.dma(sy, bcsP[64 * par:64 * par + 64, :],
                      AP(rzD, par * 2 * TB, [[0, 64], [1, 2 * TB]]), "DI")
                B.w(sy, "VS", vs_ob[qb])
                r0 = hp * 128
                B.dma(sy, oaD.ap()[r0:r0 + 64, qb * TB:(qb + 1) * TB],
                      ob[par][0][:, :], f"D{par}")
                B.dma(sy, oaD.ap()[r0 + 64:r0 + 128, qb * TB:(qb + 1) * TB],
                      ob[par][1][:, :], f"D{par}")
            if not causal:
                B.w(sy, "AS", a_hc)
                B.dma(sy, cst["ohalo"][64:128, hp:hp + 1],
                      cst["hstage"][0:64, 0:1], "DI")
        block.sync(sy_att)

        for s_name in ("SS", "AS", "VS", "PS", "DO", "D0", "D1", "DI"):
            assert B.cnt[s_name] == c[s_name], (s_name, B.cnt[s_name],
                                                c[s_name], hp)
        if hp + 1 < HP:
            emit_u_loads(hp + 1)


def _attn_wo_residual(B, block, xF, oaD, wo_d):
    nc = B.nc
    ent_w = dict(B.cnt)
    with (
        nc.sbuf_tensor(B.nm("wo_sb"), [128, CH * D], BF16) as wo_sb,
        nc.sbuf_tensor(B.nm("wr_x0"), [128, CH * TB], F32) as wr_x0,
        nc.sbuf_tensor(B.nm("wr_o0"), [128, CH * TB], BF16) as wr_o0,
        nc.psum_tensor(B.nm("ps_w0"), [128, TB], F32) as ps_w0,
        nc.psum_tensor(B.nm("ps_w1"), [128, TB], F32) as ps_w1,
    ):
        def gw(g):
            B.entry(g, ent_w)
            B.dma(g, wo_sb[:, :].rearrange("p (cb j) -> p cb j", j=D),
                  wo_d.ap().rearrange("(cb p) j -> p cb j", p=128), "DG")
        block.gpsimd(gw)
        dg_w = B.cnt["DG"]
        pss = (ps_w0, ps_w1)
        di0, ts0, vs0, do0 = (B.cnt["DI"], B.cnt["SS"], B.cnt["VS"],
                              B.cnt["DO"])

        def syio(sy):
            B.entry(sy, ent_w)
            for tb in range(NTB):
                if tb > 0:
                    B.w(sy, "VS", vs0 + tb * CH)
                    B.dma(sy, xf_ap(xF)[:, :, (tb - 1) * TB:tb * TB],
                          wr_x0[:, :].rearrange("p (c t) -> p c t", t=TB),
                          "DO")
                    B.w(sy, "DO", B.cnt["DO"])
                B.dma(sy, wr_o0[:, :].rearrange("p (c t) -> p c t", t=TB),
                      oaD.ap().rearrange(
                          "(c p) s -> p c s",
                          p=128)[:, :, tb * TB:(tb + 1) * TB], "DI")
                B.dma(sy, wr_x0[:, :].rearrange("p (c t) -> p c t", t=TB),
                      xf_ap(xF)[:, :, tb * TB:(tb + 1) * TB], "DI")
            B.w(sy, "VS", vs0 + NTB * CH)
            B.dma(sy, xf_ap(xF)[:, :, (NTB - 1) * TB:NTB * TB],
                  wr_x0[:, :].rearrange("p (c t) -> p c t", t=TB), "DO")
        block.sync(syio)

        def tb_(t):
            B.entry(t, ent_w)
            B.w(t, "DG", dg_w)
            for tb in range(NTB):
                B.w(t, "DI", di0 + 32 * (tb + 1))
                for oc in range(CH):
                    ps = pss[oc % 2]
                    n = tb * CH + oc
                    B.w(t, "VS", vs0 + max(0, n - 1))
                    for cc in range(CH):
                        i = t.matmul(ps[:, :],
                                     wo_sb[:, cc * D + oc * 128:
                                           cc * D + (oc + 1) * 128],
                                     wr_o0[:, cc * TB:(cc + 1) * TB],
                                     start=(cc == 0), stop=(cc == CH - 1))
                    B.inc(i, "SS")
        block.tensor(tb_)

        def vb(v):
            B.entry(v, ent_w)
            for tb in range(NTB):
                B.w(v, "DI", di0 + 32 * (tb + 1))
                for oc in range(CH):
                    n = tb * CH + oc
                    B.w(v, "SS", ts0 + n + 1)
                    i = v.tensor_add(wr_x0[:, oc * TB:(oc + 1) * TB],
                                     wr_x0[:, oc * TB:(oc + 1) * TB],
                                     pss[oc % 2][:, :])
                    B.inc(i, "VS")
        block.vector(vb)


def _attn_plain_residual(B, block, xF, oaD):
    nc = B.nc
    ent_p = dict(B.cnt)
    with (
        nc.sbuf_tensor(B.nm("pr_x0"), [128, CH * TB], F32) as pr_x0,
        nc.sbuf_tensor(B.nm("pr_o0"), [128, CH * TB], BF16) as pr_o0,
    ):
        di0, vs0, do0 = B.cnt["DI"], B.cnt["VS"], B.cnt["DO"]

        def syio(sy):
            B.entry(sy, ent_p)
            for tb in range(NTB):
                if tb > 0:
                    B.w(sy, "VS", vs0 + tb)
                    B.dma(sy, xf_ap(xF)[:, :, (tb - 1) * TB:tb * TB],
                          pr_x0[:, :].rearrange("p (c t) -> p c t", t=TB),
                          "DO")
                    B.w(sy, "DO", B.cnt["DO"])
                B.dma(sy, pr_o0[:, :].rearrange("p (c t) -> p c t", t=TB),
                      oaD.ap().rearrange(
                          "(c p) s -> p c s",
                          p=128)[:, :, tb * TB:(tb + 1) * TB], "DI")
                B.dma(sy, pr_x0[:, :].rearrange("p (c t) -> p c t", t=TB),
                      xf_ap(xF)[:, :, tb * TB:(tb + 1) * TB], "DI")
            B.w(sy, "VS", vs0 + NTB)
            B.dma(sy, xf_ap(xF)[:, :, (NTB - 1) * TB:NTB * TB],
                  pr_x0[:, :].rearrange("p (c t) -> p c t", t=TB), "DO")
        block.sync(syio)

        def vb(v):
            B.entry(v, ent_p)
            for tb in range(NTB):
                B.w(v, "DI", di0 + 32 * (tb + 1))
                i = v.tensor_add(pr_x0[:, :], pr_x0[:, :], pr_o0[:, :])
                B.inc(i, "VS")
        block.vector(vb)


def emit_ffn(B, block, xF, cst, wrT_d, wk2T_d, wv2T_d, ntok=S):
    nc = B.nc
    NB = ntok // TB
    ent_f = dict(B.cnt)
    with (
        nc.sbuf_tensor(B.nm("fw_r"), [128, CH * D], BF16) as fw_r,
        nc.sbuf_tensor(B.nm("fw_k"), [128, CH * D], BF16) as fw_k,
        nc.sbuf_tensor(B.nm("fw_v"), [128, CH * D], BF16) as fw_v,
        nc.sbuf_tensor(B.nm("ff_xa"), [128, CH * TB], F32) as ff_xa,
        nc.sbuf_tensor(B.nm("ff_xb"), [128, CH * TB], F32) as ff_xb,
        nc.sbuf_tensor(B.nm("ff_xr0"), [128, CH * TB], BF16) as ff_xr0,
        nc.sbuf_tensor(B.nm("ff_xk0"), [128, CH * TB], BF16) as ff_xk0,
        nc.sbuf_tensor(B.nm("ff_xr1"), [128, CH * TB], BF16) as ff_xr1,
        nc.sbuf_tensor(B.nm("ff_xk1"), [128, CH * TB], BF16) as ff_xk1,
        nc.sbuf_tensor(B.nm("ff_r"), [128, CH * TB], F32) as ff_r,
        nc.sbuf_tensor(B.nm("ff_ka"), [128, CH * TB], BF16) as ff_ka,
        nc.sbuf_tensor(B.nm("ff_t2"), [128, TB], F32) as ff_t2,
        nc.psum_tensor(B.nm("ps_f0"), [128, TB], F32) as ps_f0,
        nc.psum_tensor(B.nm("ps_f1"), [128, TB], F32) as ps_f1,
        nc.psum_tensor(B.nm("ps_f2"), [128, TB], F32) as ps_f2,
        nc.psum_tensor(B.nm("ps_f3"), [128, TB], F32) as ps_f3,
    ):
        def gw(g):
            B.entry(g, ent_f)
            for w_sb, w_d in ((fw_r, wrT_d), (fw_k, wk2T_d), (fw_v, wv2T_d)):
                B.dma(g, w_sb[:, :].rearrange("p (cb j) -> p cb j", j=D),
                      w_d.ap().rearrange("(cb p) j -> p cb j", p=128), "DG")
        block.gpsimd(gw)
        dg_w = B.cnt["DG"]
        xn_vs = B.cnt["VS"]
        pss_sets = ((ps_f0, ps_f1), (ps_f2, ps_f3))
        fxs = (ff_xa, ff_xb)
        xrs = (ff_xr0, ff_xr1)
        xks = (ff_xk0, ff_xk1)
        ssp = {}
        vsm = {}
        prev_vs3 = 0

        def emit_vmix(tbx, first):
            war = ssp.get(tbx - 2)

            def vmix(v, tbx=tbx, first=first, war=war):
                if first:
                    B.entry(v, ent_f)
                    B.w(v, "VS", xn_vs)
                if war is not None:
                    B.w(v, "SS", war)
                for mu1, dst in ((cst["mur1"], xrs[tbx % 2]),
                                 (cst["muk1"], xks[tbx % 2])):
                    for c in range(CH):
                        base = c * (S + 1) + 1 + tbx * TB
                        xp_c = cst["xnb"][:, base - 1:base - 1 + TB]
                        i = v.tensor_scalar(dst[:, c * TB:(c + 1) * TB],
                                            xp_c, mu1[:, c:c + 1], None,
                                            ALU.mult)
                B.chain(v, i, "CV")
                for mu_sb, dst in ((cst["mur"], xrs[tbx % 2]),
                                   (cst["muk"], xks[tbx % 2])):
                    for c in range(CH):
                        base = c * (S + 1) + 1 + tbx * TB
                        x_c = cst["xnb"][:, base:base + TB]
                        i = v.scalar_tensor_tensor(
                            dst[:, c * TB:(c + 1) * TB],
                            x_c, mu_sb[:, c:c + 1],
                            dst[:, c * TB:(c + 1) * TB],
                            ALU.mult, ALU.add)
                B.inc(i, "VS")
            block.vector(vmix)
            vsm[tbx] = B.cnt["VS"]

        emit_vmix(0, True)

        def sy0(sy):
            B.entry(sy, ent_f)
            B.dma(sy, ff_xa[:, :].rearrange("p (c t) -> p c t", t=TB),
                  xf_ap(xF)[:, :, 0:TB], "V0")
        block.sync(sy0)
        di_ready0 = B.cnt["V0"]

        for tb in range(NB):
            ff_xr, ff_xk = xrs[tb % 2], xks[tb % 2]
            ff_x = fxs[tb % 2]
            pss = pss_sets[tb % 2]
            if tb == 0:
                di_all = di_ready0
            vs_mix = vsm[tb]
            ts0 = B.cnt["SS"]
            as0 = B.cnt["AS"]
            vs0 = B.cnt["VS"]

            vadd = 1 if tb + 1 < NB else 0

            def tproj(t, tb=tb, ts0=ts0, as0=as0, vs0=vs0, vs_mix=vs_mix,
                      ff_xr=ff_xr, ff_xk=ff_xk, pss=pss, vadd=vadd):
                if tb == 0:
                    B.entry(t, ent_f)
                    B.w(t, "DG", dg_w)
                B.w(t, "VS", vs_mix)
                n = 0
                for w_sb, src in ((fw_r, ff_xr), (fw_k, ff_xk)):
                    for oc in range(CH):
                        ps = pss[n % 2]
                        if n >= 2:
                            if n - 2 < CH:
                                B.w(t, "AS", as0 + (n - 2) + 1)
                            else:
                                B.w(t, "VS", vs0 + vadd + (n - 2 - CH) + 1)
                        for cc in range(CH):
                            i = t.matmul(ps[:, :],
                                         w_sb[:, cc * D + oc * 128:
                                              cc * D + (oc + 1) * 128],
                                         src[:, cc * TB:(cc + 1) * TB],
                                         start=(cc == 0),
                                         stop=(cc == CH - 1))
                        B.inc(i, "SS")
                        n += 1
            block.tensor(tproj)
            ssp[tb] = B.cnt["SS"]

            def sact(s, ts0=ts0, tb=tb, pss=pss, pv3=prev_vs3):
                if tb == 0:
                    B.entry(s, ent_f)
                for oc in range(CH):
                    if pv3:
                        B.w(s, "VS", pv3 + oc + 1)
                    B.w(s, "SS", ts0 + oc + 1)
                    i = s.activation(ff_r[:, oc * TB:(oc + 1) * TB],
                                     pss[oc % 2][:, :], AF.Sigmoid)
                    B.inc(i, "AS")
            block.scalar(sact)

            if tb + 1 < NB:
                emit_vmix(tb + 1, False)
            vs_emis2 = B.cnt["VS"]

            def vact(v, ts0=ts0, vs_emis2=vs_emis2, pss=pss):
                B.w(v, "VS", vs_emis2)
                for oc in range(CH):
                    n = CH + oc
                    if oc:
                        B.w(v, "VS", vs_emis2 + oc)
                    B.w(v, "SS", ts0 + n + 1)
                    B.chain(v, v.tensor_scalar_max(ff_t2[:, :],
                                                   pss[n % 2][:, :], 0.0),
                            "CV")
                    i = v.tensor_mul(ff_ka[:, oc * TB:(oc + 1) * TB],
                                     ff_t2[:, :], ff_t2[:, :])
                    B.inc(i, "VS")
            block.vector(vact)
            as_r = B.cnt["AS"]
            vs_k = B.cnt["VS"]

            ts1 = B.cnt["SS"]
            vs1 = B.cnt["VS"]

            def tkv(t, ts1=ts1, vs1=vs1, pss=pss):
                B.w(t, "VS", vs_k)
                for oc in range(CH):
                    ps = pss[oc % 2]
                    if oc >= 2:
                        B.w(t, "VS", vs1 + (oc - 2) + 1)
                    for cc in range(CH):
                        i = t.matmul(ps[:, :],
                                     fw_v[:, cc * D + oc * 128:
                                          cc * D + (oc + 1) * 128],
                                     ff_ka[:, cc * TB:(cc + 1) * TB],
                                     start=(cc == 0), stop=(cc == CH - 1))
                    B.inc(i, "SS")
            block.tensor(tkv)

            vs_emis3 = B.cnt["VS"]

            def vkv(v, ts1=ts1, di_all=di_all, vs_emis3=vs_emis3,
                    pss=pss, ff_x=ff_x, tb=tb):
                B.w(v, "VS", vs_emis3)
                B.w(v, "AS", as_r)
                B.w(v, f"V{tb % 2}", di_all)
                for oc in range(CH):
                    if oc:
                        B.w(v, "VS", vs_emis3 + oc)
                    B.w(v, "SS", ts1 + oc + 1)
                    B.chain(v, v.tensor_mul(ff_t2[:, :], pss[oc % 2][:, :],
                                            ff_r[:, oc * TB:(oc + 1) * TB]),
                            "CV")
                    i = v.tensor_add(ff_x[:, oc * TB:(oc + 1) * TB],
                                     ff_x[:, oc * TB:(oc + 1) * TB],
                                     ff_t2[:, :])
                    B.inc(i, "VS")
            block.vector(vkv)
            prev_vs3 = vs_emis3

            def syio(sy, tb=tb, vs_emis3=vs_emis3, ff_x=ff_x,
                     dpar=(B.cnt["D0"], B.cnt["D1"])):
                if tb + 1 < NB:
                    B.w(sy, "D0", dpar[0])
                    B.w(sy, "D1", dpar[1])
                    B.dma(sy, fxs[(tb + 1) % 2][:, :].rearrange(
                              "p (c t) -> p c t", t=TB),
                          xf_ap(xF)[:, :, (tb + 1) * TB:(tb + 2) * TB],
                          f"V{(tb + 1) % 2}")
                for c in range(CH):
                    B.w(sy, "VS", vs_emis3 + c + 1)
                    B.dma(sy, xf_ap(xF)[:, c:c + 1,
                                        tb * TB:(tb + 1) * TB],
                          ff_x[:, c * TB:(c + 1) * TB].rearrange(
                              "p (o t) -> p o t", o=1), f"D{c % 2}")
            block.sync(syio)
            if tb + 1 < NB:
                di_all = B.cnt[f"V{(tb + 1) % 2}"]


def emit_output(B, block, xF, ident, y_d, cst, dyn=False):
    nc = B.nc
    ntt = SOUT // 128
    with (
        nc.sbuf_tensor(B.nm("yo0"), [128, D], F32) as yo0,
        nc.sbuf_tensor(B.nm("yo1"), [128, D], F32) as yo1,
        nc.sbuf_tensor(B.nm("yt"), [128, D], F32) as yt,
        nc.psum_tensor(B.nm("ps_y"), [128, D], F32) as ps_y,
    ):
        yos = (yo0, yo1)
        ent_o = dict(B.cnt)
        dp = (B.cnt["D0"], B.cnt["D1"])
        ts0, as0, do0 = B.cnt["SS"], B.cnt["AS"], B.cnt["DO"]

        def syio(sy):
            B.entry(sy, ent_o)
            if dyn:
                sv = sy.value_load(cst["ridx"][0:1, 0:1])
            for tt in range(ntt):
                B.w(sy, "SS", ts0 + max(0, tt - 1))
                if dyn:
                    srca = AP(xF, sv + tt * 128,
                              [[S, 128], [128 * S, CH], [1, 128]])
                else:
                    srca = xf_ap(xF)[:, :, tt * 128:(tt + 1) * 128]
                B.dma(sy, yos[tt % 2][:, :].rearrange("p (c j) -> p c j",
                                                      j=128),
                      srca, f"D{tt % 2}")
                if tt >= 1:
                    B.w(sy, "AS", as0 + tt)
                    B.dma(sy, y_d[(tt - 1) * 128:tt * 128, :], yt[:, :],
                          "DO")
            B.w(sy, "AS", as0 + ntt)
            B.dma(sy, y_d[(ntt - 1) * 128:ntt * 128, :], yt[:, :], "DO")
            B.w(sy, "DO", B.cnt["DO"])
        block.sync(syio)

        def tb_(t):
            B.entry(t, ent_o)
            for tt in range(ntt):
                B.w(t, f"D{tt % 2}", dp[tt % 2] + 16 * (tt // 2 + 1))
                B.w(t, "AS", as0 + tt)
                for c in range(CH):
                    i = t.transpose(ps_y[:, c * 128:(c + 1) * 128],
                                    yos[tt % 2][:, c * 128:(c + 1) * 128],
                                    ident[:, :])
                B.inc(i, "SS")
        block.tensor(tb_)

        def sb(s):
            B.entry(s, ent_o)
            for tt in range(ntt):
                B.w(s, "SS", ts0 + tt + 1)
                B.w(s, "DO", do0 + 16 * tt)
                i = s.copy(yt[:, :], ps_y[:, :])
                B.inc(i, "AS")
        block.scalar(sb)


_prog_cache = {}


def _get_prog():
    if "nc" not in _prog_cache:
        _prog_cache["nc"] = build_program()
    return _prog_cache["nc"]


def kernel(L0_attn_a_x, L0_attn_a_W_Q, L0_attn_a_W_K, L0_attn_a_W_V,
           L0_attn_a_W_o, L0_attn_b_w_in, L0_attn_b_kernel_offsets,
           L0_attn_b_kernel_amplitudes, L0_attn_b_kernel_sharpness,
           L0_ffn_W_r, L0_ffn_W_k, L0_ffn_W_v, L0_ffn_mu_r, L0_ffn_mu_k):
    def f32(a):
        return np.ascontiguousarray(np.asarray(a, dtype=np.float32))

    x = f32(L0_attn_a_x)
    wq, wk, wv, wo = (f32(L0_attn_a_W_Q), f32(L0_attn_a_W_K),
                      f32(L0_attn_a_W_V), f32(L0_attn_a_W_o))
    winT = np.ascontiguousarray(f32(L0_attn_b_w_in).T)
    wrT = np.ascontiguousarray(f32(L0_ffn_W_r).T)
    wk2T = np.ascontiguousarray(f32(L0_ffn_W_k).T)
    wv2T = np.ascontiguousarray(f32(L0_ffn_W_v).T)
    mur = np.ascontiguousarray(f32(L0_ffn_mu_r).reshape(CH, 128).T)
    muk = np.ascontiguousarray(f32(L0_ffn_mu_k).reshape(CH, 128).T)
    off = f32(L0_attn_b_kernel_offsets)
    amp = f32(L0_attn_b_kernel_amplitudes)
    shp = f32(L0_attn_b_kernel_sharpness)
    toff = np.zeros((128, 2), np.float32)
    tshp = np.zeros((128, 2), np.float32)
    tamp = np.zeros((256, 16), np.float32)
    for h in range(H):
        for kap in range(8):
            for half in range(2):
                toff[h * 8 + kap, half] = off[h, kap + 8 * half]
                tshp[h * 8 + kap, half] = shp[h, kap + 8 * half]
                tamp[half * 128 + h * 8 + kap, h] = amp[h, kap + 8 * half]

    nc = _get_prog()
    in_maps = []
    for cid in range(8):
        seq = cid // 2
        role = cid % 2
        delta = SOUT * role
        rolec = np.array([[float(delta), float(role), 0.0, 0.0]], np.float32)
        ridx = np.array([[delta, 3968 if role == 0 else 1920, 0, 0]],
                        np.uint32)
        in_maps.append({
            "xin": x[seq], "wq": wq, "wk": wk, "wv": wv, "wo": wo,
            "winT": winT, "wrT": wrT, "wk2T": wk2T, "wv2T": wv2T,
            "mur": mur, "muk": muk, "toff": toff, "tshp": tshp,
            "tampbd": tamp, "rolec": rolec, "ridx": ridx,
        })
    res = run_bass_kernel_spmd(nc, in_maps, core_ids=list(range(8)))
    out = np.empty((4, S, D), np.float32)
    for cid in range(8):
        seq, role = cid // 2, cid % 2
        out[seq, SOUT * role:SOUT * (role + 1)] = np.asarray(
            res.results[cid]["y"], dtype=np.float32)
    return out


# version 19
# speedup vs baseline: 1.2164x; 1.0061x over previous
import numpy as np
import concourse.bass as bass
import concourse.mybir as mybir
from concourse.ap import AP
from concourse.bass_utils import run_bass_kernel_spmd

F32 = mybir.dt.float32
BF16 = mybir.dt.bfloat16
AF = mybir.ActivationFunctionType
ALU = mybir.AluOpType

S = 4096
D = 1024
CH = 8
TB = 512
NTB = S // TB
H = 16
HP = 8
DH = 64
KC = S // 128
VTW = H * 65
SW = 2 * S
UW = 8064
SOUT = 2048
NQB = SOUT // TB
UOFF = 2048
UWB = 6018
EPS = 1e-5
SCALE = 0.125
import os as _os
NSUB = int(_os.environ.get("KNSUB", "4"))


class Bld:
    def __init__(self, nc):
        self.nc = nc
        self.sem = {}
        self.cnt = {}
        for name in ("DI", "DO", "DG", "SS", "PS", "AS", "VS", "GS",
                     "CA", "CV", "CG", "D0", "D1", "L0", "L1", "U0", "U1",
                     "V0", "V1"):
            self.sem[name] = nc.alloc_semaphore(name)
            self.cnt[name] = 0

    def inc(self, instr, s, n=1):
        self.cnt[s] += n
        instr.then_inc(self.sem[s], n)
        return self.cnt[s]

    def w(self, eng, s, val):
        if val > 0:
            eng.wait_ge(self.sem[s], val)

    ENTRY = ("SS", "PS", "AS", "VS", "DO", "DG", "D0", "D1", "DI",
             "L0", "L1", "U0", "U1", "V0", "V1")

    def entry(self, eng, ent):
        for s in self.ENTRY:
            self.w(eng, s, ent[s])

    def guard(self, eng, *sems):
        for s in sems:
            self.w(eng, s, self.cnt[s])

    def wi(self, instr, s, val):
        if val > 0:
            instr._wait_ge(self.sem[s], val)

    def dma(self, eng, out, in_, s):
        i = eng.dma_start(out=out, in_=in_)
        return self.inc(i, s, 16)

    def chain(self, eng, instr, s):
        v = self.inc(instr, s, 1)
        eng.wait_ge(self.sem[s], v)

    _uid = 0

    def nm(self, base):
        Bld._uid += 1
        return f"{base}_{Bld._uid}" 


def xf_ap(xf):
    return xf.ap().rearrange("(c p) s -> p c s", p=128)


def build_program():
    nc = bass.Bass("TRN2", target_bir_lowering=False)
    B = Bld(nc)

    xin = nc.dram_tensor("xin", [S, D], F32, kind="ExternalInput")
    wq_d = nc.dram_tensor("wq", [D, D], F32, kind="ExternalInput")
    wk_d = nc.dram_tensor("wk", [D, D], F32, kind="ExternalInput")
    wv_d = nc.dram_tensor("wv", [D, D], F32, kind="ExternalInput")
    wo_d = nc.dram_tensor("wo", [D, D], F32, kind="ExternalInput")
    winT_d = nc.dram_tensor("winT", [D, 3 * D], F32, kind="ExternalInput")
    wrT_d = nc.dram_tensor("wrT", [D, D], F32, kind="ExternalInput")
    wk2T_d = nc.dram_tensor("wk2T", [D, D], F32, kind="ExternalInput")
    wv2T_d = nc.dram_tensor("wv2T", [D, D], F32, kind="ExternalInput")
    mur_d = nc.dram_tensor("mur", [128, CH], F32, kind="ExternalInput")
    muk_d = nc.dram_tensor("muk", [128, CH], F32, kind="ExternalInput")
    toff_d = nc.dram_tensor("toff", [128, 2], F32, kind="ExternalInput")
    tshp_d = nc.dram_tensor("tshp", [128, 2], F32, kind="ExternalInput")
    tamp_d = nc.dram_tensor("tampbd", [256, 16], F32, kind="ExternalInput")
    rolec_d = nc.dram_tensor("rolec", [1, 4], F32, kind="ExternalInput")
    ridx_d = nc.dram_tensor("ridx", [1, 4], mybir.dt.uint32,
                            kind="ExternalInput")
    y_d = nc.dram_tensor("y", [SOUT, D], F32, kind="ExternalOutput")

    xF = nc.dram_tensor("xF", [CH * 128, S], F32)
    xF2 = nc.dram_tensor("xF2", [CH * 128, SOUT], F32)
    xnD = nc.dram_tensor("xnD", [CH * 128, S], BF16)
    zD = nc.dram_tensor("zD", [1, 16], F32)
    oaD = nc.dram_tensor("oaD", [CH * 128, S], BF16)
    vtD = nc.dram_tensor("vtD", [KC * 128, VTW], BF16)
    sexpD = nc.dram_tensor("sexpD", [H, SW], F32)

    ones128 = nc.alloc_sbuf_tensor("ones128", [128, 1], F32)
    ones1 = nc.alloc_sbuf_tensor("ones1", [1, 128], F32)
    ident = nc.alloc_sbuf_tensor("ident", [128, 128], F32)
    masks = nc.alloc_sbuf_tensor("masks", [128, 4 * TB], BF16)
    mur_sb = nc.alloc_sbuf_tensor("mur_sb", [128, CH], F32)
    muk_sb = nc.alloc_sbuf_tensor("muk_sb", [128, CH], F32)
    mur1_sb = nc.alloc_sbuf_tensor("mur1_sb", [128, CH], F32)
    muk1_sb = nc.alloc_sbuf_tensor("muk1_sb", [128, CH], F32)
    deltaB = nc.alloc_sbuf_tensor("deltaB", [128, 1], F32)
    hmB = nc.alloc_sbuf_tensor("hmB", [128, 1], F32)
    ridx_sb = nc.alloc_sbuf_tensor("ridx_sb", [1, 4], mybir.dt.uint32)
    xnb = nc.alloc_sbuf_tensor("xnb", [128, CH * (S + 1)], BF16)

    cst = dict(ones128=ones128, ones1=ones1, ident=ident, masks=masks,
               mur=mur_sb, muk=muk_sb, mur1=mur1_sb, muk1=muk1_sb,
               xnb=xnb, deltaB=deltaB, hmB=hmB, ridx=ridx_sb)

    with nc.Block() as block:
        emit_setup(B, block, xin, mur_d, muk_d, toff_d, tshp_d, tamp_d,
                   rolec_d, ridx_d, xF, sexpD, cst)
        for sub in range(NSUB):
            xf_src = xF2 if sub == 3 else xF
            ntok = SOUT if sub == 3 else S
            emit_statsxn(B, block, xf_src, cst, ntok,
                         mirror=(xnD if sub == 2 else None))
            if sub % 2 == 0:
                causal = (sub == 0)
                emit_attn(B, block, xF, xF2, oaD, vtD, sexpD, cst,
                          wq_d, wk_d, wv_d, wo_d, winT_d, causal, zD, xnD)
            else:
                emit_ffn(B, block, xf_src, cst, wrT_d, wk2T_d, wv2T_d, ntok)
        emit_output(B, block, xF2 if NSUB >= 3 else xF, ident, y_d, cst,
                    dyn=(NSUB < 3))
    return nc


def emit_setup(B, block, xin, mur_d, muk_d, toff_d, tshp_d, tamp_d,
               rolec_d, ridx_d, xF, sexpD, cst):
    nc = B.nc
    with (
        nc.sbuf_tensor(B.nm("toff_sb"), [128, 2], F32) as toff_sb,
        nc.sbuf_tensor(B.nm("tshp_sb"), [128, 2], F32) as tshp_sb,
        nc.sbuf_tensor(B.nm("tnsh_sb"), [128, 2], F32) as tnsh_sb,
        nc.sbuf_tensor(B.nm("tamp_sb"), [128, 2 * 16], F32) as tamp_sb,
        nc.sbuf_tensor(B.nm("iotb"), [128, 2048], F32) as iotb,
        nc.sbuf_tensor(B.nm("tk0b"), [128, 2048], F32) as tk0b,
        nc.sbuf_tensor(B.nm("tk1b"), [128, 2048], F32) as tk1b,
        nc.sbuf_tensor(B.nm("sexpb"), [16, 2 * TB], F32) as sexpb,
        nc.psum_tensor(B.nm("ps_s"), [16, 2 * TB], F32) as ps_s,
    ):
        def g_body(g):
            B.dma(g, cst["mur"][:, :], mur_d[:, :], "DG")
            B.dma(g, cst["muk"][:, :], muk_d[:, :], "DG")
            B.dma(g, toff_sb[:, :], toff_d[:, :], "DG")
            B.dma(g, tshp_sb[:, :], tshp_d[:, :], "DG")
            B.dma(g, tamp_sb[:, :].rearrange("p (u k) -> p u k", k=16),
                  tamp_d.ap().rearrange("(u p) k -> p u k", p=128), "DG")
            B.dma(g, cst["deltaB"][:, :],
                  AP(rolec_d, 0, [[0, 128], [1, 1]]), "DG")
            B.dma(g, cst["hmB"][:, :],
                  AP(rolec_d, 1, [[0, 128], [1, 1]]), "DG")
            B.dma(g, cst["ridx"][:, :], ridx_d[:, :], "DG")
            g.memset(cst["ones128"][:, :], 1.0)
            g.memset(cst["ones1"][:, :], 1.0)
            B.chain(g, g.memset(cst["ident"][:, :], 0.0), "CG")
            g.affine_select(out=cst["ident"][:, :], in_=cst["ident"][:, :],
                            pattern=[[-1, 128]], compare_op=ALU.not_equal,
                            fill=1.0, base=0, channel_multiplier=1)
            B.chain(g, g.memset(cst["masks"][:, :], 1.0), "CG")
            for r in range(4):
                g.affine_select(out=cst["masks"][:, r * TB:(r + 1) * TB],
                                in_=cst["masks"][:, r * TB:(r + 1) * TB],
                                pattern=[[1, TB]], compare_op=ALU.is_ge,
                                fill=0.0, base=-128 * r,
                                channel_multiplier=-1)
            i = g.memset(cst["xnb"][:, 0:CH * (S + 1):(S + 1)], 0.0)
            B.inc(i, "GS")
        block.gpsimd(g_body)
        gs_ready = B.cnt["GS"]
        dg_ready = B.cnt["DG"]

        def v_mu1(v):
            B.w(v, "DG", dg_ready)
            i = v.tensor_scalar(cst["mur1"][:, :], cst["mur"][:, :],
                                -1.0, 1.0, ALU.mult, ALU.add)
            B.inc(i, "VS")
            i = v.tensor_scalar(cst["muk1"][:, :], cst["muk"][:, :],
                                -1.0, 1.0, ALU.mult, ALU.add)
            B.inc(i, "VS")
        block.vector(v_mu1)

        def s_abs(s):
            B.w(s, "DG", dg_ready)
            i = s.activation(tnsh_sb[:, :], tshp_sb[:, :], AF.Abs)
            B.inc(i, "AS")
        block.scalar(s_abs)
        as_abs = B.cnt["AS"]

        def v_neg(v):
            B.w(v, "AS", as_abs)
            i = v.tensor_scalar_mul(tnsh_sb[:, :], tnsh_sb[:, :], -1.0)
            B.inc(i, "VS")
        block.vector(v_neg)
        vs_neg = B.cnt["VS"]

        for blk in range(SW // 2048):
            i0 = blk * 2048

            def g_io(g, i0=i0):
                B.guard(g, "VS")
                i = g.iota(iotb[:, :], pattern=[[-1, 2048]],
                           base=S - 1 - i0, channel_multiplier=0,
                           allow_small_or_imprecise_dtypes=True)
                B.inc(i, "GS")
            block.gpsimd(g_io)
            gs_io = B.cnt["GS"]

            def v_tk(v, gs_io=gs_io):
                B.w(v, "GS", gs_io)
                B.w(v, "VS", vs_neg)
                B.guard(v, "AS")
                B.chain(v, v.tensor_scalar(iotb[:, :], iotb[:, :],
                                           cst["deltaB"][:, 0:1], None,
                                           ALU.add), "CV")
                for tk, half in ((tk0b, 0), (tk1b, 1)):
                    B.chain(v, v.tensor_scalar(tk[:, :], iotb[:, :],
                                               toff_sb[:, half:half + 1],
                                               None, ALU.subtract), "CV")
                    i = v.scalar_tensor_tensor(tk[:, :], tk[:, :],
                                               tnsh_sb[:, half:half + 1],
                                               tk[:, :], ALU.mult, ALU.mult)
                    B.inc(i, "VS")
            block.vector(v_tk)
            vs_tk = B.cnt["VS"]

            def s_ek(s, vs_tk=vs_tk):
                B.w(s, "VS", vs_tk)
                B.guard(s, "SS")
                s.activation(tk0b[:, :], tk0b[:, :], AF.Exp)
                i = s.activation(tk1b[:, :], tk1b[:, :], AF.Exp)
                B.inc(i, "AS")
            block.scalar(s_ek)
            as_ek = B.cnt["AS"]

            ss_mm0 = B.cnt["SS"]
            as_se0 = B.cnt["AS"]

            def t_amp(t, as_ek=as_ek, ss_mm0=ss_mm0, as_se0=as_se0):
                B.w(t, "AS", as_ek)
                for ch in range(4):
                    c0 = ch * TB
                    bank = ch % 2
                    B.w(t, "AS", as_se0 + max(0, ch - 1))
                    t.matmul(ps_s[0:16, bank * TB:(bank + 1) * TB],
                             tamp_sb[:, 0:16], tk0b[:, c0:c0 + TB],
                             start=True, stop=False)
                    i = t.matmul(ps_s[0:16, bank * TB:(bank + 1) * TB],
                                 tamp_sb[:, 16:32], tk1b[:, c0:c0 + TB],
                                 start=False, stop=True)
                    B.inc(i, "SS")
            block.tensor(t_amp)

            do_se0 = B.cnt["DO"]

            def s_se(s, ss_mm0=ss_mm0, do_se0=do_se0):
                for ch in range(4):
                    bank = ch % 2
                    B.w(s, "SS", ss_mm0 + ch + 1)
                    B.w(s, "DO", do_se0 + 16 * max(0, ch - 1))
                    i = s.activation(sexpb[0:16, bank * TB:(bank + 1) * TB],
                                     ps_s[0:16, bank * TB:(bank + 1) * TB],
                                     AF.Exp)
                    B.inc(i, "AS")
            block.scalar(s_se)
            as_se1 = B.cnt["AS"]

            do_se1 = B.cnt["DO"]

            def sy_se(sy, i0=i0, as_se0=as_se0, do_se1=do_se1):
                for ch in range(4):
                    bank = ch % 2
                    B.w(sy, "AS", as_se0 + ch + 1)
                    B.w(sy, "DO", do_se1 + 16 * ch)
                    B.dma(sy, sexpD.ap()[:, i0 + ch * TB:i0 + (ch + 1) * TB],
                          sexpb[0:16, bank * TB:(bank + 1) * TB], "DO")
            block.sync(sy_se)

    if True:
        do_sexp = B.cnt["DO"]
        with (
            nc.sbuf_tensor(B.nm("xt0"), [128, D], F32) as xt0,
            nc.sbuf_tensor(B.nm("xt1"), [128, D], F32) as xt1,
            nc.sbuf_tensor(B.nm("xt2"), [128, D], F32) as xt2,
            nc.psum_tensor(B.nm("ps_t"), [128, D], F32) as ps_t,
        ):
            xts = (xt0, xt1)
            ent_t = dict(B.cnt)
            dp = (B.cnt["D0"], B.cnt["D1"])
            ts0, as1, do0 = B.cnt["SS"], B.cnt["AS"], B.cnt["DO"]

            def syio(sy):
                B.entry(sy, ent_t)
                B.w(sy, "GS", ent_t["GS"])
                B.w(sy, "DO", do_sexp)
                for tt in range(KC):
                    B.w(sy, "SS", ts0 + max(0, tt - 1))
                    B.dma(sy, xts[tt % 2][:, :],
                          xin[tt * 128:(tt + 1) * 128, :], f"D{tt % 2}")
                    if tt >= 1:
                        B.w(sy, "AS", as1 + tt)
                        dst = xf_ap(xF)[:, :, (tt - 1) * 128:tt * 128]
                        B.dma(sy, dst,
                              xt2[:, :].rearrange("p (c j) -> p c j", j=128),
                              "DO")
                B.w(sy, "AS", as1 + KC)
                dst = xf_ap(xF)[:, :, (KC - 1) * 128:KC * 128]
                B.dma(sy, dst,
                      xt2[:, :].rearrange("p (c j) -> p c j", j=128), "DO")
            block.sync(syio)

            def tb_(t):
                B.entry(t, ent_t)
                B.w(t, "GS", gs_ready)
                for tt in range(KC):
                    B.w(t, f"D{tt % 2}", dp[tt % 2] + 16 * (tt // 2 + 1))
                    B.w(t, "AS", as1 + tt)
                    for c in range(CH):
                        i = t.transpose(ps_t[:, c * 128:(c + 1) * 128],
                                        xts[tt % 2][:, c * 128:(c + 1) * 128],
                                        cst["ident"][:, :])
                    B.inc(i, "SS")
            block.tensor(tb_)

            def sb_(s):
                B.entry(s, ent_t)
                for tt in range(KC):
                    B.w(s, "SS", ts0 + tt + 1)
                    B.w(s, "DO", do0 + 16 * tt)
                    i = s.copy(xt2[:, :], ps_t[:, :])
                    B.inc(i, "AS")
            block.scalar(sb_)




def emit_statsxn(B, block, xF, cst, ntok=S, mirror=None):
    NB = ntok // TB
    nc = B.nc
    with (
        nc.sbuf_tensor(B.nm("sx_x0"), [128, CH * TB], F32) as sx_x0,
        nc.sbuf_tensor(B.nm("sx_x1"), [128, CH * TB], F32) as sx_x1,
        nc.sbuf_tensor(B.nm("sx_q0"), [128, CH * TB], F32) as sx_q0,
        nc.sbuf_tensor(B.nm("sx_q1"), [128, CH * TB], F32) as sx_q1,
        nc.sbuf_tensor(B.nm("sx_m"), [1, 2 * TB], F32) as sx_m,
        nc.sbuf_tensor(B.nm("sx_s"), [1, 2 * TB], F32) as sx_s,
        nc.sbuf_tensor(B.nm("sx_t"), [1, 2 * TB], F32) as sx_t,
        nc.psum_tensor(B.nm("ps_m0"), [1, 2 * TB], F32) as ps_m0,
        nc.psum_tensor(B.nm("ps_m1"), [1, 2 * TB], F32) as ps_m1,
        nc.psum_tensor(B.nm("ps_b0"), [128, 2 * TB], F32) as ps_b0,
        nc.psum_tensor(B.nm("ps_b1"), [128, 2 * TB], F32) as ps_b1,
    ):
        xbs = (sx_x0, sx_x1)
        sqs = (sx_q0, sx_q1)
        pms = (ps_m0, ps_m1)
        pbs = (ps_b0, ps_b1)
        ent = dict(B.cnt)
        dp = (B.cnt["D0"], B.cnt["D1"])
        vs0 = B.cnt["VS"]
        ss0 = B.cnt["SS"]
        as0 = B.cnt["AS"]
        NVT = 2 + CH

        def vs_tb(tb):
            return vs0 + NVT * (tb + 1)

        def syi(sy):
            B.entry(sy, ent)
            for tb in range(NB):
                if tb >= 2:
                    B.w(sy, "VS", vs_tb(tb - 2))
                B.dma(sy, xbs[tb % 2][:, :].rearrange(
                    "p (c t) -> p c t", t=TB),
                    xf_ap(xF)[:, :, tb * TB:(tb + 1) * TB], f"D{tb % 2}")
        block.sync(syi)

        def vb(v):
            B.entry(v, ent)
            for tb in range(NB):
                B.w(v, f"D{tb % 2}", dp[tb % 2] + 16 * (tb // 2 + 1))
                if tb >= 2:
                    B.w(v, "SS", ss0 + 3 * (tb - 2) + 2)
                i = v.tensor_mul(sqs[tb % 2][:, :], xbs[tb % 2][:, :],
                                 xbs[tb % 2][:, :])
                B.inc(i, "VS")
                B.w(v, "SS", ss0 + 3 * tb + 2)
                pm = pms[tb % 2]
                B.chain(v, v.tensor_scalar_mul(sx_m[0:1, :], pm[0:1, :],
                                               1.0 / D), "CV")
                B.chain(v, v.tensor_mul(sx_t[0:1, 0:TB], sx_m[0:1, 0:TB],
                                        sx_m[0:1, 0:TB]), "CV")
                B.chain(v, v.tensor_sub(sx_m[0:1, TB:2 * TB],
                                        sx_m[0:1, TB:2 * TB],
                                        sx_t[0:1, 0:TB]), "CV")
                i = v.tensor_scalar_add(sx_m[0:1, TB:2 * TB],
                                        sx_m[0:1, TB:2 * TB], EPS)
                B.inc(i, "VS")
                B.w(v, "SS", ss0 + 3 * tb + 3)
                pb = pbs[tb % 2]
                tmp = sqs[tb % 2]
                for c in range(CH):
                    i = v.tensor_sub(tmp[:, c * TB:(c + 1) * TB],
                                     xbs[tb % 2][:, c * TB:(c + 1) * TB],
                                     pb[:, 0:TB])
                B.chain(v, i, "CV")
                for c in range(CH):
                    base = c * (S + 1) + 1 + tb * TB
                    i = v.tensor_mul(cst["xnb"][:, base:base + TB],
                                     tmp[:, c * TB:(c + 1) * TB],
                                     pb[:, TB:2 * TB])
                    B.inc(i, "VS")
        block.vector(vb)

        def tb_(t):
            B.entry(t, ent)
            for tb in range(NB):
                pm = pms[tb % 2]
                B.w(t, f"D{tb % 2}", dp[tb % 2] + 16 * (tb // 2 + 1))
                if tb >= 2:
                    B.w(t, "VS", vs0 + NVT * (tb - 2) + 2)
                for c in range(CH):
                    i = t.matmul(pm[0:1, 0:TB], cst["ones128"][:, :],
                                 xbs[tb % 2][:, c * TB:(c + 1) * TB],
                                 start=(c == 0), stop=(c == CH - 1))
                B.inc(i, "SS")
                B.w(t, "VS", vs0 + NVT * tb + 1)

                for c in range(CH):
                    i = t.matmul(pm[0:1, TB:2 * TB], cst["ones128"][:, :],
                                 sqs[tb % 2][:, c * TB:(c + 1) * TB],
                                 start=(c == 0), stop=(c == CH - 1))
                B.inc(i, "SS")
                B.w(t, "AS", as0 + tb + 1)
                if tb >= 2:
                    B.w(t, "VS", vs_tb(tb - 2))
                pb = pbs[tb % 2]
                t.matmul(pb[:, 0:TB], cst["ones1"][0:1, :],
                         sx_m[0:1, 0:TB], start=True, stop=True)
                i = t.matmul(pb[:, TB:2 * TB], cst["ones1"][0:1, :],
                             sx_s[0:1, 0:TB], start=True, stop=True)
                B.inc(i, "SS")
        block.tensor(tb_)

        def sb(s):
            B.entry(s, ent)
            for tb in range(NB):
                B.w(s, "VS", vs0 + NVT * tb + 2)
                B.chain(s, s.activation(sx_t[0:1, TB:2 * TB],
                                        sx_m[0:1, TB:2 * TB], AF.Ln), "CA")
                i = s.activation(sx_s[0:1, 0:TB], sx_t[0:1, TB:2 * TB],
                                 AF.Exp, scale=-0.5)
                B.inc(i, "AS")
        block.scalar(sb)

        if mirror is not None:
            def sym(sy):
                for tb in range(NB):
                    B.w(sy, "VS", vs_tb(tb))
                    src_ap = AP(cst["xnb"].tensor
                                if hasattr(cst["xnb"], "tensor")
                                else cst["xnb"], 1 + tb * TB,
                                [[CH * (S + 1), 128], [S + 1, CH], [1, TB]])
                    B.dma(sy, xf_ap(mirror)[:, :, tb * TB:(tb + 1) * TB],
                          src_ap, "DO")
            block.sync(sym)


def xn_sl(cst, c, t0, tl):
    return cst["xnb"][:, c * (S + 1) + 1 + t0: c * (S + 1) + 1 + t0 + tl]


def emit_attn(B, block, xF, xF2, oaD, vtD, sexpD, cst,
              wq_d, wk_d, wv_d, wo_d, winT_d, causal, zD, xnD):
    nc = B.nc
    from contextlib import ExitStack
    with ExitStack() as actx:
        if not causal:
            ae = actx.enter_context
            cst = dict(cst)
            cst["xn2own"] = ae(nc.sbuf_tensor(B.nm("xn2own"),
                                              [128, CH * SOUT], BF16))
            cst["xnqh"] = ae(nc.sbuf_tensor(B.nm("xnqh"), [128, CH * 128],
                                            BF16))
            cst["x2hp"] = ae(nc.sbuf_tensor(B.nm("x2hp"), [128, CH * 128],
                                            F32))
            cst["ohalo"] = ae(nc.sbuf_tensor(B.nm("ohalo"), [128, CH], F32))
            cst["zhst"] = ae(nc.sbuf_tensor(B.nm("zhst"), [65, 16], F32))
            cst["hstage"] = ae(nc.sbuf_tensor(B.nm("hstage"), [65, 1], F32))
            cst["zrow"] = ae(nc.sbuf_tensor(B.nm("zrow"), [1, 16], F32))
            cst["zcol"] = ae(nc.sbuf_tensor(B.nm("zcol"), [128, CH], F32))
        _emit_attn_inner(B, block, xF, xF2, oaD, vtD, sexpD, cst,
                         wq_d, wk_d, wv_d, wo_d, winT_d, causal, zD, xnD)


def _emit_attn_inner(B, block, xF, xF2, oaD, vtD, sexpD, cst,
                     wq_d, wk_d, wv_d, wo_d, winT_d, causal, zD, xnD):
    nc = B.nc
    ent_a = dict(B.cnt)
    with (
        nc.sbuf_tensor(B.nm("wv_sb"), [128, CH * D], BF16) as wv_sb,
        nc.sbuf_tensor(B.nm("vt_st"), [128, VTW], BF16) as vt_st,
        nc.psum_tensor(B.nm("ps_v0"), [128, D], F32) as ps_v0,
        nc.psum_tensor(B.nm("ps_v1"), [128, D], F32) as ps_v1,
    ):
        def gld(g):
            B.entry(g, ent_a)
            src = wv_d.ap() if causal else winT_d.ap()[:, D:2 * D]
            B.dma(g, wv_sb[:, :].rearrange("p (cb j) -> p cb j", j=D),
                  src.rearrange("(cb p) j -> p cb j", p=128), "DG")
            i = g.memset(vt_st[:, 64:VTW:65], 1.0)
            B.inc(i, "GS")
        block.gpsimd(gld)
        dg_w = B.cnt["DG"]
        gs_st = B.cnt["GS"]
        xn_vs = B.cnt["VS"]

        psv = (ps_v0, ps_v1)
        ts0, vs0, do0 = B.cnt["SS"], B.cnt["VS"], B.cnt["DO"]

        def tb_(t):
            B.entry(t, ent_a)
            B.w(t, "DG", dg_w)
            B.w(t, "VS", xn_vs)
            for tt in range(KC):
                ps = psv[tt % 2]
                B.w(t, "VS", vs0 + max(0, tt - 1))
                for c in range(CH):
                    t.matmul(ps[:, 0:TB],
                             xn_sl(cst, c, tt * 128, 128),
                             wv_sb[:, c * D:c * D + TB],
                             start=(c == 0), stop=(c == CH - 1))
                    i = t.matmul(ps[:, TB:D],
                                 xn_sl(cst, c, tt * 128, 128),
                                 wv_sb[:, c * D + TB:(c + 1) * D],
                                 start=(c == 0), stop=(c == CH - 1))
                B.inc(i, "SS")
        block.tensor(tb_)

        def vb(v):
            B.entry(v, ent_a)
            B.w(v, "GS", gs_st)
            for tt in range(KC):
                ps = psv[tt % 2]
                B.w(v, "SS", ts0 + tt + 1)
                B.w(v, "DO", do0 + 16 * tt)
                for half in range(2):
                    src = ps[:, half * TB:(half + 1) * TB].rearrange(
                        "p (h d) -> p h d", d=64)
                    dst = AP(vt_st, half * 8 * 65,
                             [[VTW, 128], [65, 8], [1, 64]])
                    i = v.tensor_copy(dst, src)
                B.inc(i, "VS")
        block.vector(vb)

        def syo(sy):
            B.entry(sy, ent_a)
            for tt in range(KC):
                B.w(sy, "VS", vs0 + tt + 1)
                dst = vtD.ap().rearrange("(tt p) w -> p tt w",
                                         p=128)[:, tt:tt + 1, :]
                B.dma(sy, dst,
                      vt_st[:, :].rearrange("p (o w) -> p o w", o=1), "DO")
        block.sync(syo)

    if not causal:
        ent_q = dict(B.cnt)

        def syq(sy):
            B.entry(sy, ent_q)
            sv = sy.value_load(cst["ridx"][0:1, 0:1])
            svh = sy.value_load(cst["ridx"][0:1, 1:2])
            B.dma(sy, cst["xn2own"][:, :].rearrange("p (c t) -> p c t",
                                                    t=SOUT),
                  AP(xnD, sv, [[S, 128], [128 * S, CH], [1, SOUT]]), "DI")
            B.dma(sy, cst["xnqh"][:, :].rearrange("p (c t) -> p c t", t=128),
                  AP(xnD, svh, [[S, 128], [128 * S, CH], [1, 128]]), "DI")
            B.dma(sy, cst["x2hp"][:, :].rearrange("p (c t) -> p c t", t=128),
                  AP(xF, svh, [[S, 128], [128 * S, CH], [1, 128]]), "DI")
        block.sync(syq)

    uw = UWB if not causal else 2
    from contextlib import ExitStack as _ES
    with _ES() as lctx:
        le = lctx.enter_context
        u0 = le(nc.sbuf_tensor(B.nm("ldu0"), [128, uw], BF16))
        u1 = le(nc.sbuf_tensor(B.nm("ldu1"), [128, uw], BF16))
        ldsets = []
        for _b in range(2):
            ldsets.append({
                "wA": le(nc.sbuf_tensor(B.nm("ldwA"), [128, CH * 128],
                                        BF16)),
                "wB": le(nc.sbuf_tensor(B.nm("ldwB"), [128, CH * 128],
                                        BF16)),
                "vt": le(nc.sbuf_tensor(B.nm("ldvt"), [128, KC * 130],
                                        BF16)),
                "u0": u0,
                "u1": u1,
            })
        lc = {}

        def emit_hp_loads(hpx, ent):
            ld = ldsets[hpx % 2]
            sname = f"L{hpx % 2}"

            def gl(g, hpx=hpx, ld=ld, ent=ent, sname=sname):
                B.entry(g, ent)
                if causal:
                    sq, sk = wq_d.ap(), wk_d.ap()
                else:
                    sk = winT_d.ap()[:, 0:D]
                    sq = winT_d.ap()[:, 2 * D:3 * D]
                for w_sb, srcw in ((ld["wA"], sq), (ld["wB"], sk)):
                    B.dma(g, w_sb[:, :].rearrange("p (cb j) -> p cb j",
                                                  j=128),
                          srcw[:, hpx * 128:(hpx + 1) * 128].rearrange(
                              "(cb p) j -> p cb j", p=128), sname)
            block.gpsimd(gl)

            vname = f"V{hpx % 2}"

            def syl(sy, hpx=hpx, ld=ld, ent=ent, vname=vname):
                B.entry(sy, ent)
                srcv = vtD.ap().rearrange(
                    "(tt p) w -> p tt w",
                    p=128)[:, :, hpx * 130:(hpx + 1) * 130]
                B.dma(sy, ld["vt"][:, :].rearrange("p (tt w) -> p tt w",
                                                   w=130), srcv, vname)
            block.sync(syl)
            lc[hpx] = (B.cnt[sname], B.cnt[vname])

        def emit_u_loads(hpx):
            if causal:
                return
            uname = f"U{hpx % 2}"
            vs_tgt = B.cnt["VS"]

            def gu(g, hpx=hpx, uname=uname, vs_tgt=vs_tgt):
                B.w(g, "VS", vs_tgt)
                B.dma(g, u0[:, 1:UWB],
                      AP(sexpD, (2 * hpx) * SW + UOFF,
                         [[1, 128], [1, UWB - 1]]), uname)
                B.dma(g, u1[:, 1:UWB],
                      AP(sexpD, (2 * hpx + 1) * SW + UOFF,
                         [[1, 128], [1, UWB - 1]]), uname)
            block.gpsimd(gu)

        emit_hp_loads(0, dict(B.cnt))
        emit_u_loads(0)
        qw = S if causal else SOUT + 128
        khb = [le(nc.sbuf_tensor(B.nm("khb"), [128, S], BF16))
               for _ in range(2)]
        qhb = [le(nc.sbuf_tensor(B.nm("qhb"), [128, qw], BF16))
               for _ in range(2)]

        with (
            nc.psum_tensor(B.nm("pp0"), [128, TB], F32) as pp0,
            nc.psum_tensor(B.nm("pp1"), [128, TB], F32) as pp1,
        ):
            ent_p0 = dict(B.cnt)
            jobs0 = _attn_jobs(cst, causal, ldsets[0]["wA"], ldsets[0]["wB"],
                               khb[0], qhb[0])
            pts0, pas0 = B.cnt["SS"], B.cnt["AS"]
            pps = (pp0, pp1)

            def tp0(t):
                B.entry(t, ent_p0)
                B.w(t, "L0", lc[0][0])
                for n, (w_sb, dst, rhs_fn, width) in enumerate(jobs0):
                    ps = pps[n % 2]
                    B.w(t, "AS", pas0 + max(0, n - 1))
                    for cc in range(CH):
                        i = t.matmul(ps[:, 0:width],
                                     w_sb[:, cc * 128:(cc + 1) * 128],
                                     rhs_fn(cc), start=(cc == 0),
                                     stop=(cc == CH - 1))
                    B.inc(i, "SS")
            block.tensor(tp0)

            def sp0_(s):
                B.entry(s, ent_p0)
                for n, (w_sb, dst, rhs_fn, width) in enumerate(jobs0):
                    B.w(s, "SS", pts0 + n + 1)
                    i = s.copy(dst, pps[n % 2][:, 0:width])
                    B.inc(i, "AS")
            block.scalar(sp0_)

        for hp in range(HP):
            _attn_head_pair(B, block, oaD, vtD, sexpD, cst, hp, causal,
                            wq_d, wk_d, winT_d, ldsets[hp % 2], lc,
                            emit_hp_loads, emit_u_loads, khb, qhb, ldsets)

    if causal:
        _attn_wo_residual(B, block, xF, oaD, wo_d)
    else:
        _halo_finalize(B, block, cst, zD, xF2)
        _attn_split_residual(B, block, xF, xF2, oaD, cst)


def _halo_finalize(B, block, cst, zD, xF2):
    nc = B.nc
    ent = dict(B.cnt)
    with (
        nc.sbuf_tensor(B.nm("hx3"), [128, CH], F32) as hx3,
        nc.sbuf_tensor(B.nm("hsq"), [128, CH], F32) as hsq,
        nc.sbuf_tensor(B.nm("hfold"), [128, CH], F32) as hfold,
        nc.sbuf_tensor(B.nm("hrow"), [1, CH], F32) as hrow,
        nc.sbuf_tensor(B.nm("hsc"), [1, 2], F32) as hsc,
        nc.psum_tensor(B.nm("ps_h"), [1, 2 * TB], F32) as ps_h,
        nc.psum_tensor(B.nm("ps_hb"), [128, 2], F32) as ps_hb,
    ):
        def sy1(sy):
            B.entry(sy, ent)
            B.dma(sy, cst["zrow"][0:1, :], cst["zhst"][64:65, :], "DI")
        block.sync(sy1)
        di_z = B.cnt["DI"]

        def s0(s):
            B.entry(s, ent)
            B.w(s, "DI", di_z)
            B.chain(s, s.activation(cst["zrow"][0:1, :], cst["zrow"][0:1, :],
                                    AF.Ln), "CA")
            i = s.activation(cst["zrow"][0:1, :], cst["zrow"][0:1, :],
                             AF.Exp, scale=-1.0)
            B.inc(i, "AS")
        block.scalar(s0)
        vs_rz = B.cnt["AS"]

        def sy2(sy):
            B.w(sy, "AS", vs_rz)
            B.dma(sy, zD[0:1, :], cst["zrow"][0:1, :], "DO")
            B.w(sy, "DO", B.cnt["DO"])
            B.dma(sy, cst["zcol"][0:64, :],
                  AP(zD, 0, [[0, 64], [1, CH]]), "DI")
            B.dma(sy, cst["zcol"][64:128, :],
                  AP(zD, CH, [[0, 64], [1, CH]]), "DI")
        block.sync(sy2)
        di_zc = B.cnt["DI"]

        def v2(v):
            B.w(v, "DI", di_zc)
            B.chain(v, v.tensor_mul(hx3[:, :], cst["ohalo"][:, :],
                                    cst["zcol"][:, :]), "CV")
            B.chain(v, v.tensor_add(
                hx3[:, :], hx3[:, :],
                AP(cst["x2hp"], 127, [[CH * 128, 128], [128, CH]])), "CV")
            B.chain(v, v.tensor_mul(hsq[:, :], hx3[:, :], hx3[:, :]), "CV")
            B.chain(v, v.tensor_add(hfold[:, 0:4], hx3[:, 0:4],
                                    hx3[:, 4:8]), "CV")
            B.chain(v, v.tensor_add(hfold[:, 0:2], hfold[:, 0:2],
                                    hfold[:, 2:4]), "CV")
            B.chain(v, v.tensor_add(hfold[:, 0:1], hfold[:, 0:1],
                                    hfold[:, 1:2]), "CV")
            B.chain(v, v.tensor_add(hfold[:, 4:8], hsq[:, 0:4],
                                    hsq[:, 4:8]), "CV")
            B.chain(v, v.tensor_add(hfold[:, 4:6], hfold[:, 4:6],
                                    hfold[:, 6:8]), "CV")
            i = v.tensor_add(hfold[:, 4:5], hfold[:, 4:5], hfold[:, 5:6])
            B.inc(i, "VS")
        block.vector(v2)
        vs_x3 = B.cnt["VS"]

        def t1(t):
            B.entry(t, ent)
            B.w(t, "VS", vs_x3)
            i = t.matmul(ps_h[0:1, 0:1], cst["ones128"][:, :],
                         hfold[:, 0:1], start=True, stop=True)
            B.inc(i, "SS")
            i = t.matmul(ps_h[0:1, 1:2], cst["ones128"][:, :],
                         hfold[:, 4:5], start=True, stop=True)
            B.inc(i, "SS")
        block.tensor(t1)
        ss_h = B.cnt["SS"]

        def v3(v):
            B.w(v, "SS", ss_h)
            B.chain(v, v.tensor_scalar_mul(hsc[0:1, 0:2], ps_h[0:1, 0:2],
                                           1.0 / D), "CV")
            B.chain(v, v.tensor_mul(hrow[0:1, 0:1], hsc[0:1, 0:1],
                                    hsc[0:1, 0:1]), "CV")
            B.chain(v, v.tensor_sub(hsc[0:1, 1:2], hsc[0:1, 1:2],
                                    hrow[0:1, 0:1]), "CV")
            i = v.tensor_scalar_add(hsc[0:1, 1:2], hsc[0:1, 1:2], EPS)
            B.inc(i, "VS")
        block.vector(v3)
        vs_st = B.cnt["VS"]

        def s1(s):
            B.entry(s, ent)
            B.w(s, "VS", vs_st)
            B.chain(s, s.activation(hrow[0:1, 1:2], hsc[0:1, 1:2], AF.Ln),
                    "CA")
            i = s.activation(hsc[0:1, 1:2], hrow[0:1, 1:2], AF.Exp,
                             scale=-0.5)
            B.inc(i, "AS")
        block.scalar(s1)
        as_r = B.cnt["AS"]

        def t2(t):
            B.w(t, "AS", as_r)
            i = t.matmul(ps_hb[:, 0:2], cst["ones1"][0:1, :], hsc[0:1, 0:2],
                         start=True, stop=True)
            B.inc(i, "SS")
        block.tensor(t2)
        ss_b = B.cnt["SS"]

        def v4(v):
            B.w(v, "SS", ss_b)
            B.chain(v, v.tensor_scalar(hx3[:, :], hx3[:, :],
                                       ps_hb[:, 0:1], None, ALU.subtract),
                    "CV")
            B.chain(v, v.tensor_scalar(hx3[:, :], hx3[:, :],
                                       ps_hb[:, 1:2], None, ALU.mult), "CV")
            B.chain(v, v.tensor_scalar(hx3[:, :], hx3[:, :],
                                       cst["hmB"][:, 0:1], None, ALU.mult),
                    "CV")
            i = v.tensor_copy(
                AP(cst["xnb"], 0, [[CH * (S + 1), 128], [S + 1, CH]]),
                hx3[:, :])
            B.inc(i, "VS")
        block.vector(v4)


def _attn_split_residual(B, block, xF, xF2, oaD, cst):
    nc = B.nc
    ent_p = dict(B.cnt)
    with (
        nc.sbuf_tensor(B.nm("pr_x0"), [128, CH * TB], F32) as pr_x0,
        nc.sbuf_tensor(B.nm("pr_x1"), [128, CH * TB], F32) as pr_x1,
        nc.sbuf_tensor(B.nm("pr_o0"), [128, CH * TB], BF16) as pr_o0,
        nc.sbuf_tensor(B.nm("pr_o1"), [128, CH * TB], BF16) as pr_o1,
    ):
        xs = (pr_x0, pr_x1)
        os_ = (pr_o0, pr_o1)
        dp = (B.cnt["D0"], B.cnt["D1"])
        vs0, do0 = B.cnt["VS"], B.cnt["DO"]

        def syio(sy):
            B.entry(sy, ent_p)
            sv = sy.value_load(cst["ridx"][0:1, 0:1])
            for tb in range(NQB):
                B.w(sy, "VS", vs0 + max(0, tb - 1))
                if tb >= 2:
                    B.w(sy, "DO", do0 + 16 * (tb - 1))
                B.dma(sy, os_[tb % 2][:, :].rearrange("p (c t) -> p c t",
                                                      t=TB),
                      oaD.ap().rearrange(
                          "(c p) s -> p c s",
                          p=128)[:, :, tb * TB:(tb + 1) * TB], f"D{tb % 2}")
                B.dma(sy, xs[tb % 2][:, :].rearrange("p (c t) -> p c t",
                                                     t=TB),
                      AP(xF, sv + tb * TB,
                         [[S, 128], [128 * S, CH], [1, TB]]), f"D{tb % 2}")
                if tb >= 1:
                    B.w(sy, "VS", vs0 + tb)
                    B.dma(sy, xf_ap(xF2)[:, :, (tb - 1) * TB:tb * TB],
                          xs[(tb - 1) % 2][:, :].rearrange(
                              "p (c t) -> p c t", t=TB), "DO")
            B.w(sy, "VS", vs0 + NQB)
            B.dma(sy, xf_ap(xF2)[:, :, (NQB - 1) * TB:NQB * TB],
                  xs[(NQB - 1) % 2][:, :].rearrange("p (c t) -> p c t",
                                                    t=TB), "DO")
        block.sync(syio)

        def vb(v):
            B.entry(v, ent_p)
            for tb in range(NQB):
                B.w(v, f"D{tb % 2}", dp[tb % 2] + 32 * (tb // 2 + 1))
                i = v.tensor_add(xs[tb % 2][:, :], xs[tb % 2][:, :],
                                 os_[tb % 2][:, :])
                B.inc(i, "VS")
        block.vector(vb)


def _attn_jobs(cst, causal, wA, wB, k_dst, q_dst):
    jobs = []
    for tb in range(NTB):
        jobs.append((wB, k_dst[:, tb * TB:(tb + 1) * TB],
                     (lambda tb: lambda c: xn_sl(cst, c, tb * TB,
                                                 TB))(tb), TB))
    if causal:
        for tb in range(NTB):
            jobs.append((wA, q_dst[:, tb * TB:(tb + 1) * TB],
                         (lambda tb: lambda c: xn_sl(cst, c, tb * TB,
                                                     TB))(tb), TB))
    else:
        xo = cst["xn2own"]
        for qb in range(NQB):
            jobs.append((wA, q_dst[:, qb * TB:(qb + 1) * TB],
                         (lambda qb: lambda c: xo[
                             :, c * SOUT + qb * TB:
                             c * SOUT + (qb + 1) * TB])(qb), TB))
        jobs.append((wA, q_dst[:, SOUT:SOUT + 1],
                     lambda c: cst["xnqh"][:, c * 128 + 127:
                                           c * 128 + 128], 1))
    return jobs


def _attn_head_pair(B, block, oaD, vtD, sexpD, cst, hp, causal,
                    wq_d, wk_d, winT_d, ld, lc, emit_hp_loads,
                    emit_u_loads, khb, qhb, ldsets):
    nc = B.nc
    ent_h = dict(B.cnt)
    if hp + 1 < HP:
        emit_hp_loads(hp + 1, ent_h)
    NQ = NTB if causal else NQB
    wA, wB, vt_hp = ld["wA"], ld["wB"], ld["vt"]
    u0, u1 = ld["u0"], ld["u1"]
    lsem, vsem = f"L{hp % 2}", f"V{hp % 2}"
    lc_hp, vc_hp = lc[hp]
    k_hp, q_hp = khb[hp % 2], qhb[hp % 2]
    if hp + 1 < HP:
        ld2 = ldsets[(hp + 1) % 2]
        jobs2 = _attn_jobs(cst, causal, ld2["wA"], ld2["wB"],
                           khb[(hp + 1) % 2], qhb[(hp + 1) % 2])
        l2sem, lc2 = f"L{(hp + 1) % 2}", None
    else:
        jobs2 = []
    from contextlib import ExitStack
    with ExitStack() as ctx:
        e = ctx.enter_context
        psb0 = e(nc.sbuf_tensor(B.nm(f"psb0_{hp}"), [128, 2 * TB], BF16))
        psb1 = e(nc.sbuf_tensor(B.nm(f"psb1_{hp}"), [128, 2 * TB], BF16))
        psb2 = e(nc.sbuf_tensor(B.nm(f"psb2_{hp}"), [128, 2 * TB], BF16))
        ob = [[e(nc.sbuf_tensor(B.nm(f"ob{hp}"), [64, TB], BF16))
               for _ in range(2)] for _ in range(2)]
        zrP = e(nc.sbuf_tensor(B.nm(f"zrP{hp}"), [65, 2 * TB], F32))
        rzP = e(nc.sbuf_tensor(B.nm(f"rzP{hp}"), [65, 2 * TB], F32))
        bcsP = e(nc.sbuf_tensor(B.nm(f"bcP{hp}"), [128, 2 * TB], F32))
        sp0 = e(nc.psum_tensor(B.nm(f"sp0_{hp}"), [128, 2 * TB], F32))
        sp1 = e(nc.psum_tensor(B.nm(f"sp1_{hp}"), [128, 2 * TB], F32))
        pv = [e(nc.psum_tensor(B.nm(f"pv{hp}"), [65, 2 * TB], F32))
              for _ in range(2)]
        rzD = nc.dram_tensor(B.nm(f"rzD{hp}"), [2, 2 * TB], F32)

        iters = []
        g_first = {}
        for qb in range(NQ):
            cmax = 4 * qb + 4 if causal else KC
            g_first[qb] = len(iters)
            for ci in range(cmax):
                if causal:
                    kind = "mask" if ci >= 4 * qb else "plain"
                else:
                    kind = "bias"
                iters.append((qb, ci, cmax, kind))
        nit = len(iters)
        units = []
        step = max(1, nit // len(jobs2)) if jobs2 else 0
        nins = 0
        for g in range(nit):
            units.append(("S", g))
            if jobs2 and (g + 1) % step == 0 and nins < len(jobs2):
                units.append(("P", nins))
                nins += 1
        while nins < len(jobs2):
            units.append(("P", nins))
            nins += 1
        s2k = {}
        for k, (tp_u, idx) in enumerate(units):
            if tp_u == "S":
                s2k[idx] = k

        c = dict(B.cnt)
        ss_u = [0] * len(units)
        as_e = [0] * nit
        vs_m = [None] * nit
        ps_p = [0] * nit
        vs_pe = [0] * len(jobs2)
        as_rz, vs_ob = {}, {}
        do_rz, di_bc, do_ob = {}, {}, {}

        def sim_evacA(qb):
            c["AS"] += 1
            as_rz[qb] = c["AS"]

        def sim_evacV(qb):
            c["VS"] += 2
            vs_ob[qb] = c["VS"]

        def sim_evacS(qb):
            c["DO"] += 16
            do_rz[qb] = c["DO"]
            c["DI"] += 16
            di_bc[qb] = c["DI"]
            c[f"D{qb % 2}"] += 32
            do_ob[qb] = c[f"D{qb % 2}"]

        for k, (tp_u, idx) in enumerate(units):
            if tp_u == "S":
                g = idx
                qb, ci, cmax, kind = iters[g]
                if ci == 0 and qb >= 1:
                    sim_evacA(qb - 1)
                c["SS"] += 1
                ss_u[k] = c["SS"]
                c["AS"] += 1
                as_e[g] = c["AS"]
                if ci == 4 and qb >= 1:
                    sim_evacV(qb - 1)
                if kind in ("mask", "bias"):
                    c["VS"] += 1
                    vs_m[g] = c["VS"]
                if g >= 2:
                    c["PS"] += 1
                    ps_p[g - 2] = c["PS"]
            else:
                c["SS"] += 1
                ss_u[k] = c["SS"]
                c["VS"] += 1
                vs_pe[idx] = c["VS"]
        for gg in (nit - 2, nit - 1):
            c["PS"] += 1
            ps_p[gg] = c["PS"]
        sim_evacA(NQ - 1)
        sim_evacV(NQ - 1)
        for qb in range(NQ):
            sim_evacS(qb)
        if not causal:
            c["SS"] += 1
            hs_ss = c["SS"]
            c["AS"] += 2
            a_he = c["AS"]
            c["VS"] += 1
            v_hm = c["VS"]
            c["PS"] += 1
            hs_ps = c["PS"]
            c["AS"] += 4
            a_hc = c["AS"]
            c["DI"] += 16
            d_hm = c["DI"]

        sps = (sp0, sp1)
        pbs = (psb0, psb1, psb2)
        as_prior = B.cnt["AS"]
        ps_prior = B.cnt["PS"]
        vs_prior = B.cnt["VS"]
        lc2v = lc[hp + 1][0] if jobs2 else 0

        def t_att(t):
            B.w(t, lsem, lc_hp)
            B.w(t, vsem, vc_hp)
            B.w(t, "AS", as_prior)
            B.w(t, "VS", vs_prior)

            def pv_part(j):
                qb, ci, cmax, kind = iters[j]
                par = qb % 2
                if ci == 0 and qb >= 2:
                    B.w(t, "VS", vs_ob[qb - 2])
                pb = pbs[j % 3]
                i0 = t.matmul(pv[par][:, 0:TB],
                              vt_hp[:, ci * 130:ci * 130 + 65],
                              pb[:, 0:TB], start=(ci == 0),
                              stop=(ci == cmax - 1))
                if kind == "plain":
                    B.wi(i0, "AS", as_e[j])
                else:
                    B.wi(i0, "VS", vs_m[j])
                i = t.matmul(pv[par][:, TB:2 * TB],
                             vt_hp[:, ci * 130 + 65:ci * 130 + 130],
                             pb[:, TB:2 * TB], start=(ci == 0),
                             stop=(ci == cmax - 1))
                B.inc(i, "PS")

            def slot_wait(i0, k):
                if k < 2:
                    return
                pt, pidx = units[k - 2]
                if pt == "S":
                    B.wi(i0, "AS", as_e[pidx])
                else:
                    B.wi(i0, "VS", vs_pe[pidx])

            for k, (tp_u, idx) in enumerate(units):
                sp = sps[k % 2]
                if tp_u == "S":
                    g = idx
                    qb, ci, cmax, kind = iters[g]
                    k0 = ci * 128
                    i0 = t.matmul(sp[:, 0:TB], k_hp[0:64, k0:k0 + 128],
                                  q_hp[0:64, qb * TB:(qb + 1) * TB],
                                  start=True, stop=True,
                                  tile_position=(0, 0))
                    slot_wait(i0, k)
                    i = t.matmul(sp[:, TB:2 * TB],
                                 k_hp[64:128, k0:k0 + 128],
                                 q_hp[64:128, qb * TB:(qb + 1) * TB],
                                 start=True, stop=True,
                                 tile_position=(64, 0))
                    B.inc(i, "SS")
                    if g >= 2:
                        pv_part(g - 2)
                else:
                    j = idx
                    if j == 0:
                        B.w(t, l2sem, lc2v)
                    w_sb, dst, rhs_fn, width = jobs2[j]
                    for cc in range(CH):
                        i = t.matmul(sp[:, 0:width],
                                     w_sb[:, cc * 128:(cc + 1) * 128],
                                     rhs_fn(cc), start=(cc == 0),
                                     stop=(cc == CH - 1))
                        if cc == 0:
                            slot_wait(i, k)
                    B.inc(i, "SS")
            if nit >= 2:
                pv_part(nit - 2)
            pv_part(nit - 1)
            if not causal:
                for kk in (len(units) - 2, len(units) - 1):
                    pt, pidx = units[kk]
                    if pt == "S":
                        B.w(t, "AS", as_e[pidx])
                    else:
                        B.w(t, "VS", vs_pe[pidx])
                for cc in range(KC):
                    t.matmul(sp0[:, cc:cc + 1],
                             k_hp[0:64, cc * 128:(cc + 1) * 128],
                             q_hp[0:64, SOUT:SOUT + 1],
                             start=True, stop=True, tile_position=(0, 0))
                    i = t.matmul(sp0[:, TB + cc:TB + cc + 1],
                                 k_hp[64:128, cc * 128:(cc + 1) * 128],
                                 q_hp[64:128, SOUT:SOUT + 1],
                                 start=True, stop=True,
                                 tile_position=(64, 0))
                B.inc(i, "SS")
                B.w(t, "VS", v_hm)
                B.w(t, "VS", vs_ob[NQ - 2])
                for cc in range(KC):
                    t.matmul(pv[0][:, 0:1],
                             vt_hp[:, cc * 130:cc * 130 + 65],
                             pbs[0][:, cc:cc + 1], start=(cc == 0),
                             stop=(cc == KC - 1))
                    i = t.matmul(pv[0][:, TB:TB + 1],
                                 vt_hp[:, cc * 130 + 65:cc * 130 + 130],
                                 pbs[0][:, TB + cc:TB + cc + 1],
                                 start=(cc == 0), stop=(cc == KC - 1))
                B.inc(i, "PS")
        block.tensor(t_att)

        def emit_evacA(s, qb):
            par = qb % 2
            if qb >= 2:
                B.w(s, "DO", do_rz[qb - 2])
            zr_r = zrP[64 * par:64 * par + 1, :]
            rz_r = rzP[64 * par:64 * par + 1, :]
            i = s.activation(zr_r, pv[par][64:65, :], AF.Ln)
            B.wi(i, "PS", ps_p[g_first[qb] + iters[g_first[qb]][2] - 1])
            B.chain(s, i, "CA")
            i = s.activation(rz_r, zr_r, AF.Exp, scale=-1.0)
            B.inc(i, "AS")

        def s_att(s):
            B.entry(s, ent_h)
            for k, (tp_u, idx) in enumerate(units):
                if tp_u != "S":
                    continue
                g = idx
                qb, ci, cmax, kind = iters[g]
                if ci == 0 and qb >= 1:
                    emit_evacA(s, qb - 1)
                B.w(s, "PS", ps_p[g - 3] if g >= 3 else ps_prior)
                i = s.activation(pbs[g % 3][:, :], sps[k % 2][:, :],
                                 AF.Exp, scale=SCALE)
                B.wi(i, "SS", ss_u[k])
                B.inc(i, "AS")
            emit_evacA(s, NQ - 1)
            if not causal:
                B.w(s, "SS", hs_ss)
                B.w(s, "PS", ps_p[nit - 1])
                i = s.activation(pbs[0][:, 0:KC], sp0[:, 0:KC], AF.Exp,
                                 scale=SCALE)
                B.inc(i, "AS")
                i = s.activation(pbs[0][:, TB:TB + KC], sp0[:, TB:TB + KC],
                                 AF.Exp, scale=SCALE)
                B.inc(i, "AS")
                B.w(s, "PS", hs_ps)
                B.w(s, "DI", ent_h["DI"])
                i = s.copy(cst["ohalo"][0:64, hp:hp + 1],
                           pv[0][0:64, 0:1])
                B.inc(i, "AS")
                i = s.copy(cst["hstage"][0:64, 0:1], pv[0][0:64, TB:TB + 1])
                B.inc(i, "AS")
                i = s.copy(cst["zhst"][64:65, hp:hp + 1],
                           pv[0][64:65, 0:1])
                B.inc(i, "AS")
                i = s.copy(cst["zhst"][64:65, 8 + hp:9 + hp],
                           pv[0][64:65, TB:TB + 1])
                B.inc(i, "AS")
        block.scalar(s_att)

        def emit_evacV(v, qb):
            par = qb % 2
            B.w(v, "PS", ps_p[g_first[qb] + iters[g_first[qb]][2] - 1])
            B.w(v, "DI", di_bc[qb])
            if qb >= 2:
                B.w(v, f"D{qb % 2}", do_ob[qb - 2])
            i = v.tensor_mul(ob[par][0][:, :], pv[par][0:64, 0:TB],
                             bcsP[64 * par:64 * par + 64, 0:TB])
            B.inc(i, "VS")
            i = v.tensor_mul(ob[par][1][:, :], pv[par][0:64, TB:2 * TB],
                             bcsP[64 * par:64 * par + 64, TB:2 * TB])
            B.inc(i, "VS")

        def v_att(v):
            B.entry(v, ent_h)
            for k, (tp_u, idx) in enumerate(units):
                if tp_u == "P":
                    j = idx
                    w_sb, dst, rhs_fn, width = jobs2[j]
                    i = v.tensor_copy(dst, sps[k % 2][:, 0:width])
                    B.wi(i, "SS", ss_u[k])
                    B.inc(i, "VS")
                    continue
                g = idx
                qb, ci, cmax, kind = iters[g]
                if ci == 4 and qb >= 1:
                    emit_evacV(v, qb - 1)
                if kind in ("mask", "bias"):
                    if g >= 3:
                        B.w(v, "PS", ps_p[g - 3])
                    pb = pbs[g % 3]
                    first = True
                    if kind == "mask":
                        r = ci - 4 * qb
                        for e2 in range(2):
                            i = v.tensor_mul(
                                pb[:, e2 * TB:(e2 + 1) * TB],
                                pb[:, e2 * TB:(e2 + 1) * TB],
                                cst["masks"][:, r * TB:(r + 1) * TB])
                            if first:
                                B.wi(i, "AS", as_e[g])
                                first = False
                    else:
                        M0 = UOFF - qb * TB + ci * 128 + 1
                        for e2, u in ((0, u0), (1, u1)):
                            i = v.tensor_mul(
                                pb[:, e2 * TB:(e2 + 1) * TB],
                                pb[:, e2 * TB:(e2 + 1) * TB],
                                u[:, M0 - (TB - 1):M0 + 1][:, ::-1])
                            if first:
                                B.wi(i, "AS", as_e[g])
                                first = False
                    B.inc(i, "VS")
            emit_evacV(v, NQ - 1)
            if not causal:
                B.w(v, "AS", a_he)
                v.tensor_mul(pbs[0][:, 0:KC], pbs[0][:, 0:KC],
                             u0[:, UOFF + 1:UWB:128])
                i = v.tensor_mul(pbs[0][:, TB:TB + KC],
                                 pbs[0][:, TB:TB + KC],
                                 u1[:, UOFF + 1:UWB:128])
                B.inc(i, "VS")
        block.vector(v_att)

        def sy_att(sy):
            B.entry(sy, ent_h)
            for qb in range(NQ):
                par = qb % 2
                B.w(sy, "AS", as_rz[qb])
                B.dma(sy, rzD.ap()[par:par + 1, :],
                      rzP[64 * par:64 * par + 1, :], "DO")
                B.w(sy, "DO", do_rz[qb])
                if qb >= 2:
                    B.w(sy, "VS", vs_ob[qb - 2])
                B.dma(sy, bcsP[64 * par:64 * par + 64, :],
                      AP(rzD, par * 2 * TB, [[0, 64], [1, 2 * TB]]), "DI")
                B.w(sy, "VS", vs_ob[qb])
                r0 = hp * 128
                B.dma(sy, oaD.ap()[r0:r0 + 64, qb * TB:(qb + 1) * TB],
                      ob[par][0][:, :], f"D{par}")
                B.dma(sy, oaD.ap()[r0 + 64:r0 + 128, qb * TB:(qb + 1) * TB],
                      ob[par][1][:, :], f"D{par}")
            if not causal:
                B.w(sy, "AS", a_hc)
                B.dma(sy, cst["ohalo"][64:128, hp:hp + 1],
                      cst["hstage"][0:64, 0:1], "DI")
        block.sync(sy_att)

        for s_name in ("SS", "AS", "VS", "PS", "DO", "D0", "D1", "DI"):
            assert B.cnt[s_name] == c[s_name], (s_name, B.cnt[s_name],
                                                c[s_name], hp)
        if hp + 1 < HP:
            emit_u_loads(hp + 1)


def _attn_wo_residual(B, block, xF, oaD, wo_d):
    nc = B.nc
    ent_w = dict(B.cnt)
    with (
        nc.sbuf_tensor(B.nm("wo_sb"), [128, CH * D], BF16) as wo_sb,
        nc.sbuf_tensor(B.nm("wr_x0"), [128, CH * TB], F32) as wr_x0,
        nc.sbuf_tensor(B.nm("wr_o0"), [128, CH * TB], BF16) as wr_o0,
        nc.psum_tensor(B.nm("ps_w0"), [128, TB], F32) as ps_w0,
        nc.psum_tensor(B.nm("ps_w1"), [128, TB], F32) as ps_w1,
    ):
        def gw(g):
            B.entry(g, ent_w)
            B.dma(g, wo_sb[:, :].rearrange("p (cb j) -> p cb j", j=D),
                  wo_d.ap().rearrange("(cb p) j -> p cb j", p=128), "DG")
        block.gpsimd(gw)
        dg_w = B.cnt["DG"]
        pss = (ps_w0, ps_w1)
        di0, ts0, vs0, do0 = (B.cnt["DI"], B.cnt["SS"], B.cnt["VS"],
                              B.cnt["DO"])

        def syio(sy):
            B.entry(sy, ent_w)
            for tb in range(NTB):
                if tb > 0:
                    B.w(sy, "VS", vs0 + tb * CH)
                    B.dma(sy, xf_ap(xF)[:, :, (tb - 1) * TB:tb * TB],
                          wr_x0[:, :].rearrange("p (c t) -> p c t", t=TB),
                          "DO")
                    B.w(sy, "DO", B.cnt["DO"])
                B.dma(sy, wr_o0[:, :].rearrange("p (c t) -> p c t", t=TB),
                      oaD.ap().rearrange(
                          "(c p) s -> p c s",
                          p=128)[:, :, tb * TB:(tb + 1) * TB], "DI")
                B.dma(sy, wr_x0[:, :].rearrange("p (c t) -> p c t", t=TB),
                      xf_ap(xF)[:, :, tb * TB:(tb + 1) * TB], "DI")
            B.w(sy, "VS", vs0 + NTB * CH)
            B.dma(sy, xf_ap(xF)[:, :, (NTB - 1) * TB:NTB * TB],
                  wr_x0[:, :].rearrange("p (c t) -> p c t", t=TB), "DO")
        block.sync(syio)

        def tb_(t):
            B.entry(t, ent_w)
            B.w(t, "DG", dg_w)
            for tb in range(NTB):
                B.w(t, "DI", di0 + 32 * (tb + 1))
                for oc in range(CH):
                    ps = pss[oc % 2]
                    n = tb * CH + oc
                    B.w(t, "VS", vs0 + max(0, n - 1))
                    for cc in range(CH):
                        i = t.matmul(ps[:, :],
                                     wo_sb[:, cc * D + oc * 128:
                                           cc * D + (oc + 1) * 128],
                                     wr_o0[:, cc * TB:(cc + 1) * TB],
                                     start=(cc == 0), stop=(cc == CH - 1))
                    B.inc(i, "SS")
        block.tensor(tb_)

        def vb(v):
            B.entry(v, ent_w)
            for tb in range(NTB):
                B.w(v, "DI", di0 + 32 * (tb + 1))
                for oc in range(CH):
                    n = tb * CH + oc
                    B.w(v, "SS", ts0 + n + 1)
                    i = v.tensor_add(wr_x0[:, oc * TB:(oc + 1) * TB],
                                     wr_x0[:, oc * TB:(oc + 1) * TB],
                                     pss[oc % 2][:, :])
                    B.inc(i, "VS")
        block.vector(vb)


def _attn_plain_residual(B, block, xF, oaD):
    nc = B.nc
    ent_p = dict(B.cnt)
    with (
        nc.sbuf_tensor(B.nm("pr_x0"), [128, CH * TB], F32) as pr_x0,
        nc.sbuf_tensor(B.nm("pr_o0"), [128, CH * TB], BF16) as pr_o0,
    ):
        di0, vs0, do0 = B.cnt["DI"], B.cnt["VS"], B.cnt["DO"]

        def syio(sy):
            B.entry(sy, ent_p)
            for tb in range(NTB):
                if tb > 0:
                    B.w(sy, "VS", vs0 + tb)
                    B.dma(sy, xf_ap(xF)[:, :, (tb - 1) * TB:tb * TB],
                          pr_x0[:, :].rearrange("p (c t) -> p c t", t=TB),
                          "DO")
                    B.w(sy, "DO", B.cnt["DO"])
                B.dma(sy, pr_o0[:, :].rearrange("p (c t) -> p c t", t=TB),
                      oaD.ap().rearrange(
                          "(c p) s -> p c s",
                          p=128)[:, :, tb * TB:(tb + 1) * TB], "DI")
                B.dma(sy, pr_x0[:, :].rearrange("p (c t) -> p c t", t=TB),
                      xf_ap(xF)[:, :, tb * TB:(tb + 1) * TB], "DI")
            B.w(sy, "VS", vs0 + NTB)
            B.dma(sy, xf_ap(xF)[:, :, (NTB - 1) * TB:NTB * TB],
                  pr_x0[:, :].rearrange("p (c t) -> p c t", t=TB), "DO")
        block.sync(syio)

        def vb(v):
            B.entry(v, ent_p)
            for tb in range(NTB):
                B.w(v, "DI", di0 + 32 * (tb + 1))
                i = v.tensor_add(pr_x0[:, :], pr_x0[:, :], pr_o0[:, :])
                B.inc(i, "VS")
        block.vector(vb)


def emit_ffn(B, block, xF, cst, wrT_d, wk2T_d, wv2T_d, ntok=S):
    nc = B.nc
    NB = ntok // TB
    ent_f = dict(B.cnt)
    with (
        nc.sbuf_tensor(B.nm("fw_r"), [128, CH * D], BF16) as fw_r,
        nc.sbuf_tensor(B.nm("fw_k"), [128, CH * D], BF16) as fw_k,
        nc.sbuf_tensor(B.nm("fw_v"), [128, CH * D], BF16) as fw_v,
        nc.sbuf_tensor(B.nm("ff_xa"), [128, CH * TB], F32) as ff_xa,
        nc.sbuf_tensor(B.nm("ff_xb"), [128, CH * TB], F32) as ff_xb,
        nc.sbuf_tensor(B.nm("ff_xr0"), [128, CH * TB], BF16) as ff_xr0,
        nc.sbuf_tensor(B.nm("ff_xk0"), [128, CH * TB], BF16) as ff_xk0,
        nc.sbuf_tensor(B.nm("ff_xr1"), [128, CH * TB], BF16) as ff_xr1,
        nc.sbuf_tensor(B.nm("ff_xk1"), [128, CH * TB], BF16) as ff_xk1,
        nc.sbuf_tensor(B.nm("ff_r"), [128, CH * TB], F32) as ff_r,
        nc.sbuf_tensor(B.nm("ff_ka"), [128, CH * TB], BF16) as ff_ka,
        nc.sbuf_tensor(B.nm("ff_t2"), [128, TB], F32) as ff_t2,
        nc.psum_tensor(B.nm("ps_f0"), [128, TB], F32) as ps_f0,
        nc.psum_tensor(B.nm("ps_f1"), [128, TB], F32) as ps_f1,
        nc.psum_tensor(B.nm("ps_f2"), [128, TB], F32) as ps_f2,
        nc.psum_tensor(B.nm("ps_f3"), [128, TB], F32) as ps_f3,
    ):
        def gw(g):
            B.entry(g, ent_f)
            for w_sb, w_d in ((fw_r, wrT_d), (fw_k, wk2T_d), (fw_v, wv2T_d)):
                B.dma(g, w_sb[:, :].rearrange("p (cb j) -> p cb j", j=D),
                      w_d.ap().rearrange("(cb p) j -> p cb j", p=128), "DG")
        block.gpsimd(gw)
        dg_w = B.cnt["DG"]
        xn_vs = B.cnt["VS"]
        pss_sets = ((ps_f0, ps_f1), (ps_f2, ps_f3))
        fxs = (ff_xa, ff_xb)
        xrs = (ff_xr0, ff_xr1)
        xks = (ff_xk0, ff_xk1)
        ssp = {}
        vsm = {}
        prev_vs3 = 0

        def emit_vmix(tbx, first):
            war = ssp.get(tbx - 2)

            def vmix(v, tbx=tbx, first=first, war=war):
                if first:
                    B.entry(v, ent_f)
                    B.w(v, "VS", xn_vs)
                if war is not None:
                    B.w(v, "SS", war)
                for mu1, dst in ((cst["mur1"], xrs[tbx % 2]),
                                 (cst["muk1"], xks[tbx % 2])):
                    for c in range(CH):
                        base = c * (S + 1) + 1 + tbx * TB
                        xp_c = cst["xnb"][:, base - 1:base - 1 + TB]
                        i = v.tensor_scalar(dst[:, c * TB:(c + 1) * TB],
                                            xp_c, mu1[:, c:c + 1], None,
                                            ALU.mult)
                B.chain(v, i, "CV")
                for mu_sb, dst in ((cst["mur"], xrs[tbx % 2]),
                                   (cst["muk"], xks[tbx % 2])):
                    for c in range(CH):
                        base = c * (S + 1) + 1 + tbx * TB
                        x_c = cst["xnb"][:, base:base + TB]
                        i = v.scalar_tensor_tensor(
                            dst[:, c * TB:(c + 1) * TB],
                            x_c, mu_sb[:, c:c + 1],
                            dst[:, c * TB:(c + 1) * TB],
                            ALU.mult, ALU.add)
                B.inc(i, "VS")
            block.vector(vmix)
            vsm[tbx] = B.cnt["VS"]

        emit_vmix(0, True)

        def sy0(sy):
            B.entry(sy, ent_f)
            B.dma(sy, ff_xa[:, :].rearrange("p (c t) -> p c t", t=TB),
                  xf_ap(xF)[:, :, 0:TB], "V0")
        block.sync(sy0)
        di_ready0 = B.cnt["V0"]

        for tb in range(NB):
            ff_xr, ff_xk = xrs[tb % 2], xks[tb % 2]
            ff_x = fxs[tb % 2]
            pss = pss_sets[tb % 2]
            if tb == 0:
                di_all = di_ready0
            vs_mix = vsm[tb]
            ts0 = B.cnt["SS"]
            as0 = B.cnt["AS"]
            vs0 = B.cnt["VS"]

            vadd = 1 if tb + 1 < NB else 0

            def tproj(t, tb=tb, ts0=ts0, as0=as0, vs0=vs0, vs_mix=vs_mix,
                      ff_xr=ff_xr, ff_xk=ff_xk, pss=pss, vadd=vadd):
                if tb == 0:
                    B.entry(t, ent_f)
                    B.w(t, "DG", dg_w)
                B.w(t, "VS", vs_mix)
                n = 0
                for w_sb, src in ((fw_r, ff_xr), (fw_k, ff_xk)):
                    for oc in range(CH):
                        ps = pss[n % 2]
                        if n >= 2:
                            if n - 2 < CH:
                                B.w(t, "AS", as0 + (n - 2) + 1)
                            else:
                                B.w(t, "VS", vs0 + vadd + (n - 2 - CH) + 1)
                        for cc in range(CH):
                            i = t.matmul(ps[:, :],
                                         w_sb[:, cc * D + oc * 128:
                                              cc * D + (oc + 1) * 128],
                                         src[:, cc * TB:(cc + 1) * TB],
                                         start=(cc == 0),
                                         stop=(cc == CH - 1))
                        B.inc(i, "SS")
                        n += 1
            block.tensor(tproj)
            ssp[tb] = B.cnt["SS"]

            def sact(s, ts0=ts0, tb=tb, pss=pss, pv3=prev_vs3):
                if tb == 0:
                    B.entry(s, ent_f)
                for oc in range(CH):
                    if pv3:
                        B.w(s, "VS", pv3 + oc + 1)
                    B.w(s, "SS", ts0 + oc + 1)
                    i = s.activation(ff_r[:, oc * TB:(oc + 1) * TB],
                                     pss[oc % 2][:, :], AF.Sigmoid)
                    B.inc(i, "AS")
            block.scalar(sact)

            if tb + 1 < NB:
                emit_vmix(tb + 1, False)
            vs_emis2 = B.cnt["VS"]

            def vact(v, ts0=ts0, vs_emis2=vs_emis2, pss=pss):
                B.w(v, "VS", vs_emis2)
                for oc in range(CH):
                    n = CH + oc
                    if oc:
                        B.w(v, "VS", vs_emis2 + oc)
                    B.w(v, "SS", ts0 + n + 1)
                    B.chain(v, v.tensor_scalar_max(ff_t2[:, :],
                                                   pss[n % 2][:, :], 0.0),
                            "CV")
                    i = v.tensor_mul(ff_ka[:, oc * TB:(oc + 1) * TB],
                                     ff_t2[:, :], ff_t2[:, :])
                    B.inc(i, "VS")
            block.vector(vact)
            as_r = B.cnt["AS"]
            vs_k = B.cnt["VS"]

            ts1 = B.cnt["SS"]
            vs1 = B.cnt["VS"]

            def tkv(t, ts1=ts1, vs1=vs1, pss=pss):
                B.w(t, "VS", vs_k)
                for oc in range(CH):
                    ps = pss[oc % 2]
                    if oc >= 2:
                        B.w(t, "VS", vs1 + (oc - 2) + 1)
                    for cc in range(CH):
                        i = t.matmul(ps[:, :],
                                     fw_v[:, cc * D + oc * 128:
                                          cc * D + (oc + 1) * 128],
                                     ff_ka[:, cc * TB:(cc + 1) * TB],
                                     start=(cc == 0), stop=(cc == CH - 1))
                    B.inc(i, "SS")
            block.tensor(tkv)

            vs_emis3 = B.cnt["VS"]

            def vkv(v, ts1=ts1, di_all=di_all, vs_emis3=vs_emis3,
                    pss=pss, ff_x=ff_x, tb=tb):
                B.w(v, "VS", vs_emis3)
                B.w(v, "AS", as_r)
                B.w(v, f"V{tb % 2}", di_all)
                for oc in range(CH):
                    if oc:
                        B.w(v, "VS", vs_emis3 + oc)
                    B.w(v, "SS", ts1 + oc + 1)
                    B.chain(v, v.tensor_mul(ff_t2[:, :], pss[oc % 2][:, :],
                                            ff_r[:, oc * TB:(oc + 1) * TB]),
                            "CV")
                    i = v.tensor_add(ff_x[:, oc * TB:(oc + 1) * TB],
                                     ff_x[:, oc * TB:(oc + 1) * TB],
                                     ff_t2[:, :])
                    B.inc(i, "VS")
            block.vector(vkv)
            prev_vs3 = vs_emis3

            def syio(sy, tb=tb, vs_emis3=vs_emis3, ff_x=ff_x,
                     dpar=(B.cnt["D0"], B.cnt["D1"])):
                if tb + 1 < NB:
                    B.w(sy, "D0", dpar[0])
                    B.w(sy, "D1", dpar[1])
                    B.dma(sy, fxs[(tb + 1) % 2][:, :].rearrange(
                              "p (c t) -> p c t", t=TB),
                          xf_ap(xF)[:, :, (tb + 1) * TB:(tb + 2) * TB],
                          f"V{(tb + 1) % 2}")
                for c in range(CH):
                    B.w(sy, "VS", vs_emis3 + c + 1)
                    B.dma(sy, xf_ap(xF)[:, c:c + 1,
                                        tb * TB:(tb + 1) * TB],
                          ff_x[:, c * TB:(c + 1) * TB].rearrange(
                              "p (o t) -> p o t", o=1), f"D{c % 2}")
            block.sync(syio)
            if tb + 1 < NB:
                di_all = B.cnt[f"V{(tb + 1) % 2}"]


def emit_output(B, block, xF, ident, y_d, cst, dyn=False):
    nc = B.nc
    ntt = SOUT // 128
    with (
        nc.sbuf_tensor(B.nm("yo0"), [128, D], F32) as yo0,
        nc.sbuf_tensor(B.nm("yo1"), [128, D], F32) as yo1,
        nc.sbuf_tensor(B.nm("yt"), [128, D], F32) as yt,
        nc.psum_tensor(B.nm("ps_y"), [128, D], F32) as ps_y,
    ):
        yos = (yo0, yo1)
        ent_o = dict(B.cnt)
        dp = (B.cnt["D0"], B.cnt["D1"])
        ts0, as0, do0 = B.cnt["SS"], B.cnt["AS"], B.cnt["DO"]

        def syio(sy):
            B.entry(sy, ent_o)
            if dyn:
                sv = sy.value_load(cst["ridx"][0:1, 0:1])
            for tt in range(ntt):
                B.w(sy, "SS", ts0 + max(0, tt - 1))
                if dyn:
                    srca = AP(xF, sv + tt * 128,
                              [[S, 128], [128 * S, CH], [1, 128]])
                else:
                    srca = xf_ap(xF)[:, :, tt * 128:(tt + 1) * 128]
                B.dma(sy, yos[tt % 2][:, :].rearrange("p (c j) -> p c j",
                                                      j=128),
                      srca, f"D{tt % 2}")
                if tt >= 1:
                    B.w(sy, "AS", as0 + tt)
                    B.dma(sy, y_d[(tt - 1) * 128:tt * 128, :], yt[:, :],
                          "DO")
            B.w(sy, "AS", as0 + ntt)
            B.dma(sy, y_d[(ntt - 1) * 128:ntt * 128, :], yt[:, :], "DO")
            B.w(sy, "DO", B.cnt["DO"])
        block.sync(syio)

        def tb_(t):
            B.entry(t, ent_o)
            for tt in range(ntt):
                B.w(t, f"D{tt % 2}", dp[tt % 2] + 16 * (tt // 2 + 1))
                B.w(t, "AS", as0 + tt)
                for c in range(CH):
                    i = t.transpose(ps_y[:, c * 128:(c + 1) * 128],
                                    yos[tt % 2][:, c * 128:(c + 1) * 128],
                                    ident[:, :])
                B.inc(i, "SS")
        block.tensor(tb_)

        def sb(s):
            B.entry(s, ent_o)
            for tt in range(ntt):
                B.w(s, "SS", ts0 + tt + 1)
                B.w(s, "DO", do0 + 16 * tt)
                i = s.copy(yt[:, :], ps_y[:, :])
                B.inc(i, "AS")
        block.scalar(sb)


_prog_cache = {}


def _get_prog():
    if "nc" not in _prog_cache:
        _prog_cache["nc"] = build_program()
    return _prog_cache["nc"]


def kernel(L0_attn_a_x, L0_attn_a_W_Q, L0_attn_a_W_K, L0_attn_a_W_V,
           L0_attn_a_W_o, L0_attn_b_w_in, L0_attn_b_kernel_offsets,
           L0_attn_b_kernel_amplitudes, L0_attn_b_kernel_sharpness,
           L0_ffn_W_r, L0_ffn_W_k, L0_ffn_W_v, L0_ffn_mu_r, L0_ffn_mu_k):
    def f32(a):
        return np.ascontiguousarray(np.asarray(a, dtype=np.float32))

    x = f32(L0_attn_a_x)
    wq, wk, wv, wo = (f32(L0_attn_a_W_Q), f32(L0_attn_a_W_K),
                      f32(L0_attn_a_W_V), f32(L0_attn_a_W_o))
    winT = np.ascontiguousarray(f32(L0_attn_b_w_in).T)
    wrT = np.ascontiguousarray(f32(L0_ffn_W_r).T)
    wk2T = np.ascontiguousarray(f32(L0_ffn_W_k).T)
    wv2T = np.ascontiguousarray(f32(L0_ffn_W_v).T)
    mur = np.ascontiguousarray(f32(L0_ffn_mu_r).reshape(CH, 128).T)
    muk = np.ascontiguousarray(f32(L0_ffn_mu_k).reshape(CH, 128).T)
    off = f32(L0_attn_b_kernel_offsets)
    amp = f32(L0_attn_b_kernel_amplitudes)
    shp = f32(L0_attn_b_kernel_sharpness)
    toff = np.zeros((128, 2), np.float32)
    tshp = np.zeros((128, 2), np.float32)
    tamp = np.zeros((256, 16), np.float32)
    for h in range(H):
        for kap in range(8):
            for half in range(2):
                toff[h * 8 + kap, half] = off[h, kap + 8 * half]
                tshp[h * 8 + kap, half] = shp[h, kap + 8 * half]
                tamp[half * 128 + h * 8 + kap, h] = amp[h, kap + 8 * half]

    nc = _get_prog()
    in_maps = []
    for cid in range(8):
        seq = cid // 2
        role = cid % 2
        delta = SOUT * role
        rolec = np.array([[float(delta), float(role), 0.0, 0.0]], np.float32)
        ridx = np.array([[delta, 3968 if role == 0 else 1920, 0, 0]],
                        np.uint32)
        in_maps.append({
            "xin": x[seq], "wq": wq, "wk": wk, "wv": wv, "wo": wo,
            "winT": winT, "wrT": wrT, "wk2T": wk2T, "wv2T": wv2T,
            "mur": mur, "muk": muk, "toff": toff, "tshp": tshp,
            "tampbd": tamp, "rolec": rolec, "ridx": ridx,
        })
    res = run_bass_kernel_spmd(nc, in_maps, core_ids=list(range(8)))
    out = np.empty((4, S, D), np.float32)
    for cid in range(8):
        seq, role = cid // 2, cid % 2
        out[seq, SOUT * role:SOUT * (role + 1)] = np.asarray(
            res.results[cid]["y"], dtype=np.float32)
    return out
